# version 06
# baseline (speedup 1.0000x reference)
import numpy as np
import ml_dtypes
import concourse.bass as bass
import concourse.mybir as mybir
from concourse.bass_utils import run_bass_kernel_spmd

F32 = mybir.dt.float32
BF16 = mybir.dt.bfloat16
AF = mybir.ActivationFunctionType
ALU = mybir.AluOpType
AX = mybir.AxisListType
BF = ml_dtypes.bfloat16

NC = 8
T = 2048
NT = 16
D = 1024
B_, S_ = 2, 8192
EPS = 1e-5



def tm_host(x):
    t, f = x.shape
    return np.ascontiguousarray(x.reshape(t // 128, 128, f))


def fm_host(w):
    k, m = w.shape
    assert k % 128 == 0
    return np.ascontiguousarray(w.reshape(k // 128, 128, m))


def core_slices(x):
    return [np.ascontiguousarray(x[c // 4, (c % 4) * T:(c % 4 + 1) * T])
            for c in range(NC)]


def assemble(slices, f):
    out = np.empty((2, S_, f), slices[0].dtype)
    for c in range(NC):
        out[c // 4, (c % 4) * T:(c % 4 + 1) * T] = slices[c]
    return out



ENGS = ("sync", "gpsimd", "scalar", "vector", "tensor")


class _Rec:
    __slots__ = ("name", "args", "kw", "waits", "incs")

    def __init__(self, name, args, kw, waits, incs):
        self.name, self.args, self.kw = name, args, kw
        self.waits, self.incs = waits, incs


class Eng:

    def __init__(self, prog, ename, sem, chain):
        self.prog = prog
        self.ename = ename
        self.sem = sem
        self.chain = chain
        self.n = 0

    def wait(self, sem, val):
        self.prog.ops[self.ename].append(
            _Rec("__wait__", (sem, val), {}, [], []))

    def emit(self, name, *args, waits=(), **kw):
        w = list(waits)
        incs = []
        if self.chain:
            if self.n > 0:
                w.append((self.sem, self.n))
            self.n += 1
            incs.append((self.sem, 1))
        self.prog.ops[self.ename].append(_Rec(name, args, kw, w, incs))
        return self.n

    def emit_unchained(self, name, *args, waits=(), incs=(), **kw):
        self.prog.ops[self.ename].append(
            _Rec(name, args, kw, list(waits), list(incs)))


class Prog:
    def __init__(self, nc):
        self.nc = nc
        self.ops = {e: [] for e in ENGS}

    def engines(self, chain_sems):
        res = {}
        for e in ENGS:
            sem = chain_sems.get(e)
            res[e] = Eng(self, e, sem, sem is not None)
        return res

    def build(self, blk):
        def player(lst):
            def f(eng):
                for r in lst:
                    if r.name == "__wait__":
                        eng.wait_ge(*r.args)
                        continue
                    for sem, val in r.waits:
                        eng.wait_ge(sem, val)
                    ins = getattr(eng, r.name)(*r.args, **r.kw)
                    for sem, val in r.incs:
                        ins.then_inc(sem, val)
            return f
        for e in ENGS:
            if self.ops[e]:
                getattr(blk, e)(player(self.ops[e]))


def ln_stats(v, s, src, sq, sums, sqs, mu, rs, epscol, ntiles, fdim,
             v_ready=(), s_ready=()):
    msq = s.emit("activation", sq, src, AF.Square, waits=s_ready)
    view = src.rearrange("p (t f) -> p t f", t=ntiles)
    v.emit("tensor_reduce", sums, view, axis=AX.X, op=ALU.add, waits=v_ready)
    v.wait(s.sem, msq)
    v.emit("tensor_reduce", sqs, sq.rearrange("p (t f) -> p t f", t=ntiles),
           axis=AX.X, op=ALU.add)
    v.emit("tensor_scalar_mul", mu, sums, 1.0 / fdim)
    v.emit("tensor_scalar_mul", sqs, sqs, 1.0 / fdim)
    v.emit("tensor_tensor", sums, mu, mu, op=ALU.mult)
    mvar = v.emit("tensor_tensor", sqs, sqs, sums, op=ALU.subtract)
    s.wait(v.sem, mvar)
    msd = s.emit("activation", rs, sqs, AF.Sqrt, bias=epscol)
    v.wait(s.sem, msd)
    return v.emit("reciprocal", rs, rs)



def build_a0():
    nc = bass.Bass("TRN2", target_bir_lowering=False)
    XS = nc.dram_tensor("XS", [NT, 128, D], F32, kind="ExternalInput").ap()
    XN = nc.dram_tensor("XN", [NT, 128, D], BF16, kind="ExternalOutput").ap()
    with (
        nc.sbuf_tensor("xs", [128, NT * D], F32) as xs,
        nc.sbuf_tensor("sq", [128, NT * D], F32) as sq,
        nc.sbuf_tensor("xn", [128, NT * D], BF16) as xn,
        nc.sbuf_tensor("sums", [128, NT], F32) as sums,
        nc.sbuf_tensor("sqs", [128, NT], F32) as sqs,
        nc.sbuf_tensor("mu", [128, NT], F32) as mu,
        nc.sbuf_tensor("rs", [128, NT], F32) as rs,
        nc.sbuf_tensor("epscol", [128, 1], F32) as epscol,
        nc.semaphore("dL") as dL,
        nc.semaphore("dS") as dS,
        nc.semaphore("vs") as vs,
        nc.semaphore("ss") as ss,
        nc.semaphore("gs") as gs,
        nc.Block() as blk,
    ):
        p = Prog(nc)
        E = p.engines({"vector": vs, "scalar": ss, "gpsimd": gs})
        v, s, g, sy = E["vector"], E["scalar"], E["gpsimd"], E["sync"]

        sy.emit_unchained("dma_start",
                          xs[:].rearrange("p (t f) -> p t f", t=NT),
                          XS.transpose([1, 0, 2]), incs=[(dL, 16)])
        g.emit("memset", epscol[:], EPS)
        s.wait(g.sem, 1)
        ln_stats(v, s, xs[:], sq[:], sums[:], sqs[:], mu[:], rs[:], epscol[:],
                 NT, D, v_ready=[(dL, 16)], s_ready=[(dL, 16)])
        for t in range(NT):
            v.emit("tensor_scalar", xn[:, t * D:(t + 1) * D],
                   xs[:, t * D:(t + 1) * D], mu[:, t:t + 1], rs[:, t:t + 1],
                   op0=ALU.subtract, op1=ALU.mult)
        g.wait(v.sem, v.n)
        g.emit_unchained("dma_start", XN.transpose([1, 0, 2]),
                         xn[:].rearrange("p (t f) -> p t f", t=NT),
                         incs=[(dS, 16)])
        sy.wait(dS, 16)
        p.build(blk)
    return nc


def run_a0(x_slices):
    nc = build_a0()
    in_maps = [{"XS": tm_host(xs)} for xs in x_slices]
    res = run_bass_kernel_spmd(nc, in_maps, core_ids=list(range(NC)))
    return [np.asarray(r["XN"]).reshape(T, D) for r in res.results]



TCA = 128
NCHA = S_ // TCA
CH5 = 512
NCH5 = S_ // CH5



def build_a1():
    nc = bass.Bass("TRN2", target_bir_lowering=False)
    XNT = nc.dram_tensor("XNT", [NCHA, 128, 8 * TCA], BF16, kind="ExternalInput").ap()
    WV = nc.dram_tensor("WV", [8, 128, 256], F32, kind="ExternalInput").ap()
    WK = nc.dram_tensor("WK", [8, 128, 68], F32, kind="ExternalInput").ap()
    IND = nc.dram_tensor("IND", [128, 256], F32, kind="ExternalInput").ap()
    CMASK = nc.dram_tensor("CMASK", [2, 128, 4], F32, kind="ExternalInput").ap()
    AB = nc.dram_tensor("AB", [128, 8], F32, kind="ExternalInput").ap()
    OUT = nc.dram_tensor("OUT", [2, 128, S_], BF16, kind="ExternalOutput").ap()

    import contextlib
    _st = contextlib.ExitStack()
    xnt = _st.enter_context(nc.sbuf_tensor("xnt", [128, 2 * 8 * TCA], BF16))
    swv = _st.enter_context(nc.sbuf_tensor("swv", [128, 8 * 256], BF16))
    swk = _st.enter_context(nc.sbuf_tensor("swk", [128, 8 * 68], BF16))
    sind = _st.enter_context(nc.sbuf_tensor("sind", [128, 2 * 128], F32))
    scm = _st.enter_context(nc.sbuf_tensor("scm", [128, 2 * 4], F32))
    sab = _st.enter_context(nc.sbuf_tensor("sab", [128, 8], F32))
    valsT = _st.enter_context(nc.sbuf_tensor("valsT", [128, S_], F32))
    rowsA = _st.enter_context(nc.sbuf_tensor("rowsA", [128, S_], F32))
    rowsB = _st.enter_context(nc.sbuf_tensor("rowsB", [128, S_], F32))
    rowsC = _st.enter_context(nc.sbuf_tensor("rowsC", [4, S_], F32))
    nB = _st.enter_context(nc.sbuf_tensor("nB", [128, S_], F32))
    cv = _st.enter_context(nc.sbuf_tensor("cv", [128, S_], F32))
    sinv1 = _st.enter_context(nc.sbuf_tensor("sinv1", [128, TCA], F32))
    sinv2 = _st.enter_context(nc.sbuf_tensor("sinv2", [128, 2 * TCA], F32))
    ones_col = _st.enter_context(nc.sbuf_tensor("ones_col", [128, 1], F32))
    psV = _st.enter_context(nc.psum_tensor("psV", [128, 2 * 512], F32))
    psC = _st.enter_context(nc.psum_tensor("psC", [68, 512], F32))
    psP1 = _st.enter_context(nc.psum_tensor("psP1", [4, 512], F32))
    psP2 = _st.enter_context(nc.psum_tensor("psP2", [4, 512], F32))
    psB = _st.enter_context(nc.psum_tensor("psB", [128, 2 * 512], F32))
    dW = _st.enter_context(nc.semaphore("dW"))
    dWg = _st.enter_context(nc.semaphore("dWg"))
    dX0 = _st.enter_context(nc.semaphore("dX0"))
    dX1 = _st.enter_context(nc.semaphore("dX1"))
    dS = _st.enter_context(nc.semaphore("dS"))
    vs = _st.enter_context(nc.semaphore("vs"))
    ss = _st.enter_context(nc.semaphore("ss"))
    gs = _st.enter_context(nc.semaphore("gs"))
    ts_ = _st.enter_context(nc.semaphore("ts"))
    blk = _st.enter_context(nc.Block())
    with _st:
        p = Prog(nc)
        E = p.engines({"vector": vs, "scalar": ss, "gpsimd": gs})
        v, s, g, sy, t = (E["vector"], E["scalar"], E["gpsimd"], E["sync"],
                          E["tensor"])
        tn = [0]

        def tmm(*args, waits=(), inc=False, **kw):
            incs = []
            if inc:
                tn[0] += 1
                incs = [(ts_, 1)]
            t.emit_unchained("matmul", *args, waits=waits, incs=incs, **kw)
            return tn[0]

        g.emit_unchained("dma_start", swv[:].rearrange("p (c m) -> p c m", c=8),
                         WV.transpose([1, 0, 2]), incs=[(dWg, 16)])
        g.emit_unchained("dma_start", swk[:].rearrange("p (c m) -> p c m", c=8),
                         WK.transpose([1, 0, 2]), incs=[(dWg, 16)])
        sy.emit_unchained("dma_start", sind[:], IND, incs=[(dW, 16)])
        sy.emit_unchained("dma_start", scm[:].rearrange("p (c m) -> p c m", c=2),
                          CMASK.transpose([1, 0, 2]), incs=[(dW, 16)])
        sy.emit_unchained("dma_start", sab[:], AB, incs=[(dW, 16)])
        g.emit("memset", ones_col[:], 1.0)
        g.emit("iota", nB[:], [[1, S_]], channel_multiplier=0,
               allow_small_or_imprecise_dtypes=True)
        g.emit("memset", rowsA[:], 0.0)
        v.wait(dW, 48)
        v.wait(gs, 2)
        acols = sab[:].rearrange("p (c m) -> p c m", c=4)[:, :, 0:1]
        mab = v.emit("tensor_scalar_mul", acols, acols, 1.0 / 8191.0)

        M = {}
        last_sin = 0
        last_fold = {0: 0, 1: 0}
        ndma = [0, 0]
        mrows = 0
        for pair in range(2):
            pair_v0 = v.n
            if pair == 1:
                v.wait(dS, 16)
            for ct in range(NCHA):
                buf = ct % 2
                xch = xnt[:, buf * 8 * TCA:(buf + 1) * 8 * TCA]
                w_dma = []
                if ct >= 2:
                    w_dma = [(ts_, M[("xch", pair, ct - 2)])]
                elif pair == 1:
                    w_dma = [(ts_, M[("xch", 0, NCHA - 2 + ct)])]
                par = ct % 2
                dXp = dX0 if par == 0 else dX1
                sy.emit_unchained("dma_start", xch, XNT[ct],
                                  waits=w_dma, incs=[(dXp, 16)])
                ndma[par] += 1
                wait_first = [(dXp, 16 * ndma[par])]
                if pair == 0 and ct == 0:
                    wait_first.append((dW, 48))
                    wait_first.append((dWg, 32))
                if pair == 1 and ct == 0:
                    wait_first.append((vs, pair_v0))
                    wait_first.append((dS, 16))
                pk = ("v", pair, ct - 2) if ct >= 2 else ("v", 0, NCHA - 2 + ct)
                if pk in M:
                    wait_first.append((ss, M[pk]))
                pv = psV[:, (ct % 2) * 512:(ct % 2) * 512 + TCA]
                for f in range(8):
                    tmm(pv, swv[:, f * 256 + pair * 128:f * 256 + pair * 128 + 128],
                        xch[:, f * TCA:(f + 1) * TCA],
                        start=(f == 0), stop=(f == 7),
                        waits=(wait_first if f == 0 else ()),
                        inc=(f == 7))
                mV = tn[0]
                s.wait(ts_, mV)
                M[("v", pair, ct)] = s.emit(
                    "copy", valsT[:, ct * TCA:(ct + 1) * TCA], pv)
                if pair == 0:
                    wc = [(ss, M[("c", ct - 1)])] if ct > 0 else []
                    for f in range(8):
                        tmm(psC[:, 0:TCA], swk[:, f * 68:(f + 1) * 68],
                            xch[:, f * TCA:(f + 1) * TCA],
                            start=(f == 0), stop=(f == 7),
                            waits=(wc if f == 0 else ()),
                            inc=(f == 7))
                    mC = tn[0]
                    s.wait(ts_, mC)
                    if ct == 0:
                        s.wait(gs, 3)
                    mk = 0
                    for (po, ro) in ((0, 0), (32, 32), (64, 64)):
                        mk = s.emit("copy",
                                    rowsA[ro:ro + 4, ct * TCA:(ct + 1) * TCA],
                                    psC[po:po + 4, 0:TCA])
                    M[("c", ct)] = mk
                    for pp in range(2):
                        for pr in range(2):
                            col = 4 * pr + 2 * pp
                            wv_ = [(vs, mab)] if (ct == 0 and pp == 0 and
                                                  pr == 0) else []
                            if last_sin:
                                wv_.append((ss, last_sin))
                            va = v.emit("tensor_scalar", sinv1[:],
                                        nB[:, ct * TCA:(ct + 1) * TCA],
                                        sab[:, col:col + 1],
                                        sab[:, col + 1:col + 2],
                                        op0=ALU.mult, op1=ALU.add, waits=wv_)
                            ws_ = [(vs, va)]
                            if last_fold[pr]:
                                ws_.append((ts_, last_fold[pr]))
                            last_sin = s.emit(
                                "activation", sinv2[:, pr * TCA:(pr + 1) * TCA],
                                sinv1[:], AF.Sin, waits=ws_)
                            psP = psP1 if pp == 0 else psP2
                            wt_ = [(ss, last_sin)]
                            if pr == 0 and M.get(("p", pp, ct - 1)):
                                wt_.append((vs, M[("p", pp, ct - 1)]))
                            last_fold[pr] = tmm(
                                psP[:, 0:TCA], scm[:, pr * 4:(pr + 1) * 4],
                                sinv2[:, pr * TCA:(pr + 1) * TCA],
                                start=(pr == 0), stop=(pr == 1),
                                waits=wt_, inc=(pr == 1))
                        mP = tn[0]
                        v.wait(ts_, mP)
                        if pp == 0:
                            v.wait(ss, M[("c", ct)])
                            M[("p", 0, ct)] = v.emit(
                                "tensor_tensor",
                                rowsA[0:4, ct * TCA:(ct + 1) * TCA],
                                rowsA[0:4, ct * TCA:(ct + 1) * TCA],
                                psP1[:, 0:TCA], op=ALU.add)
                        else:
                            M[("p", 1, ct)] = v.emit(
                                "tensor_copy",
                                rowsC[0:4, ct * TCA:(ct + 1) * TCA],
                                psP2[:, 0:TCA])
                M[("xch", pair, ct)] = tn[0]

            if pair == 0:
                v.emit("tensor_scalar_max", rowsA[:], rowsA[:], -20.0)
                mclip = v.emit("tensor_scalar_min", rowsA[:], rowsA[:], 20.0)
                s.wait(vs, mclip)
                mexp = s.emit("activation", rowsA[:], rowsA[:], AF.Exp)
                v.wait(ss, mexp)
                v.emit("tensor_scalar_max", rowsC[:], rowsC[:], -20.0)
                mclip2 = v.emit("tensor_scalar_min", rowsC[:], rowsC[:], 20.0)
                s.wait(vs, mclip2)
                mexp2 = s.emit("activation", rowsC[:], rowsC[:], AF.Exp)
                v.wait(ss, mexp2)
                ones4 = ones_col[0:4, 0:1].broadcast_to([4, S_])
                v.emit("tensor_tensor_scan", rowsB[32:36, :], ones4,
                       rowsA[0:4, :], 0.0, op0=ALU.mult, op1=ALU.add)
                v.emit("tensor_tensor", rowsB[0:4, :], rowsB[32:36, :],
                       rowsA[32:36, :], op=ALU.mult)
                v.emit("tensor_tensor", rowsB[64:68, :], rowsB[0:4, :],
                       rowsC[0:4, :], op=ALU.mult)
                v.emit("tensor_tensor", rowsB[64:68, :], rowsB[64:68, :],
                       rowsA[64:68, :], op=ALU.add)
                mrows = v.emit("reciprocal", rowsB[64:68, :], rowsB[64:68, :])

            pb_last_use = {0: 0, 1: 0}
            pbu = 0
            for ch in range(NCH5):
                half = pbu % 2
                pbu += 1
                pb = psB[:, half * 512:half * 512 + 512]
                wt_ = [(vs, mrows), (ss, M[("v", pair, NCHA - 1)])]
                if pb_last_use[half]:
                    wt_.append((vs, pb_last_use[half]))
                if pair == 1 and ch == 0:
                    wt_.append((vs, pair_v0))
                for h2 in range(2):
                    tmm(pb[:, h2 * 256:(h2 + 1) * 256],
                        sind[0:4, pair * 128:(pair + 1) * 128],
                        rowsA[0:4, ch * 512 + h2 * 256:ch * 512 + (h2 + 1) * 256],
                        start=True, stop=True,
                        waits=(wt_ if h2 == 0 else ()), inc=(h2 == 1))
                v.wait(ts_, tn[0])
                pb_last_use[half] = v.emit(
                    "tensor_tensor", cv[:, ch * 512:(ch + 1) * 512],
                    valsT[:, ch * 512:(ch + 1) * 512], pb, op=ALU.mult)
            ones128 = ones_col[:, 0:1].broadcast_to([128, S_])
            v.emit("tensor_tensor_scan", nB[:], ones128, cv[:], 0.0,
                   op0=ALU.mult, op1=ALU.add)
            outb = cv[:].bitcast(BF16)[:, 0:S_]
            for ch in range(NCH5):
                sl = slice(ch * 512, (ch + 1) * 512)
                for use, base, rows_src in (
                        (0, 32, rowsA[32:36, :]), (1, 0, rowsC[0:4, :]),
                        (2, 64, rowsA[64:68, :]), (3, 64, rowsB[64:68, :])):
                    half = pbu % 2
                    pbu += 1
                    pb = psB[:, half * 512:half * 512 + 512]
                    wt_ = []
                    if pb_last_use[half]:
                        wt_.append((vs, pb_last_use[half]))
                    for h2 in range(2):
                        tmm(pb[:, h2 * 256:(h2 + 1) * 256],
                            sind[base:base + 4, pair * 128:(pair + 1) * 128],
                            rows_src[:, ch * 512 + h2 * 256:
                                     ch * 512 + (h2 + 1) * 256],
                            start=True, stop=True,
                            waits=(wt_ if h2 == 0 else ()), inc=(h2 == 1))
                    v.wait(ts_, tn[0])
                    if use in (0, 1):
                        pb_last_use[half] = v.emit(
                            "tensor_tensor", nB[:, sl], nB[:, sl], pb,
                            op=ALU.mult)
                    elif use == 2:
                        pb_last_use[half] = v.emit(
                            "tensor_tensor", valsT[:, sl], valsT[:, sl], pb,
                            op=ALU.mult)
                        v.emit("tensor_tensor", nB[:, sl], nB[:, sl],
                               valsT[:, sl], op=ALU.add)
                    else:
                        pb_last_use[half] = v.emit(
                            "tensor_tensor", outb[:, sl], nB[:, sl], pb,
                            op=ALU.mult)
            g.wait(vs, v.n)
            g.emit_unchained("dma_start", OUT[pair], outb,
                             incs=[(dS, 16)])
        sy.wait(dS, 32)
        p.build(blk)
    return nc


def prep_a1(ins, xnT_b):
    vw = ins["L0_attn_value_weight"]
    k1, k2, k3 = (ins["L0_attn_k1"], ins["L0_attn_k2"], ins["L0_attn_k3"])
    a1, a2 = ins["L0_attn_a1"], ins["L0_attn_a2"]
    b1, b2 = ins["L0_attn_b1"], ins["L0_attn_b2"]
    cc = ins["L0_attn_c"]
    maps = []
    for c in range(NC):
        b, gq = c // 4, c % 4
        hs = slice(4 * gq, 4 * gq + 4)
        WVh = np.ascontiguousarray(
            vw[hs].transpose(1, 0, 2).reshape(D, 256))
        WKh = np.zeros((D, 68), np.float32)
        WKh[:, 0:4] = k1[hs].T
        WKh[:, 64:68] = k2[hs].T
        WKh[:, 32:36] = k3[hs].T
        INDh = np.zeros((128, 256), np.float32)
        for pr in range(2):
            for col in range(128):
                for base in (0, 32, 64):
                    INDh[base + 2 * pr + col // 64, pr * 128 + col] = 1.0
        CM = np.zeros((2, 128, 4), np.float32)
        for pr in range(2):
            for j in (2 * pr, 2 * pr + 1):
                CM[pr, (j - 2 * pr) * 64:(j - 2 * pr) * 64 + 64, j] =                     cc[4 * gq + j, :, 0]
        ABh = np.zeros((128, 8), np.float32)
        for pr in range(2):
            for pp, (aa, bb) in enumerate(((a1, b1), (a2, b2))):
                col = 4 * pr + 2 * pp
                ABh[0:64, col] = aa[4 * gq + 2 * pr, :, 0]
                ABh[64:128, col] = aa[4 * gq + 2 * pr + 1, :, 0]
                ABh[0:64, col + 1] = bb[4 * gq + 2 * pr, :, 0]
                ABh[64:128, col + 1] = bb[4 * gq + 2 * pr + 1, :, 0]
        maps.append({"XNT": xnT_b[b], "WV": fm_host(WVh), "WK": fm_host(WKh),
                     "IND": INDh, "CMASK": CM, "AB": ABh})
    return maps


def run_a1(ins, xn_full):
    xnT_b = []
    for b in range(2):
        xt = np.ascontiguousarray(xn_full[b].T)
        x8 = xt.reshape(8, 128, NCHA, TCA)
        xnT_b.append(np.ascontiguousarray(x8.transpose(2, 1, 0, 3)
                                          ).reshape(NCHA, 128, 8 * TCA))
    nc = build_a1()
    maps = prep_a1(ins, xnT_b)
    res = run_bass_kernel_spmd(nc, maps, core_ids=list(range(NC)))
    outT = np.zeros((2, 16 * 64, S_), BF)
    for c in range(NC):
        b, gq = c // 4, c % 4
        o = np.asarray(res.results[c]["OUT"])
        outT[b, gq * 256:gq * 256 + 128] = o[0]
        outT[b, gq * 256 + 128:gq * 256 + 256] = o[1]
    return outT



def build_proj_boundary(emit_ln=False):
    nc = bass.Bass("TRN2", target_bir_lowering=False)
    XS = nc.dram_tensor("XS", [NT, 128, D], F32, kind="ExternalInput").ap()
    CT = nc.dram_tensor("CT", [8, 128, T], BF16, kind="ExternalInput").ap()
    WO = nc.dram_tensor("WO", [8, 128, D], F32, kind="ExternalInput").ap()
    XO = nc.dram_tensor("XO", [NT, 128, D], F32, kind="ExternalOutput").ap()
    if emit_ln:
        XT = nc.dram_tensor("XT", [NT, 128, D], F32, kind="ExternalOutput").ap()
    import contextlib
    _st = contextlib.ExitStack()
    xs = _st.enter_context(nc.sbuf_tensor("xs", [128, NT * D], F32))
    ct_ = _st.enter_context(nc.sbuf_tensor("ct", [128, 8 * T], BF16))
    swo = _st.enter_context(nc.sbuf_tensor("swo", [128, 8 * D], BF16))
    xout = _st.enter_context(nc.sbuf_tensor("xout", [128, NT * D], F32))
    xtn = _st.enter_context(nc.sbuf_tensor("xtn", [128, D], F32))
    sq = _st.enter_context(nc.sbuf_tensor("sq", [128, D], F32))
    cols = _st.enter_context(nc.sbuf_tensor("cols", [128, 8], F32))
    st_sums = _st.enter_context(nc.sbuf_tensor("st_sums", [128, NT], F32))
    st_sqs = _st.enter_context(nc.sbuf_tensor("st_sqs", [128, NT], F32))
    st_mu = _st.enter_context(nc.sbuf_tensor("st_mu", [128, NT], F32))
    st_rs = _st.enter_context(nc.sbuf_tensor("st_rs", [128, NT], F32))
    epscol = _st.enter_context(nc.sbuf_tensor("epscol", [128, 1], F32))
    ts_ = _st.enter_context(nc.semaphore("ts"))
    psY = _st.enter_context(nc.psum_tensor("psY", [128, 2 * 1024], F32))
    dL = _st.enter_context(nc.semaphore("dL"))
    dWg = _st.enter_context(nc.semaphore("dWg"))
    dS = _st.enter_context(nc.semaphore("dS"))
    vs = _st.enter_context(nc.semaphore("vs"))
    ss = _st.enter_context(nc.semaphore("ss"))
    gs = _st.enter_context(nc.semaphore("gs"))
    blk = _st.enter_context(nc.Block())
    with _st:
        p = Prog(nc)
        E = p.engines({"vector": vs, "scalar": ss, "gpsimd": gs})
        v, s, g, sy, t = (E["vector"], E["scalar"], E["gpsimd"], E["sync"],
                          E["tensor"])
        tn = [0]

        def tmm(*args, waits=(), inc=False, **kw):
            incs = []
            if inc:
                tn[0] += 1
                incs = [(ts_, 1)]
            t.emit_unchained("matmul", *args, waits=waits, incs=incs, **kw)
            return tn[0]

        sy.emit_unchained("dma_start", xs[:].rearrange("p (t f) -> p t f", t=NT),
                          XS.transpose([1, 0, 2]), incs=[(dL, 16)])
        sy.emit_unchained("dma_start", ct_[:].rearrange("p (c m) -> p c m", c=8),
                          CT.transpose([1, 0, 2]), incs=[(dL, 16)])
        g.emit_unchained("dma_start", swo[:].rearrange("p (c m) -> p c m", c=8),
                         WO.transpose([1, 0, 2]), incs=[(dWg, 16)])
        g.emit("memset", epscol[:], EPS)

        apply_mark = {}
        for tt in range(NT):
            half = tt % 2
            ps = psY[:, half * 1024:(half + 1) * 1024]
            w0 = [(dL, 32), (dWg, 16)] if tt == 0 else []
            if apply_mark.get(tt - 2):
                w0 = [(vs, apply_mark[tt - 2])]
            for mch in range(2):
                for f in range(8):
                    tmm(ps[:, mch * 512:(mch + 1) * 512],
                        ct_[:, f * T + tt * 128:f * T + (tt + 1) * 128],
                        swo[:, f * D + mch * 512:f * D + (mch + 1) * 512],
                        start=(f == 0), stop=(f == 7),
                        waits=(w0 if (mch == 0 and f == 0) else ()),
                        inc=(f == 7 and mch == 1))
            mY = tn[0]
            v.wait(ts_, mY)
            mrd = v.emit("tensor_reduce", cols[:, 0:1], ps, axis=AX.X,
                         op=ALU.add)
            s.wait(vs, mrd)
            if tt == 0:
                s.wait(g.sem, 1)
            msq = s.emit("activation", sq[:], ps, AF.Square,
                         accum_out=cols[:, 1:2])
            v.wait(ss, msq)
            v.emit("tensor_scalar_mul", cols[:, 2:3], cols[:, 0:1], 1.0 / D)
            v.emit("tensor_scalar_mul", cols[:, 3:4], cols[:, 1:2], 1.0 / D)
            v.emit("tensor_tensor", cols[:, 4:5], cols[:, 2:3], cols[:, 2:3],
                   op=ALU.mult)
            mv = v.emit("tensor_tensor", cols[:, 5:6], cols[:, 3:4],
                        cols[:, 4:5], op=ALU.subtract)
            s.wait(vs, mv)
            msd = s.emit("activation", cols[:, 6:7], cols[:, 5:6], AF.Sqrt,
                         bias=epscol[:])
            v.wait(ss, msd)
            v.emit("reciprocal", cols[:, 7:8], cols[:, 6:7])
            sl = slice(tt * D, (tt + 1) * D)
            v.emit("tensor_scalar", xtn[:], ps, cols[:, 2:3], cols[:, 7:8],
                   op0=ALU.subtract, op1=ALU.mult, waits=[(dL, 32)])
            apply_mark[tt] = v.emit("tensor_tensor", xout[:, sl], xs[:, sl],
                                    xtn[:], op=ALU.add)
        g.wait(vs, v.n)
        g.emit_unchained("dma_start", XO.transpose([1, 0, 2]),
                         xout[:].rearrange("p (t f) -> p t f", t=NT),
                         incs=[(dS, 16)])
        nout = 16
        if emit_ln:
            for tt in range(NT):
                sl = slice(tt * D, (tt + 1) * D)
                v.emit("tensor_reduce", cols[:, 0:1], xout[:, sl], axis=AX.X,
                       op=ALU.add)
                msq = s.emit("activation", sq[:], xout[:, sl], AF.Square,
                             accum_out=cols[:, 1:2], waits=[(vs, v.n)])
                v.wait(ss, msq)
                v.emit("tensor_scalar_mul", cols[:, 2:3], cols[:, 0:1], 1.0 / D)
                v.emit("tensor_scalar_mul", cols[:, 3:4], cols[:, 1:2], 1.0 / D)
                v.emit("tensor_tensor", cols[:, 4:5], cols[:, 2:3],
                       cols[:, 2:3], op=ALU.mult)
                mv = v.emit("tensor_tensor", cols[:, 5:6], cols[:, 3:4],
                            cols[:, 4:5], op=ALU.subtract)
                s.wait(vs, mv)
                msd = s.emit("activation", cols[:, 6:7], cols[:, 5:6], AF.Sqrt,
                             bias=epscol[:])
                v.wait(ss, msd)
                v.emit("reciprocal", cols[:, 7:8], cols[:, 6:7])
                v.emit("tensor_scalar", xs[:, sl], xout[:, sl], cols[:, 2:3],
                       cols[:, 7:8], op0=ALU.subtract, op1=ALU.mult)
            g.wait(vs, v.n)
            g.emit_unchained("dma_start", XT.transpose([1, 0, 2]),
                             xs[:].rearrange("p (t f) -> p t f", t=NT),
                             incs=[(dS, 16)])
            nout = 32
        sy.wait(dS, nout)
        p.build(blk)
    return nc


def run_proj_boundary(x_slices, catT_slices, W, emit_ln=False):
    nc = build_proj_boundary(emit_ln)
    maps = []
    for c in range(NC):
        maps.append({"XS": tm_host(x_slices[c]),
                     "CT": np.ascontiguousarray(
                         catT_slices[c].reshape(8, 128, T)),
                     "WO": fm_host(W)})
    res = run_bass_kernel_spmd(nc, maps, core_ids=list(range(NC)))
    xo = [np.asarray(r["XO"]).reshape(T, D) for r in res.results]
    xt = ([np.asarray(r["XT"]).reshape(T, D) for r in res.results]
          if emit_ln else None)
    return xo, xt



NE = 9
NI = 16
TH = 1024


def build_b():
    nc = bass.Bass("TRN2", target_bir_lowering=False)
    X1 = nc.dram_tensor("X1", [NT, 128, D], F32, kind="ExternalInput").ap()
    LNW = nc.dram_tensor("LNW", [1, D], F32, kind="ExternalInput").ap()
    LNB = nc.dram_tensor("LNB", [1, D], F32, kind="ExternalInput").ap()
    RW = nc.dram_tensor("RW", [8, 128, 8], F32, kind="ExternalInput").ap()
    DESC = nc.dram_tensor("DESC", [128, 8], F32, kind="ExternalInput").ap()
    IDTF = nc.dram_tensor("IDTF", [128, 128], F32, kind="ExternalInput").ap()
    IDTB = nc.dram_tensor("IDTB", [128, 128], BF16, kind="ExternalInput").ap()
    INDE = nc.dram_tensor("INDE", [8, 8 * 128], F32, kind="ExternalInput").ap()
    GUH = nc.dram_tensor("GUH", [NE, 2, NI, 8, 128, 128], F32,
                         kind="ExternalInput").ap()
    DNH = nc.dram_tensor("DNH", [NE, NI, 8, 128, 128], F32,
                         kind="ExternalInput").ap()
    X2 = nc.dram_tensor("X2", [NT, 128, D], F32, kind="ExternalOutput").ap()

    import contextlib
    _st = contextlib.ExitStack()
    x1t = _st.enter_context(nc.sbuf_tensor("x1t", [128, 2 * D], F32))
    xfT = _st.enter_context(nc.sbuf_tensor("xfT", [128, 8 * T], BF16))
    xeT = _st.enter_context(nc.sbuf_tensor("xeT", [128, 8 * TH], BF16))
    acc = _st.enter_context(nc.sbuf_tensor("acc", [128, 8 * TH], F32))
    hmT = _st.enter_context(nc.sbuf_tensor("hmT", [128, NI * TH], BF16))
    gub = _st.enter_context(nc.sbuf_tensor("gub", [128, 2 * 2 * 8 * 128], BF16))
    dnb = _st.enter_context(nc.sbuf_tensor("dnb", [128, NI * 8 * 128], BF16))
    srw = _st.enter_context(nc.sbuf_tensor("srw", [128, 8 * 8], BF16))
    slnw = _st.enter_context(nc.sbuf_tensor("slnw", [1, D], F32))
    slnb = _st.enter_context(nc.sbuf_tensor("slnb", [1, D], F32))
    lgB = _st.enter_context(nc.sbuf_tensor("lgB", [128, D], F32))
    lbB = _st.enter_context(nc.sbuf_tensor("lbB", [128, D], F32))
    sdesc = _st.enter_context(nc.sbuf_tensor("sdesc", [128, 8], F32))
    sidf = _st.enter_context(nc.sbuf_tensor("sidf", [128, 128], F32))
    sidb = _st.enter_context(nc.sbuf_tensor("sidb", [128, 128], BF16))
    sinde = _st.enter_context(nc.sbuf_tensor("sinde", [8, 8 * 128], F32))
    wrows = _st.enter_context(nc.sbuf_tensor("wrows", [8, T], F32))
    wb_sb = _st.enter_context(nc.sbuf_tensor("wb_sb", [128, TH], F32))
    scr1 = _st.enter_context(nc.sbuf_tensor("scr1", [128, D], F32))
    scr2 = _st.enter_context(nc.sbuf_tensor("scr2", [128, D], F32))
    scr4 = _st.enter_context(nc.sbuf_tensor("scr4", [128, D], F32))
    wcols = _st.enter_context(nc.sbuf_tensor("wcols", [128, NT * 8], F32))
    cols = _st.enter_context(nc.sbuf_tensor("cols", [128, 16], F32))
    epscol = _st.enter_context(nc.sbuf_tensor("epscol", [128, 1], F32))
    onesf = _st.enter_context(nc.sbuf_tensor("onesf", [1, 128], F32))
    psA = _st.enter_context(nc.psum_tensor("psA", [128, 2 * 512], F32))
    psB_ = _st.enter_context(nc.psum_tensor("psB_", [128, 2 * 512], F32))
    psD = _st.enter_context(nc.psum_tensor("psD", [128, 2 * 512], F32))
    psW = _st.enter_context(nc.psum_tensor("psW", [128, 2 * 512], F32))
    dL = _st.enter_context(nc.semaphore("dL"))
    dX0 = _st.enter_context(nc.semaphore("dX0"))
    dX1 = _st.enter_context(nc.semaphore("dX1"))
    dG0 = _st.enter_context(nc.semaphore("dG0"))
    dG1 = _st.enter_context(nc.semaphore("dG1"))
    dDn = _st.enter_context(nc.semaphore("dDn"))
    dRw = _st.enter_context(nc.semaphore("dRw"))
    dS = _st.enter_context(nc.semaphore("dS"))
    vs = _st.enter_context(nc.semaphore("vs"))
    ss = _st.enter_context(nc.semaphore("ss"))
    gs = _st.enter_context(nc.semaphore("gs"))
    ts_ = _st.enter_context(nc.semaphore("ts"))
    with _st:
        blk = _st.enter_context(nc.Block())
        p = Prog(nc)
        E = p.engines({"vector": vs, "scalar": ss, "gpsimd": gs})
        v, s, g, sy, t = (E["vector"], E["scalar"], E["gpsimd"], E["sync"],
                          E["tensor"])
        tn = [0]

        def tmm(*args, waits=(), inc=False, **kw):
            incs = []
            if inc:
                tn[0] += 1
                incs = [(ts_, 1)]
            t.emit_unchained("matmul", *args, waits=waits, incs=incs, **kw)
            return tn[0]

        def ttr(out, in_, ident, waits=()):
            tn[0] += 1
            t.emit_unchained("transpose", out, in_, ident, waits=waits,
                             incs=[(ts_, 1)])
            return tn[0]

        def ln_tile(src_ap, dst_ap, extra_s_wait=None):
            v.emit("tensor_reduce", cols[:, 0:1], src_ap, axis=AX.X,
                   op=ALU.add)
            wts = [(vs, v.n)]
            if extra_s_wait is not None:
                wts.append(extra_s_wait)
            msq = s.emit("activation", scr2[:], src_ap, AF.Square,
                         accum_out=cols[:, 1:2], waits=wts)
            v.wait(ss, msq)
            v.emit("tensor_scalar_mul", cols[:, 2:3], cols[:, 0:1], 1.0 / D)
            v.emit("tensor_scalar_mul", cols[:, 3:4], cols[:, 1:2], 1.0 / D)
            v.emit("tensor_tensor", cols[:, 4:5], cols[:, 2:3], cols[:, 2:3],
                   op=ALU.mult)
            mv = v.emit("tensor_tensor", cols[:, 5:6], cols[:, 3:4],
                        cols[:, 4:5], op=ALU.subtract)
            s.wait(vs, mv)
            msd = s.emit("activation", cols[:, 6:7], cols[:, 5:6], AF.Sqrt,
                         bias=epscol[:])
            v.wait(ss, msd)
            v.emit("reciprocal", cols[:, 7:8], cols[:, 6:7])
            return v.emit("tensor_scalar", dst_ap, src_ap, cols[:, 2:3],
                          cols[:, 7:8], op0=ALU.subtract, op1=ALU.mult)

        sy.emit_unchained("dma_start", slnw[:], LNW, incs=[(dL, 16)])
        sy.emit_unchained("dma_start", slnb[:], LNB, incs=[(dL, 16)])
        sy.emit_unchained("dma_start", sdesc[:], DESC, incs=[(dL, 16)])
        sy.emit_unchained("dma_start", sidf[:], IDTF, incs=[(dL, 16)])
        sy.emit_unchained("dma_start", sidb[:], IDTB, incs=[(dL, 16)])
        sy.emit_unchained("dma_start", sinde[:], INDE, incs=[(dL, 16)])
        g.emit_unchained("dma_start", srw[:].rearrange("p (c m) -> p c m", c=8),
                         RW.transpose([1, 0, 2]), incs=[(dRw, 16)])
        g.emit("memset", epscol[:], EPS)
        g.emit("memset", onesf[:], 1.0)
        mg0 = g.n

        for j in range(4):
            tmm(psW[:, j * 256:(j + 1) * 256], onesf[:],
                slnw[:, j * 256:(j + 1) * 256], start=True, stop=True,
                waits=([(dL, 96), (gs, mg0)] if j == 0 else ()), inc=(j == 3))
        for j in range(4):
            tmm(psD[:, j * 256:(j + 1) * 256], onesf[:],
                slnb[:, j * 256:(j + 1) * 256], start=True, stop=True,
                inc=(j == 3))
        v.wait(ts_, tn[0])
        v.emit("tensor_copy", lgB[:], psW[:, 0:1024])
        mlg = v.emit("tensor_copy", lbB[:], psD[:, 0:1024])

        nx = [0, 0]
        xf_done = 0
        last_tr = 0
        for tt in range(NT):
            par = tt % 2
            dXp = dX0 if par == 0 else dX1
            wdma = [(vs, xf_done)] if tt >= 2 else []
            sy.emit_unchained("dma_start", x1t[:, par * D:(par + 1) * D], X1[tt],
                              waits=wdma, incs=[(dXp, 16)])
            nx[par] += 1
            v.wait(dXp, 16 * nx[par])
            x1sl = x1t[:, par * D:(par + 1) * D]
            ln_tile(x1sl, scr1[:], extra_s_wait=(dXp, 16 * nx[par]))
            ln_tile(scr1[:], scr2[:])
            v.emit("tensor_tensor", scr2[:], scr2[:], lgB[:], op=ALU.mult,
                   waits=[(vs, mlg)] if tt == 0 else ())
            wxf = [(ts_, last_tr)] if tt > 0 else []
            mxf = v.emit("tensor_tensor", scr4[:].bitcast(BF16)[:, 0:D],
                         scr2[:], lbB[:], op=ALU.add, waits=wxf)
            trm = {}
            for f in range(8):
                half = f % 2
                wtr = [(vs, mxf)]
                if f >= 2:
                    wtr.append((vs, trm[f - 2]))
                mtr = ttr(psW[:, half * 512:half * 512 + 64].bitcast(BF16),
                          scr4[:].bitcast(BF16)[:, f * 128:(f + 1) * 128],
                          sidb[:], waits=wtr)
                v.wait(ts_, mtr)
                trm[f] = v.emit(
                    "tensor_copy",
                    xfT[:, f * T + tt * 128:f * T + (tt + 1) * 128],
                    psW[:, half * 512:half * 512 + 64].bitcast(BF16)[:, 0:128])
            xf_done = trm[7]
            last_tr = tn[0]

        M_rt = {}
        for tt in range(NT):
            w0 = [(vs, xf_done), (dRw, 16)]
            if tt >= 1:
                w0.append((vs, M_rt[tt - 1]))
            for f in range(8):
                tmm(psD[:, 0:8],
                    xfT[:, f * T + tt * 128:f * T + (tt + 1) * 128],
                    srw[:, f * 8:(f + 1) * 8], start=(f == 0), stop=(f == 7),
                    waits=(w0 if f == 0 else ()), inc=(f == 7))
            mR = tn[0]
            v.wait(ts_, mR)
            v.emit("reduce_max", cols[:, 8:9], psD[:, 0:8], axis=AX.X)
            v.emit("tensor_scalar", wcols[:, tt * 8:(tt + 1) * 8], psD[:, 0:8],
                   cols[:, 8:9], None, op0=ALU.is_ge)
            v.emit("tensor_tensor", wcols[:, tt * 8:(tt + 1) * 8],
                   wcols[:, tt * 8:(tt + 1) * 8], sdesc[:], op=ALU.mult)
            v.emit("reduce_max", cols[:, 9:10], wcols[:, tt * 8:(tt + 1) * 8],
                   axis=AX.X)
            mse = s.emit("activation", cols[:, 10:11], cols[:, 8:9],
                         AF.Sigmoid, waits=[(vs, v.n)])
            v.wait(ss, mse)
            v.emit("tensor_scalar", wcols[:, tt * 8:(tt + 1) * 8],
                   wcols[:, tt * 8:(tt + 1) * 8], cols[:, 9:10], None,
                   op0=ALU.is_equal)
            M_rt[tt] = v.emit("tensor_scalar", wcols[:, tt * 8:(tt + 1) * 8],
                              wcols[:, tt * 8:(tt + 1) * 8], cols[:, 10:11],
                              None, op0=ALU.mult)
        wr_last = 0
        for tt in range(NT):
            wtr = [(vs, M_rt[tt])]
            if wr_last:
                wtr.append((vs, wr_last))
            mtr = ttr(psW[0:8, 0:128], wcols[:, tt * 8:(tt + 1) * 8], sidf[:],
                      waits=wtr)
            v.wait(ts_, mtr)
            wr_last = v.emit("tensor_copy", wrows[0:8, tt * 128:(tt + 1) * 128],
                             psW[0:8, 0:128])

        ngu = [0, 0]
        ndn = 0
        out_dmas = 0
        for half in range(2):
            hsl = slice(half * TH, (half + 1) * TH)
            macc0 = g.emit("memset", acc[:], 0.0,
                           waits=[(vs, v.n)] if half else [])
            m_dn_done = 0
            eps_m = {}
            for e in range(NE):
                if e < 8:
                    wmm0 = [(vs, wr_last)]
                    if m_dn_done:
                        wmm0.append((vs, m_dn_done))
                    for j in range(4):
                        tmm(psW[:, (j % 2) * 512:(j % 2) * 512 + 512][:, 0:256]
                            if False else
                            psW[:, j * 256:(j + 1) * 256],
                            sinde[:, e * 128:(e + 1) * 128],
                            wrows[0:8, half * TH + j * 256:
                                  half * TH + (j + 1) * 256],
                            start=True, stop=True,
                            waits=(wmm0 if j == 0 else ()), inc=(j == 3))
                    mW = tn[0]
                    v.wait(ts_, mW)
                    mwb = v.emit("tensor_copy", wb_sb[:], psW[:, 0:1024])
                    xe_mark = 0
                    for f in range(8):
                        xe_mark = v.emit(
                            "tensor_tensor", xeT[:, f * TH:(f + 1) * TH],
                            xfT[:, f * T + half * TH:f * T + (half + 1) * TH],
                            wb_sb[:], op=ALU.mult)
                    act = xeT
                    act_stride = TH
                    act_off = 0
                else:
                    xe_mark = m_dn_done
                    act = xfT
                    act_stride = T
                    act_off = half * TH
                for i_ in range(NI):
                    g.emit_unchained(
                        "dma_start",
                        dnb[:, i_ * 1024:(i_ + 1) * 1024]
                        .rearrange("p (c m) -> p c m", c=8),
                        DNH[e, i_].transpose([1, 0, 2]),
                        waits=([(ts_, dn_mm_last)]
                               if (e + half > 0 and i_ == 0) else []),
                        incs=[(dDn, 16)])
                ndn += NI
                for i in range(NI):
                    par = i % 2
                    dGp = dG0 if par == 0 else dG1
                    wdma = []
                    if (e, i) != (0, 0) or half:
                        wdma = [(ts_, eps_m.get(("mm", e, i - 2),
                                                eps_m.get(("mmprev",), 0)))]
                    g.emit_unchained(
                        "dma_start",
                        gub[:, par * 2048:par * 2048 + 1024]
                        .rearrange("p (c m) -> p c m", c=8),
                        GUH[e, 0, i].transpose([1, 0, 2]),
                        waits=wdma, incs=[(dGp, 16)])
                    g.emit_unchained(
                        "dma_start",
                        gub[:, par * 2048 + 1024:par * 2048 + 2048]
                        .rearrange("p (c m) -> p c m", c=8),
                        GUH[e, 1, i].transpose([1, 0, 2]),
                        incs=[(dGp, 16)])
                    ngu[par] += 2
                    for h in range(2):
                        ph = (2 * i + h) % 2
                        tsl = slice(h * 512, (h + 1) * 512)
                        w0 = [(dGp, 16 * ngu[par]), (vs, xe_mark)]
                        if eps_m.get(("hm", e, i, h - 2)) or \
                           eps_m.get(("hm", e, i - 1, h)):
                            w0.append((vs, eps_m.get(("hm", e, i, h - 2),
                                      eps_m.get(("hm", e, i - 1, h), 0))))
                        for f in range(8):
                            tmm(psA[:, ph * 512:(ph + 1) * 512],
                                gub[:, par * 2048 + f * 128:
                                    par * 2048 + (f + 1) * 128],
                                act[:, f * act_stride + act_off + h * 512:
                                    f * act_stride + act_off + (h + 1) * 512],
                                start=(f == 0), stop=(f == 7),
                                waits=(w0 if f == 0 else ()), inc=(f == 7))
                        for f in range(8):
                            tmm(psB_[:, ph * 512:(ph + 1) * 512],
                                gub[:, par * 2048 + 1024 + f * 128:
                                    par * 2048 + 1024 + (f + 1) * 128],
                                act[:, f * act_stride + act_off + h * 512:
                                    f * act_stride + act_off + (h + 1) * 512],
                                start=(f == 0), stop=(f == 7), inc=(f == 7))
                        mGU = tn[0]
                        eps_m[("mm", e, i)] = mGU
                        s.wait(ts_, mGU)
                        if eps_m.get(("hmlast",)):
                            s.wait(vs, eps_m[("hmlast",)])
                        msl = s.emit("activation", scr2[:, 0:512],
                                     psA[:, ph * 512:(ph + 1) * 512], AF.Silu)
                        v.wait(ss, msl)
                        eps_m[("hm", e, i, h)] = v.emit(
                            "tensor_tensor",
                            hmT[:, i * TH + h * 512:i * TH + (h + 1) * 512],
                            scr2[:, 0:512], psB_[:, ph * 512:(ph + 1) * 512],
                            op=ALU.mult)
                        eps_m[("hmlast",)] = eps_m[("hm", e, i, h)]
                eps_m[("mmprev",)] = eps_m[("mm", e, NI - 1)]
                hm_done = eps_m[("hm", e, NI - 1, 1)]
                dn_mm_last = 0
                for mch in range(8):
                    for h2 in range(2):
                        ph = (2 * mch + h2) % 2
                        w0 = [(dDn, 16 * ndn), (vs, hm_done)]
                        if eps_m.get(("dacc", e, mch, h2 - 2)) or \
                           eps_m.get(("dacc", e, mch - 1, h2)):
                            w0.append((vs, eps_m.get(("dacc", e, mch, h2 - 2),
                                      eps_m.get(("dacc", e, mch - 1, h2), 0))))
                        for i in range(NI):
                            dn_mm_last = tmm(
                                psD[:, ph * 512:(ph + 1) * 512],
                                dnb[:, i * 1024 + mch * 128:
                                    i * 1024 + (mch + 1) * 128],
                                hmT[:, i * TH + h2 * 512:i * TH + (h2 + 1) * 512],
                                start=(i == 0), stop=(i == NI - 1),
                                waits=(w0 if i == 0 else ()),
                                inc=(i == NI - 1))
                        v.wait(ts_, dn_mm_last)
                        eps_m[("dacc", e, mch, h2)] = v.emit(
                            "tensor_tensor",
                            acc[:, mch * TH + h2 * 512:mch * TH + (h2 + 1) * 512],
                            acc[:, mch * TH + h2 * 512:mch * TH + (h2 + 1) * 512],
                            psD[:, ph * 512:(ph + 1) * 512], op=ALU.add)
                m_dn_done = eps_m[("dacc", e, 7, 1)]

            for tt_h in range(8):
                tt = half * 8 + tt_h
                par = tt % 2
                dXp = dX0 if par == 0 else dX1
                sy.emit_unchained("dma_start", x1t[:, par * D:(par + 1) * D],
                                  X1[tt], waits=[(vs, v.n)],
                                  incs=[(dXp, 16)])
                nx[par] += 1
                trl = 0
                for mch in range(8):
                    ph = mch % 2
                    wtr = [(vs, m_dn_done)]
                    if trl:
                        wtr.append((vs, trl))
                    mtr = ttr(psA[:, ph * 512:ph * 512 + 128],
                              acc[:, mch * TH + tt_h * 128:
                                  mch * TH + (tt_h + 1) * 128],
                              sidf[:], waits=wtr if mch < 2 else [(vs, trl)])
                    v.wait(ts_, mtr)
                    trl = v.emit("tensor_copy", scr1[:, mch * 128:(mch + 1) * 128],
                                 psA[:, ph * 512:ph * 512 + 128])
                v.wait(dXp, 16 * nx[par])
                ln_tile(x1t[:, par * D:(par + 1) * D], scr4[:],
                        extra_s_wait=(dXp, 16 * nx[par]))
                v.emit("tensor_tensor", scr1[:], scr1[:], scr4[:], op=ALU.add)
                ln_tile(scr1[:], scr2[:])
                v.emit("tensor_tensor", scr2[:], scr2[:],
                       x1t[:, par * D:(par + 1) * D], op=ALU.add)
                g.wait(vs, v.n)
                g.emit_unchained("dma_start", X2[tt], scr2[:],
                                 incs=[(dS, 16)])
                out_dmas += 1
                v.wait(dS, 16 * out_dmas)
        sy.wait(dS, 16 * out_dmas)
        p.build(blk)
    return nc


def prep_b(ins):
    gw, uw, dw = (ins["L0_ffn_gate_w"], ins["L0_ffn_up_w"],
                  ins["L0_ffn_down_w"])
    GUH = np.zeros((NE, 2, NI, 8, 128, 128), np.float32)
    DNH = np.zeros((NE, NI, 8, 128, 128), np.float32)
    for e in range(8):
        GUH[e, 0] = gw[e].reshape(8, 128, NI, 128).transpose(2, 0, 1, 3)
        GUH[e, 1] = uw[e].reshape(8, 128, NI, 128).transpose(2, 0, 1, 3)
        DNH[e] = dw[e].reshape(NI, 128, 8, 128).transpose(0, 2, 1, 3)
    GUH[8, 0] = ins["L0_ffn_sh_gate"].reshape(8, 128, NI, 128).transpose(2, 0, 1, 3)
    GUH[8, 1] = ins["L0_ffn_sh_up"].reshape(8, 128, NI, 128).transpose(2, 0, 1, 3)
    DNH[8] = ins["L0_ffn_sh_down"].reshape(NI, 128, 8, 128).transpose(0, 2, 1, 3)
    DESC = np.tile((8.0 - np.arange(8, dtype=np.float32)), (128, 1))
    IDTF = np.eye(128, dtype=np.float32)
    IDTB = np.eye(128, dtype=BF)
    INDE = np.zeros((8, 8 * 128), np.float32)
    for e in range(8):
        INDE[e, e * 128:(e + 1) * 128] = 1.0
    com = {"LNW": ins["L0_ffn_ln_w"].reshape(1, D).astype(np.float32),
           "LNB": ins["L0_ffn_ln_b"].reshape(1, D).astype(np.float32),
           "RW": fm_host(ins["L0_ffn_router_w"].astype(np.float32)),
           "DESC": DESC, "IDTF": IDTF, "IDTB": IDTB, "INDE": INDE,
           "GUH": GUH, "DNH": DNH}
    return com


def run_b(ins, x1_slices):
    nc = build_b()
    com = prep_b(ins)
    maps = [dict(com, X1=tm_host(x1_slices[c])) for c in range(NC)]
    res = run_bass_kernel_spmd(nc, maps, core_ids=list(range(NC)))
    return [np.asarray(r["X2"]).reshape(T, D) for r in res.results]



def build_c():
    nc = bass.Bass("TRN2", target_bir_lowering=False)
    X2 = nc.dram_tensor("X2", [NT, 128, D], F32, kind="ExternalInput").ap()
    WQ = nc.dram_tensor("WQ", [8, 128, 3072], F32, kind="ExternalInput").ap()
    IDTB = nc.dram_tensor("IDTB", [128, 128], BF16, kind="ExternalInput").ap()
    QKV = nc.dram_tensor("QKV", [NT, 128, 3072], BF16, kind="ExternalOutput").ap()

    import contextlib
    _st = contextlib.ExitStack()
    x2t = _st.enter_context(nc.sbuf_tensor("x2t", [128, 2 * D], F32))
    swq = _st.enter_context(nc.sbuf_tensor("swq", [128, 8 * 3072], BF16))
    sidb = _st.enter_context(nc.sbuf_tensor("sidb", [128, 128], BF16))
    t3T = _st.enter_context(nc.sbuf_tensor("t3T", [128, 2 * 8 * 128], BF16))
    qkvb = _st.enter_context(nc.sbuf_tensor("qkvb", [128, 2 * 3072], BF16))
    scr1 = _st.enter_context(nc.sbuf_tensor("scr1", [128, D], F32))
    scr2 = _st.enter_context(nc.sbuf_tensor("scr2", [128, D], F32))
    cols = _st.enter_context(nc.sbuf_tensor("cols", [128, 16], F32))
    epscol = _st.enter_context(nc.sbuf_tensor("epscol", [128, 1], F32))
    psQ = _st.enter_context(nc.psum_tensor("psQ", [128, 4 * 512], F32))
    psT = _st.enter_context(nc.psum_tensor("psT", [128, 2 * 512], F32))
    dL = _st.enter_context(nc.semaphore("dL"))
    dX0 = _st.enter_context(nc.semaphore("dX0"))
    dX1 = _st.enter_context(nc.semaphore("dX1"))
    dWg = _st.enter_context(nc.semaphore("dWg"))
    dS0 = _st.enter_context(nc.semaphore("dS0"))
    dS1 = _st.enter_context(nc.semaphore("dS1"))
    vs = _st.enter_context(nc.semaphore("vs"))
    ss = _st.enter_context(nc.semaphore("ss"))
    gs = _st.enter_context(nc.semaphore("gs"))
    ts_ = _st.enter_context(nc.semaphore("ts"))
    with _st:
        blk = _st.enter_context(nc.Block())
        p = Prog(nc)
        E = p.engines({"vector": vs, "scalar": ss, "gpsimd": gs})
        v, s, g, sy, t = (E["vector"], E["scalar"], E["gpsimd"], E["sync"],
                          E["tensor"])
        tn = [0]

        def tmm(*args, waits=(), inc=False, **kw):
            incs = []
            if inc:
                tn[0] += 1
                incs = [(ts_, 1)]
            t.emit_unchained("matmul", *args, waits=waits, incs=incs, **kw)
            return tn[0]

        def ttr(out, in_, ident, waits=()):
            tn[0] += 1
            t.emit_unchained("transpose", out, in_, ident, waits=waits,
                             incs=[(ts_, 1)])
            return tn[0]

        def ln_tile(src_ap, dst_ap, extra_s_wait=None):
            v.emit("tensor_reduce", cols[:, 0:1], src_ap, axis=AX.X,
                   op=ALU.add)
            wts = [(vs, v.n)]
            if extra_s_wait is not None:
                wts.append(extra_s_wait)
            msq = s.emit("activation", scr2[:], src_ap, AF.Square,
                         accum_out=cols[:, 1:2], waits=wts)
            v.wait(ss, msq)
            v.emit("tensor_scalar_mul", cols[:, 2:3], cols[:, 0:1], 1.0 / D)
            v.emit("tensor_scalar_mul", cols[:, 3:4], cols[:, 1:2], 1.0 / D)
            v.emit("tensor_tensor", cols[:, 4:5], cols[:, 2:3], cols[:, 2:3],
                   op=ALU.mult)
            mv = v.emit("tensor_tensor", cols[:, 5:6], cols[:, 3:4],
                        cols[:, 4:5], op=ALU.subtract)
            s.wait(vs, mv)
            msd = s.emit("activation", cols[:, 6:7], cols[:, 5:6], AF.Sqrt,
                         bias=epscol[:])
            v.wait(ss, msd)
            v.emit("reciprocal", cols[:, 7:8], cols[:, 6:7])
            return v.emit("tensor_scalar", dst_ap, src_ap, cols[:, 2:3],
                          cols[:, 7:8], op0=ALU.subtract, op1=ALU.mult)

        sy.emit_unchained("dma_start", sidb[:], IDTB, incs=[(dL, 16)])
        g.emit_unchained("dma_start", swq[:].rearrange("p (c m) -> p c m", c=8),
                         WQ.transpose([1, 0, 2]), incs=[(dWg, 16)])
        g.emit("memset", epscol[:], EPS)
        mg0 = g.n

        nx = [0, 0]
        ns = [0, 0]
        evac = {}
        for tt in range(NT):
            par = tt % 2
            dXp = dX0 if par == 0 else dX1
            wdma = [(vs, evac.get(("xf", tt - 2), 0))] if tt >= 2 else []
            sy.emit_unchained("dma_start", x2t[:, par * D:(par + 1) * D],
                              X2[tt], waits=wdma, incs=[(dXp, 16)])
            nx[par] += 1
            v.wait(dXp, 16 * nx[par])
            v.wait(gs, mg0)
            mxf = ln_tile(x2t[:, par * D:(par + 1) * D],
                          scr1[:].bitcast(BF16)[:, 0:D],
                          extra_s_wait=(dXp, 16 * nx[par]))
            evac[("xf", tt)] = mxf
            trm = {}
            for f in range(8):
                half = f % 2
                wtr = [(vs, mxf), (dL, 16)]
                if f >= 2:
                    wtr = [(vs, trm[f - 2])]
                mtr = ttr(psT[:, half * 512:half * 512 + 64].bitcast(BF16),
                          scr1[:].bitcast(BF16)[:, f * 128:(f + 1) * 128],
                          sidb[:], waits=wtr)
                v.wait(ts_, mtr)
                trm[f] = v.emit(
                    "tensor_copy",
                    t3T[:, par * 1024 + f * 128:par * 1024 + (f + 1) * 128],
                    psT[:, half * 512:half * 512 + 64].bitcast(BF16)[:, 0:128])
            for mch in range(6):
                ph = mch % 4
                w0 = [(vs, trm[7]), (dWg, 16)]
                if evac.get(("q", tt, mch - 4)) or evac.get(("q", tt - 1, mch + 2)):
                    w0.append((ss, evac.get(("q", tt, mch - 4),
                                            evac.get(("q", tt - 1, mch + 2), 0))))
                for f in range(8):
                    tmm(psQ[:, ph * 512:(ph + 1) * 512],
                        t3T[:, par * 1024 + f * 128:par * 1024 + (f + 1) * 128],
                        swq[:, f * 3072 + mch * 512:f * 3072 + (mch + 1) * 512],
                        start=(f == 0), stop=(f == 7),
                        waits=(w0 if f == 0 else ()), inc=(f == 7))
                s.wait(ts_, tn[0])
                if mch == 0 and tt >= 2:
                    s.wait(dS0 if par == 0 else dS1, 16 * ns[par])
                evac[("q", tt, mch)] = s.emit(
                    "copy", qkvb[:, par * 3072 + mch * 512:
                                 par * 3072 + (mch + 1) * 512],
                    psQ[:, ph * 512:(ph + 1) * 512])
            dSp = dS0 if par == 0 else dS1
            g.wait(ss, evac[("q", tt, 5)])
            if tt >= 2:
                g.wait(dSp, 16 * ns[par])
            g.emit_unchained("dma_start", QKV[tt],
                             qkvb[:, par * 3072:(par + 1) * 3072],
                             incs=[(dSp, 16)])
            ns[par] += 1
        sy.wait(dS0, 16 * ns[0])
        sy.wait(dS1, 16 * ns[1])
        p.build(blk)
    return nc


def run_c(ins, x2_slices):
    wq = ins["L1_attn_W_qkv"]
    WQh = np.ascontiguousarray(
        wq.transpose(3, 0, 1, 2).reshape(D, 3072)).astype(np.float32)
    nc = build_c()
    maps = [{"X2": tm_host(x2_slices[c]), "WQ": fm_host(WQh),
             "IDTB": np.eye(128, dtype=BF)} for c in range(NC)]
    res = run_bass_kernel_spmd(nc, maps, core_ids=list(range(NC)))
    return [np.asarray(r["QKV"]).reshape(T, 3072) for r in res.results]



def build_d():
    nc = bass.Bass("TRN2", target_bir_lowering=False)
    QT = nc.dram_tensor("QT", [4, 65, S_], BF16, kind="ExternalInput").ap()
    KT = nc.dram_tensor("KT", [4, 65, S_], BF16, kind="ExternalInput").ap()
    VA = nc.dram_tensor("VA", [4, S_, 65], BF16, kind="ExternalInput").ap()
    STRI = nc.dram_tensor("STRI", [128, 2048], F32, kind="ExternalInput").ap()
    OUT = nc.dram_tensor("OUT", [4, 64, S_], BF16, kind="ExternalOutput").ap()

    import contextlib
    _st = contextlib.ExitStack()
    qta = _st.enter_context(nc.sbuf_tensor("qta", [65, 2 * S_], BF16))
    kta = _st.enter_context(nc.sbuf_tensor("kta", [65, 2 * S_], BF16))
    va = _st.enter_context(nc.sbuf_tensor("va", [128, 2 * 64 * 65], BF16))
    sq_ = _st.enter_context(nc.sbuf_tensor("sq_", [64, S_], BF16))
    outb = _st.enter_context(nc.sbuf_tensor("outb", [64, S_], BF16))
    ptb4 = _st.enter_context(nc.sbuf_tensor("ptb4", [128, 4 * 512], BF16))
    ctxs = _st.enter_context(nc.sbuf_tensor("ctxs", [64, 512], F32))
    sstri = _st.enter_context(nc.sbuf_tensor("sstri", [128, 2048], F32))
    rl = _st.enter_context(nc.sbuf_tensor("rl", [1, 512], F32))
    kmax = _st.enter_context(nc.sbuf_tensor("kmax", [1, 2], F32))
    onesb = _st.enter_context(nc.sbuf_tensor("onesb", [64, 1], BF16))
    onesf = _st.enter_context(nc.sbuf_tensor("onesf", [1, 64], F32))
    psS4 = _st.enter_context(nc.psum_tensor("psS4", [128, 4 * 512], F32))
    psC = _st.enter_context(nc.psum_tensor("psC", [65, 512], F32))
    psN = _st.enter_context(nc.psum_tensor("psN", [1, 512], F32))
    psR = _st.enter_context(nc.psum_tensor("psR", [64, 512], F32))
    dL = _st.enter_context(nc.semaphore("dL"))
    dP0 = _st.enter_context(nc.semaphore("dP0"))
    dP1 = _st.enter_context(nc.semaphore("dP1"))
    dS = _st.enter_context(nc.semaphore("dS"))
    vs = _st.enter_context(nc.semaphore("vs"))
    ss = _st.enter_context(nc.semaphore("ss"))
    gs = _st.enter_context(nc.semaphore("gs"))
    ts_ = _st.enter_context(nc.semaphore("ts"))
    with _st:
        blk = _st.enter_context(nc.Block())
        p = Prog(nc)
        E = p.engines({"vector": vs, "scalar": ss, "gpsimd": gs})
        v, s, g, sy, t = (E["vector"], E["scalar"], E["gpsimd"], E["sync"],
                          E["tensor"])
        tn = [0]

        def tmm(*args, waits=(), inc=False, **kw):
            incs = []
            if inc:
                tn[0] += 1
                incs = [(ts_, 1)]
            t.emit_unchained("matmul", *args, waits=waits, incs=incs, **kw)
            return tn[0]

        sy.emit_unchained("dma_start", sstri[:], STRI, incs=[(dL, 16)])
        g.emit("memset", onesb[:], 1.0)
        g.emit("memset", onesf[:], 1.0)
        mg0 = g.n

        def load_pair(pair, waits):
            pb = pair % 2
            dPp = dP0 if pb == 0 else dP1
            sy.emit_unchained("dma_start", qta[:, pb * S_:(pb + 1) * S_],
                              QT[pair], waits=waits, incs=[(dPp, 16)])
            sy.emit_unchained("dma_start", kta[:, pb * S_:(pb + 1) * S_],
                              KT[pair], incs=[(dPp, 16)])
            sy.emit_unchained(
                "dma_start",
                va[:, pb * 64 * 65:(pb + 1) * 64 * 65]
                .rearrange("p (k c) -> p k c", k=64),
                VA[pair].rearrange("(k p) c -> p k c", p=128),
                incs=[(dPp, 16)])

        load_pair(0, [])
        load_pair(1, [])
        nload = [1, 1]
        last_ex = {0: 0, 1: 0}
        last_ctx = {0: 0, 1: 0}
        prev_sq = 0
        for pair in range(4):
            pb = pair % 2
            dPp = dP0 if pb == 0 else dP1
            qsl = qta[:, pb * S_:(pb + 1) * S_]
            ksl = kta[:, pb * S_:(pb + 1) * S_]
            vsl = va[:, pb * 64 * 65:(pb + 1) * 64 * 65]
            nwait = 16 * 3 * nload[pb]
            msqk = s.emit("activation", sq_[:], ksl[0:64, :], AF.Square,
                          waits=[(dPp, nwait), (vs, prev_sq)])
            knm = 0
            for ch in range(16):
                mq = tmm(psN[:], onesb[:], sq_[:, ch * 512:(ch + 1) * 512],
                         start=True, stop=True,
                         waits=([(ss, msqk), (gs, mg0)] if ch == 0 else
                                [(vs, knm)]),
                         inc=True)
                v.wait(ts_, mq)
                if ch == 0:
                    v.emit("reduce_max", kmax[:, 0:1], psN[:], axis=AX.X)
                else:
                    v.emit("reduce_max", kmax[:, 1:2], psN[:], axis=AX.X)
                    v.emit("tensor_tensor", kmax[:, 0:1], kmax[:, 0:1],
                           kmax[:, 1:2], op=ALU.max)
                knm = v.n
            mkh = v.emit("tensor_scalar_mul", kmax[:, 1:2], kmax[:, 0:1], -0.5)
            msqq = s.emit("activation", sq_[:], qsl[0:64, :], AF.Square,
                          waits=[(vs, knm)])
            qnm = 0
            for ch in range(16):
                mq = tmm(psN[:], onesb[:], sq_[:, ch * 512:(ch + 1) * 512],
                         start=True, stop=True,
                         waits=([(ss, msqq), (vs, mkh)] if ch == 0 else
                                [(vs, qnm)]),
                         inc=True)
                v.wait(ts_, mq)
                v.emit("tensor_scalar", qsl[64:65, ch * 512:(ch + 1) * 512],
                       psN[:], -0.5, kmax[:, 1:2], op0=ALU.mult, op1=ALU.add)
                qnm = v.n
            prev_sq = qnm
            maug = qnm

            mo_prev = 0
            mcs_prev = 0
            if pair > 0:
                v.wait(dS, 16 * pair)
            for qg in range(16):
                ngrp = qg + 1
                for g2 in range(ngrp):
                    diag = (g2 == ngrp - 1)
                    w0 = []
                    if qg == 0 and g2 == 0:
                        w0.append((vs, maug))
                    if last_ex[0]:
                        w0.append((ss, last_ex[0]))
                    for jj in range(4):
                        kt = 4 * g2 + jj
                        mS = tmm(psS4[:, jj * 512:(jj + 1) * 512],
                                 ksl[:, kt * 128:(kt + 1) * 128],
                                 qsl[:, qg * 512:(qg + 1) * 512],
                                 start=True, stop=True,
                                 waits=(w0 if jj == 0 else ()), inc=(jj == 3))
                    dmark = 0
                    if diag:
                        v.wait(ts_, mS)
                        dmark = v.emit(
                            "tensor_tensor", psS4[:], psS4[:], sstri[:],
                            op=ALU.add,
                            waits=[(dL, 16)] if (pair == 0 and qg == 0) else ())
                    ws = [(vs, dmark)] if dmark else [(ts_, mS)]
                    if last_ctx[0]:
                        ws.append((ts_, last_ctx[0]))
                    mex = s.emit("activation", ptb4[:], psS4[:], AF.Exp,
                                 scale=0.125, waits=ws)
                    last_ex[0] = mex
                    for jj in range(4):
                        kt = 4 * g2 + jj
                        wc = [(ss, mex)] if jj == 0 else []
                        if g2 == 0 and jj == 0:
                            if mo_prev:
                                wc.append((vs, mo_prev))
                            if mcs_prev:
                                wc.append((ss, mcs_prev))
                        last_ctx[0] = tmm(
                            psC[:], vsl[:, kt * 65:(kt + 1) * 65],
                            ptb4[:, jj * 512:(jj + 1) * 512],
                            start=(g2 == 0 and jj == 0),
                            stop=(g2 == ngrp - 1 and jj == 3),
                            waits=wc, inc=(jj == 3))
                v.wait(ts_, last_ctx[0])
                mrl = v.emit("reciprocal", rl[:], psC[64:65, :])
                for j2 in range(2):
                    tmm(psR[:, j2 * 256:(j2 + 1) * 256], onesf[:],
                        rl[:, j2 * 256:(j2 + 1) * 256], start=True, stop=True,
                        waits=([(vs, mrl), (vs, mo_prev)] if j2 == 0 else ()),
                        inc=(j2 == 1))
                mcs_prev = s.emit("copy", ctxs[:], psC[0:64, :],
                                  waits=[(vs, mrl)])
                v.wait(ts_, tn[0])
                v.wait(ss, mcs_prev)
                mo_prev = v.emit("tensor_tensor",
                                 outb[:, qg * 512:(qg + 1) * 512],
                                 ctxs[:], psR[:], op=ALU.mult)
            g.wait(vs, mo_prev)
            g.emit_unchained("dma_start", OUT[pair], outb[:], incs=[(dS, 16)])
            g.wait(dS, 16 * (pair + 1))
            if pair + 2 < 4:
                load_pair(pair + 2, [(ts_, tn[0]), (ss, s.n)])
                nload[pb] += 1
        sy.wait(dS, 64)
        p.build(blk)
    return nc


def d_masks():
    m = np.zeros((4, 128, 512), np.float32)
    tri = np.where(np.arange(128)[:, None] > np.arange(128)[None, :],
                   np.float32(-1e30), np.float32(0.0))
    for j in range(4):
        for jp in range(4):
            if jp < j:
                m[j][:, jp * 128:(jp + 1) * 128] = -1e30
            elif jp == j:
                m[j][:, jp * 128:(jp + 1) * 128] = tri
    return np.ascontiguousarray(m.transpose(1, 0, 2).reshape(128, 2048))


def run_d(ins, qkv_full):
    nc = build_d()
    stri = d_masks()
    maps = []
    for c in range(NC):
        b, gq = c // 4, c % 4
        QTh = np.zeros((4, 65, S_), BF)
        KTh = np.zeros((4, 65, S_), BF)
        VAh = np.zeros((4, S_, 65), BF)
        for j in range(4):
            h = 4 * gq + j
            QTh[j, 0:64] = qkv_full[b][:, h * 64:(h + 1) * 64].T
            KTh[j, 0:64] = qkv_full[b][:, 1024 + h * 64:1024 + (h + 1) * 64].T
            KTh[j, 64] = 1.0
            VAh[j, :, 0:64] = qkv_full[b][:, 2048 + h * 64:2048 + (h + 1) * 64]
            VAh[j, :, 64] = 1.0
        maps.append({"QT": QTh, "KT": KTh, "VA": VAh, "STRI": stri})
    res = run_bass_kernel_spmd(nc, maps, core_ids=list(range(NC)))
    catT = np.zeros((2, 1024, S_), BF)
    for c in range(NC):
        b, gq = c // 4, c % 4
        o = np.asarray(res.results[c]["OUT"])
        for j in range(4):
            h = 4 * gq + j
            catT[b, h * 64:(h + 1) * 64] = o[j]
    return catT



NLT = 20


def build_f1():
    nc = bass.Bass("TRN2", target_bir_lowering=False)
    H1T = nc.dram_tensor("H1T", [NLT, 8, 128, 128], F32, kind="ExternalInput").ap()
    XB = nc.dram_tensor("XB", [NT, 128, D], F32, kind="ExternalInput").ap()
    WQKV = nc.dram_tensor("WQKV", [8, 128, 1536], F32, kind="ExternalInput").ap()
    WO = nc.dram_tensor("WO", [4, 128, D], F32, kind="ExternalInput").ap()
    QKS = nc.dram_tensor("QKS", [128, 128], F32, kind="ExternalInput").ap()
    CST = nc.dram_tensor("CST", [NLT, 128, 128], F32, kind="ExternalInput").ap()
    SNT = nc.dram_tensor("SNT", [NLT, 128, 128], F32, kind="ExternalInput").ap()
    MBAND = nc.dram_tensor("MBAND", [8, 128, 512], F32, kind="ExternalInput").ap()
    KILLC = nc.dram_tensor("KILLC", [128, 8], F32, kind="ExternalInput").ap()
    IDTB = nc.dram_tensor("IDTB", [128, 128], BF16, kind="ExternalInput").ap()
    IDTF = nc.dram_tensor("IDTF", [128, 128], F32, kind="ExternalInput").ap()
    BSCR = nc.dram_tensor("BSCR", [8, 2048], F32).ap()
    Y1 = nc.dram_tensor("Y1", [NT, 128, D], F32, kind="ExternalOutput").ap()

    import contextlib
    _st = contextlib.ExitStack()
    h1t = _st.enter_context(nc.sbuf_tensor("h1t", [128, 8 * 128], BF16))
    swq = _st.enter_context(nc.sbuf_tensor("swq", [128, 8 * 1536], BF16))
    swo = _st.enter_context(nc.sbuf_tensor("swo", [128, 4 * D], BF16))
    sqks = _st.enter_context(nc.sbuf_tensor("sqks", [128, 128], F32))
    scst = _st.enter_context(nc.sbuf_tensor("scst", [128, NLT * 128], F32))
    ssnt = _st.enter_context(nc.sbuf_tensor("ssnt", [128, NLT * 128], F32))
    smb = _st.enter_context(nc.sbuf_tensor("smb", [128, 8 * 512], BF16))
    skill = _st.enter_context(nc.sbuf_tensor("skill", [128, 8], F32))
    sidb = _st.enter_context(nc.sbuf_tensor("sidb", [128, 128], BF16))
    sidf = _st.enter_context(nc.sbuf_tensor("sidf", [128, 128], F32))
    QA = [_st.enter_context(nc.sbuf_tensor(f"QA{h}", [65, T], BF16))
          for h in range(8)]
    KA = [_st.enter_context(nc.sbuf_tensor(f"KA{h}", [65, NLT * 128], BF16))
          for h in range(8)]
    vbuf = _st.enter_context(nc.sbuf_tensor("vbuf", [128, NLT * 520], BF16))
    oT = _st.enter_context(nc.sbuf_tensor("oT", [128, 4 * T], BF16))
    qn2T = _st.enter_context(nc.sbuf_tensor("qn2T", [8, NLT * 128], F32))
    kmaxs = _st.enter_context(nc.sbuf_tensor("kmaxs", [8, 32], F32))
    wm = _st.enter_context(nc.sbuf_tensor("wm", [8, 16], F32))
    scr_qk = _st.enter_context(nc.sbuf_tensor("scr_qk", [128, 1024], F32))
    t1b = _st.enter_context(nc.sbuf_tensor("t1b", [128, 1024], F32))
    rotb = _st.enter_context(nc.sbuf_tensor("rotb", [128, 1024], F32))
    nrm = _st.enter_context(nc.sbuf_tensor("nrm", [128, 48], F32))
    ptb4 = _st.enter_context(nc.sbuf_tensor("ptb4", [128, 4 * 512], BF16))
    ctxs = _st.enter_context(nc.sbuf_tensor("ctxs", [64, 512], F32))
    rl = _st.enter_context(nc.sbuf_tensor("rl", [1, 512], F32))
    xb = _st.enter_context(nc.sbuf_tensor("xb", [128, D], F32))
    onesf_ = _st.enter_context(nc.sbuf_tensor("onesf_", [1, 64], F32))
    psT1 = _st.enter_context(nc.psum_tensor("psT1", [128, 512], F32))
    psS4 = _st.enter_context(nc.psum_tensor("psS4", [128, 4 * 512], F32))
    psC = _st.enter_context(nc.psum_tensor("psC", [65, 512], F32))
    psR = _st.enter_context(nc.psum_tensor("psR", [64, 512], F32))
    dL = _st.enter_context(nc.semaphore("dL"))
    dH0 = _st.enter_context(nc.semaphore("dH0"))
    dH1 = _st.enter_context(nc.semaphore("dH1"))
    dB = _st.enter_context(nc.semaphore("dB"))
    dXb = _st.enter_context(nc.semaphore("dXb"))
    dXc = _st.enter_context(nc.semaphore("dXc"))
    dS = _st.enter_context(nc.semaphore("dS"))
    vs = _st.enter_context(nc.semaphore("vs"))
    ss = _st.enter_context(nc.semaphore("ss"))
    gs = _st.enter_context(nc.semaphore("gs"))
    ts_ = _st.enter_context(nc.semaphore("ts"))
    with _st:
        blk = _st.enter_context(nc.Block())
        p = Prog(nc)
        E = p.engines({"vector": vs, "scalar": ss, "gpsimd": gs})
        v, s, g, sy, t = (E["vector"], E["scalar"], E["gpsimd"], E["sync"],
                          E["tensor"])
        tn = [0]

        def tmm(*args, waits=(), inc=False, **kw):
            incs = []
            if inc:
                tn[0] += 1
                incs = [(ts_, 1)]
            t.emit_unchained("matmul", *args, waits=waits, incs=incs, **kw)
            return tn[0]

        def ttr(out, in_, ident, waits=()):
            tn[0] += 1
            t.emit_unchained("transpose", out, in_, ident, waits=waits,
                             incs=[(ts_, 1)])
            return tn[0]

        sy.emit_unchained("dma_start", sqks[:], QKS, incs=[(dL, 16)])
        sy.emit_unchained("dma_start",
                          scst[:].rearrange("p (c m) -> p c m", c=NLT),
                          CST.transpose([1, 0, 2]), incs=[(dL, 16)])
        sy.emit_unchained("dma_start",
                          ssnt[:].rearrange("p (c m) -> p c m", c=NLT),
                          SNT.transpose([1, 0, 2]), incs=[(dL, 16)])
        sy.emit_unchained("dma_start", skill[:], KILLC, incs=[(dL, 16)])
        sy.emit_unchained("dma_start", sidb[:], IDTB, incs=[(dL, 16)])
        sy.emit_unchained("dma_start", sidf[:], IDTF, incs=[(dL, 16)])
        g.emit_unchained("dma_start",
                         smb[:].rearrange("p (c m) -> p c m", c=8),
                         MBAND.transpose([1, 0, 2]), incs=[(dB, 16)])
        g.emit_unchained("dma_start",
                         swq[:].rearrange("p (c m) -> p c m", c=8),
                         WQKV.transpose([1, 0, 2]), incs=[(dB, 16)])
        g.emit_unchained("dma_start",
                         swo[:].rearrange("p (c m) -> p c m", c=4),
                         WO.transpose([1, 0, 2]), incs=[(dB, 16)])
        g.emit("memset", vbuf[:], 1.0)
        g.emit("memset", onesf_[:], 1.0)
        for h in range(8):
            g.emit("memset", KA[h][64:65, :], 1.0)
        mg0 = g.n

        nh = [0, 0]
        prev_tile_v = 0
        prev_tile_t = 0
        prev_tile_s = 0
        for lt in range(NLT):
            par = lt % 2
            dHp = dH0 if par == 0 else dH1
            wdma = [(ts_, prev_tile_t)] if lt >= 1 else []
            g.emit_unchained("dma_start",
                             h1t[:].rearrange("p (c m) -> p c m", c=8),
                             H1T[lt].transpose([1, 0, 2]), waits=wdma,
                             incs=[(dHp, 16)])
            nh[par] += 1
            w0 = [(dHp, 16 * nh[par]), (dB, 48)]
            if lt >= 1:
                w0.append((vs, prev_psq_free))
                w0.append((ss, prev_tile_s))
            for mch in range(3):
                for f in range(8):
                    tmm(psS4[:, mch * 512:(mch + 1) * 512],
                        h1t[:, f * 128:(f + 1) * 128],
                        swq[:, f * 1536 + mch * 512:f * 1536 + (mch + 1) * 512],
                        start=(f == 0), stop=(f == 7),
                        waits=(w0 if (mch == 0 and f == 0) else ()),
                        inc=(f == 7))
            mQ = tn[0]
            prev_tile_t = mQ
            s.wait(ts_, mQ)
            msq = s.emit("activation", rotb[:], psS4[:, 0:1024], AF.Square)
            v.wait(ss, msq)
            v.emit("tensor_reduce", nrm[:, 0:16],
                   rotb[:].rearrange("p (g x) -> p g x", g=16), axis=AX.X,
                   op=ALU.add)
            mnr = v.n
            s.wait(vs, mnr)
            msd = s.emit("activation", nrm[:, 16:32], nrm[:, 0:16], AF.Sqrt)
            v.wait(ss, msd)
            v.emit("tensor_scalar_max", nrm[:, 16:32], nrm[:, 16:32], 1e-12)
            v.emit("reciprocal", nrm[:, 32:48], nrm[:, 16:32])
            rnview = nrm[:, 32:48].rearrange("p (g o) -> p g o", g=16)\
                .broadcast_to([128, 16, 64])
            m_psq_free = v.emit("tensor_tensor", scr_qk[:], psS4[:, 0:1024],
                                rnview, op=ALU.mult)
            qksview = sqks[:].rearrange("p (q m) -> p q m", q=2)\
                .unsqueeze(2).broadcast_to([128, 2, 8, 64])
            v.emit("tensor_tensor", scr_qk[:],
                   scr_qk[:].rearrange("p (q h m) -> p q h m", q=2, h=8),
                   qksview, op=ALU.mult, waits=[(dL, 96)] if lt == 0 else ())
            msq2 = s.emit("activation", t1b[:], scr_qk[:], AF.Square,
                          waits=[(vs, v.n)])
            v.wait(ss, msq2)
            v.emit("tensor_reduce", nrm[:, 0:16],
                   t1b[:].rearrange("p (g x) -> p g x", g=16), axis=AX.X,
                   op=ALU.add)
            cstv = scst[:, lt * 128:(lt + 1) * 128]\
                .rearrange("p (q m) -> p q m", q=2).unsqueeze(2)\
                .broadcast_to([128, 2, 8, 64])
            sntv = ssnt[:, lt * 128:(lt + 1) * 128]\
                .rearrange("p (q m) -> p q m", q=2).unsqueeze(2)\
                .broadcast_to([128, 2, 8, 64])
            v.emit("tensor_tensor", t1b[:],
                   scr_qk[:].rearrange("p (q h m) -> p q h m", q=2, h=8),
                   cstv, op=ALU.mult)
            rview_out = rotb[:].rearrange("p (g m) -> p g m", g=16)[:, :, 0:32]
            rview_in = scr_qk[:].rearrange("p (g m) -> p g m", g=16)[:, :, 32:64]
            v.emit("tensor_scalar_mul", rview_out, rview_in, -1.0)
            rview_out2 = rotb[:].rearrange("p (g m) -> p g m", g=16)[:, :, 32:64]
            rview_in2 = scr_qk[:].rearrange("p (g m) -> p g m", g=16)[:, :, 0:32]
            v.emit("tensor_copy", rview_out2, rview_in2)
            v.emit("tensor_tensor", rotb[:],
                   rotb[:].rearrange("p (q h m) -> p q h m", q=2, h=8),
                   sntv, op=ALU.mult)
            mrot = v.emit("tensor_tensor", t1b[:], t1b[:], rotb[:], op=ALU.add)
            mbf = v.emit("tensor_copy", scr_qk[:].bitcast(BF16)[:, 0:1024],
                         t1b[:])
            trm = 0
            for f in range(8):
                wtr = [(vs, mbf)]
                if trm:
                    wtr.append((vs, trm))
                tdst = (psT1[:, 0:64] if f % 2 == 0 else
                        psS4[:, 1536:1536 + 64])
                mtr = ttr(tdst.bitcast(BF16),
                          scr_qk[:].bitcast(BF16)[:, f * 128:(f + 1) * 128],
                          sidb[:], waits=wtr)
                v.wait(ts_, mtr)
                h2_, h2b = 2 * (f % 4), 2 * (f % 4) + 1
                if f < 4:
                    if lt >= 4:
                        v.emit("tensor_copy",
                               QA[h2_][0:64, (lt - 4) * 128:(lt - 3) * 128],
                               tdst[0:64, 0:64].bitcast(BF16))
                        trm = v.emit("tensor_copy",
                                     QA[h2b][0:64, (lt - 4) * 128:(lt - 3) * 128],
                                     tdst[64:128, 0:64].bitcast(BF16))
                else:
                    v.emit("tensor_copy",
                           KA[h2_][0:64, lt * 128:(lt + 1) * 128],
                           tdst[0:64, 0:64].bitcast(BF16))
                    trm = v.emit("tensor_copy",
                                 KA[h2b][0:64, lt * 128:(lt + 1) * 128],
                                 tdst[64:128, 0:64].bitcast(BF16))
            for h in range(8):
                s.emit("copy",
                       vbuf[:, lt * 520 + h * 65:lt * 520 + h * 65 + 64]
                       if False else
                       vbuf[:, lt * 520 + h * 65:lt * 520 + h * 65 + 64],
                       psS4[:, 1024 + h * 64:1024 + (h + 1) * 64],
                       waits=[(gs, mg0), (ts_, mQ)] if h == 0 else ())
            prev_tile_s = s.n
            mtq = ttr(psT1[0:8, 0:128], nrm[:, 0:8], sidf[:],
                      waits=[(vs, v.n)])
            v.wait(ts_, mtq)
            v.emit("tensor_copy", qn2T[0:8, lt * 128:(lt + 1) * 128],
                   psT1[0:8, 0:128])
            mtk = ttr(psT1[0:8, 0:128], nrm[:, 8:16], sidf[:], waits=[(vs, v.n)])
            v.wait(ts_, mtk)
            v.emit("reduce_max", kmaxs[0:8, lt:lt + 1], psT1[0:8, 0:128],
                   axis=AX.X)
            prev_tile_v = v.n
            prev_psq_free = m_psq_free

        v.emit("tensor_copy", wm[:], kmaxs[0:8, 0:16])
        for o in range(1, 5):
            v.emit("tensor_tensor", wm[:], wm[:], kmaxs[0:8, o:o + 16],
                   op=ALU.max)
        v.emit("tensor_scalar_mul", wm[:], wm[:], -0.5)
        v.emit("tensor_scalar_mul", qn2T[0:8, 512:2560],
               qn2T[0:8, 512:2560], -0.5)
        wmv = wm[:].unsqueeze(2).broadcast_to([8, 16, 128])
        mbb = v.emit("tensor_tensor",
                     qn2T[0:8, 512:2560].rearrange("p (t m) -> p t m", t=16),
                     qn2T[0:8, 512:2560].rearrange("p (t m) -> p t m", t=16),
                     wmv, op=ALU.add)
        g.wait(vs, mbb)
        g.emit_unchained("dma_start", BSCR, qn2T[0:8, 512:2560],
                         incs=[(dXb, 16)])
        for h in range(8):
            g.emit_unchained("dma_start", QA[h][64:65, :], BSCR[h:h + 1, :],
                             waits=[(dXb, 16)] if h == 0 else [],
                             incs=[(dXb, 16)])

        last_ex = 0
        last_ctx = 0
        mo_prev = 0
        mcs_prev = 0
        for h in range(8):
            for qg in range(4):
                for g2 in range(2):
                    w0 = ([(dXb, 144), (ss, prev_tile_s)]
                          if (h == 0 and qg == 0 and g2 == 0) else [])
                    if last_ex:
                        w0.append((ss, last_ex))
                    for jj in range(4):
                        kt = 4 * qg + 4 * g2 + jj
                        mS = tmm(psS4[:, jj * 512:(jj + 1) * 512],
                                 KA[h][:, kt * 128:(kt + 1) * 128],
                                 QA[h][:, qg * 512:(qg + 1) * 512],
                                 start=True, stop=True,
                                 waits=(w0 if jj == 0 else ()), inc=(jj == 3))
                    v.wait(ts_, mS)
                    dmark = v.emit(
                        "tensor_tensor", psS4[:], psS4[:],
                        smb[:, g2 * 2048:(g2 + 1) * 2048], op=ALU.add,
                        waits=[(dB, 48)] if (h == 0 and qg == 0) else ())
                    if qg == 0 and g2 == 0:
                        for jj in range(4):
                            dmark = v.emit(
                                "tensor_scalar",
                                psS4[:, jj * 512:(jj + 1) * 512],
                                psS4[:, jj * 512:(jj + 1) * 512],
                                skill[:, jj:jj + 1], None, op0=ALU.add)
                    ws = [(vs, dmark)]
                    if last_ctx:
                        ws.append((ts_, last_ctx))
                    mex = s.emit("activation", ptb4[:], psS4[:], AF.Exp,
                                 scale=8.0, waits=ws)
                    last_ex = mex
                    for jj in range(4):
                        kt = 4 * qg + 4 * g2 + jj
                        wc = [(ss, mex)] if jj == 0 else []
                        if g2 == 0 and jj == 0:
                            if mo_prev:
                                wc.append((vs, mo_prev))
                            if mcs_prev:
                                wc.append((ss, mcs_prev))
                        last_ctx = tmm(
                            psC[:],
                            vbuf[:, kt * 520 + h * 65:kt * 520 + (h + 1) * 65],
                            ptb4[:, jj * 512:(jj + 1) * 512],
                            start=(g2 == 0 and jj == 0),
                            stop=(g2 == 1 and jj == 3),
                            waits=wc, inc=(jj == 3))
                v.wait(ts_, last_ctx)
                mrl = v.emit("reciprocal", rl[:], psC[64:65, :])
                for j2 in range(2):
                    tmm(psR[:, j2 * 256:(j2 + 1) * 256], onesf_[:],
                        rl[:, j2 * 256:(j2 + 1) * 256], start=True, stop=True,
                        waits=([(vs, mrl), (vs, mo_prev)] if j2 == 0 else ()),
                        inc=(j2 == 1))
                mcs_prev = s.emit("copy", ctxs[:], psC[0:64, :],
                                  waits=[(vs, mrl)])
                v.wait(ts_, tn[0])
                v.wait(ss, mcs_prev)
                co = (h // 2) * T + qg * 512
                ro = (h % 2) * 64
                mo_prev = v.emit("tensor_tensor",
                                 oT[ro:ro + 64, co:co + 512] if ro else
                                 oT[0:64, co:co + 512],
                                 ctxs[:], psR[:], op=ALU.mult)

        nxb = 0
        mdout = 0
        for tt in range(NT):
            sy.emit_unchained("dma_start", xb[:], XB[tt],
                              waits=([(vs, mdout), (dS, 16 * tt)] if tt
                                     else [(vs, mo_prev)]),
                              incs=[(dXc, 16)])
            nxb += 1
            w0 = [(vs, mo_prev), (dB, 48), (dXc, 16 * nxb)]
            if tt >= 1:
                w0.append((vs, mdout))
            for mch in range(2):
                for c in range(4):
                    tmm(psS4[:, mch * 512:(mch + 1) * 512],
                        oT[:, c * T + tt * 128:c * T + (tt + 1) * 128],
                        swo[:, c * D + mch * 512:c * D + (mch + 1) * 512],
                        start=(c == 0), stop=(c == 3),
                        waits=(w0 if (mch == 0 and c == 0) else ()),
                        inc=(c == 3))
            v.wait(ts_, tn[0])
            v.wait(dXc, 16 * nxb)
            v.emit("tensor_tensor", xb[:, 0:512], xb[:, 0:512],
                   psS4[:, 0:512], op=ALU.add)
            mdout = v.emit("tensor_tensor", xb[:, 512:1024],
                           xb[:, 512:1024], psS4[:, 512:1024], op=ALU.add)
            g.wait(vs, mdout)
            g.emit_unchained("dma_start", Y1[tt], xb[:], incs=[(dS, 16)])
            g.wait(dS, 16 * (tt + 1))
        sy.wait(dS, 16 * NT)
        p.build(blk)
    return nc


def f1_tables():
    DIM_HEAD, WINDOW = 64, 512
    half = np.arange(0, DIM_HEAD, 2, dtype=np.float64)
    inv_freq = 1.0 / 10000.0 ** (half / DIM_HEAD)
    base = (half + 0.4 * DIM_HEAD) / (1.4 * DIM_HEAD)

    def make(pos0):
        tpos = pos0 + np.arange(NLT * 128, dtype=np.float64)
        freqs = tpos[:, None] * inv_freq[None, :]
        freqs2 = np.concatenate([freqs, freqs], axis=1)
        power = (tpos - S_ // 2) / float(WINDOW // 2)
        scale = base[None, :] ** power[:, None]
        scale2 = np.concatenate([scale, scale], axis=1)
        cos, sin = np.cos(freqs2), np.sin(freqs2)
        cst = np.concatenate([cos * scale2, cos / scale2], axis=1)
        snt = np.concatenate([sin * scale2, sin / scale2], axis=1)
        return (cst.reshape(NLT, 128, 128).astype(np.float32),
                snt.reshape(NLT, 128, 128).astype(np.float32))
    return make


def f1_mband():
    m = np.zeros((8, 128, 512), np.float32)
    kk = np.arange(128)[:, None]
    qq = np.arange(128)[None, :]
    tri_hi = np.where(kk > qq, np.float32(-1e30), np.float32(0.0))
    tri_lo = np.where(kk < qq, np.float32(-1e30), np.float32(0.0))
    for d in range(8):
        for j in range(4):
            r = 4 + j - d
            if r < 0 or r > 4:
                m[d][:, j * 128:(j + 1) * 128] = -1e30
            elif r == 0:
                m[d][:, j * 128:(j + 1) * 128] = tri_hi
            elif r == 4:
                m[d][:, j * 128:(j + 1) * 128] = tri_lo
    return m


def prep_f1(ins, layer, x_full):
    wqkv = np.asarray(ins["L1_ffn_wqkv"][layer], np.float32)
    wo = np.asarray(ins["L1_ffn_wo"][layer], np.float32)
    qs = np.asarray(ins["L1_ffn_q_scale"][layer], np.float32)
    ks = np.asarray(ins["L1_ffn_k_scale"][layer], np.float32)
    QKS = np.tile(np.concatenate([qs, ks])[None, :], (128, 1)).astype(np.float32)
    mk = f1_tables()
    maps = []
    for c in range(NC):
        b, gq = c // 4, c % 4
        s0 = gq * T
        lo = s0 - 512
        h1 = np.zeros((NLT * 128, D), np.float32)
        xr = np.asarray(x_full[b])
        a0 = max(lo, 0)
        h1[a0 - lo:NLT * 128, 0:512] = xr[a0:s0 + T, 0:512]
        a1 = max(lo - 1, 0)
        h1[(a1 - (lo - 1)) + 0:NLT * 128, 512:1024] = xr[a1:s0 + T - 1, 512:1024]
        h1T = np.ascontiguousarray(h1.T.reshape(8, 128, NLT, 128)
                                   .transpose(2, 0, 1, 3))
        cst, snt = mk(lo)
        kill = np.zeros((128, 8), np.float32)
        if gq == 0:
            kill[:, 0:4] = -1e30
        maps.append({
            "H1T": h1T,
            "XB": tm_host(np.ascontiguousarray(xr[s0:s0 + T])),
            "WQKV": fm_host(wqkv),
            "WO": np.ascontiguousarray(wo.reshape(4, 128, D)),
            "QKS": QKS, "CST": cst, "SNT": snt,
            "MBAND": f1_mband(), "KILLC": kill,
            "IDTB": np.eye(128, dtype=BF),
            "IDTF": np.eye(128, dtype=np.float32),
        })
    return maps


def run_f1(ins, layer, x_full):
    nc = build_f1()
    maps = prep_f1(ins, layer, x_full)
    res = run_bass_kernel_spmd(nc, maps, core_ids=list(range(NC)))
    return [np.asarray(r["Y1"]).reshape(T, D) for r in res.results]



NJ = 22


def build_f2(final=False):
    nc = bass.Bass("TRN2", target_bir_lowering=False)
    Y = nc.dram_tensor("Y", [NT, 128, D], F32, kind="ExternalInput").ap()
    HS = nc.dram_tensor("HS", [NT, 128, D], F32, kind="ExternalInput").ap()
    LNG = nc.dram_tensor("LNG", [1, D], F32, kind="ExternalInput").ap()
    LNB = nc.dram_tensor("LNB", [1, D], F32, kind="ExternalInput").ap()
    W1A = nc.dram_tensor("W1A", [NJ, 8, 128, 128], F32, kind="ExternalInput").ap()
    W1G = nc.dram_tensor("W1G", [NJ, 8, 128, 128], F32, kind="ExternalInput").ap()
    B1A = nc.dram_tensor("B1A", [128, NJ], F32, kind="ExternalInput").ap()
    B1G = nc.dram_tensor("B1G", [128, NJ], F32, kind="ExternalInput").ap()
    W2H = nc.dram_tensor("W2H", [NJ, 8, 128, 128], F32, kind="ExternalInput").ap()
    B2C = nc.dram_tensor("B2C", [128, 8], F32, kind="ExternalInput").ap()
    IDTB = nc.dram_tensor("IDTB", [128, 128], BF16, kind="ExternalInput").ap()
    if final:
        X3 = nc.dram_tensor("X3", [NT, 128, D], F32, kind="ExternalInput").ap()
    OUT = nc.dram_tensor("OUT", [NT, 128, D], F32, kind="ExternalOutput").ap()

    import contextlib
    _st = contextlib.ExitStack()
    hst = _st.enter_context(nc.sbuf_tensor("hst", [128, 2 * D], F32))
    hnT = _st.enter_context(nc.sbuf_tensor("hnT", [128, 8 * T], BF16))
    h3T = _st.enter_context(nc.sbuf_tensor("h3T", [128, NJ * 1024], BF16))
    w2sb = _st.enter_context(nc.sbuf_tensor("w2sb", [128, NJ * 1024], BF16))
    w1b = _st.enter_context(nc.sbuf_tensor("w1b", [128, 2 * 2 * 1024], BF16))
    outT = _st.enter_context(nc.sbuf_tensor("outT", [128, 8 * 1024], BF16))
    slng = _st.enter_context(nc.sbuf_tensor("slng", [1, D], F32))
    slnb = _st.enter_context(nc.sbuf_tensor("slnb", [1, D], F32))
    lgB = _st.enter_context(nc.sbuf_tensor("lgB", [128, D], F32))
    lbB = _st.enter_context(nc.sbuf_tensor("lbB", [128, D], F32))
    b1a = _st.enter_context(nc.sbuf_tensor("b1a", [128, NJ], F32))
    b1g = _st.enter_context(nc.sbuf_tensor("b1g", [128, NJ], F32))
    b2c = _st.enter_context(nc.sbuf_tensor("b2c", [128, 8], F32))
    sidb = _st.enter_context(nc.sbuf_tensor("sidb", [128, 128], BF16))
    yb = _st.enter_context(nc.sbuf_tensor("yb", [128, D], F32))
    x3b = _st.enter_context(nc.sbuf_tensor("x3b", [128, D], F32))
    scr1 = _st.enter_context(nc.sbuf_tensor("scr1", [128, D], F32))
    scr2 = _st.enter_context(nc.sbuf_tensor("scr2", [128, D], F32))
    scr3 = _st.enter_context(nc.sbuf_tensor("scr3", [128, 512], F32))
    scr4 = _st.enter_context(nc.sbuf_tensor("scr4", [128, 512], F32))
    cols = _st.enter_context(nc.sbuf_tensor("cols", [128, 16], F32))
    epscol = _st.enter_context(nc.sbuf_tensor("epscol", [128, 1], F32))
    onesf = _st.enter_context(nc.sbuf_tensor("onesf", [1, 128], F32))
    psA = _st.enter_context(nc.psum_tensor("psA", [128, 2 * 512], F32))
    psB_ = _st.enter_context(nc.psum_tensor("psB_", [128, 2 * 512], F32))
    psD = _st.enter_context(nc.psum_tensor("psD", [128, 2 * 512], F32))
    psT = _st.enter_context(nc.psum_tensor("psT", [128, 2 * 512], F32))
    dL = _st.enter_context(nc.semaphore("dL"))
    dH0 = _st.enter_context(nc.semaphore("dH0"))
    dH1 = _st.enter_context(nc.semaphore("dH1"))
    dW0 = _st.enter_context(nc.semaphore("dW0"))
    dW1 = _st.enter_context(nc.semaphore("dW1"))
    dW2 = _st.enter_context(nc.semaphore("dW2"))
    dXc = _st.enter_context(nc.semaphore("dXc"))
    dS = _st.enter_context(nc.semaphore("dS"))
    vs = _st.enter_context(nc.semaphore("vs"))
    ss = _st.enter_context(nc.semaphore("ss"))
    gs = _st.enter_context(nc.semaphore("gs"))
    ts_ = _st.enter_context(nc.semaphore("ts"))
    with _st:
        blk = _st.enter_context(nc.Block())
        p = Prog(nc)
        E = p.engines({"vector": vs, "scalar": ss, "gpsimd": gs})
        v, s, g, sy, t = (E["vector"], E["scalar"], E["gpsimd"], E["sync"],
                          E["tensor"])
        tn = [0]

        def tmm(*args, waits=(), inc=False, **kw):
            incs = []
            if inc:
                tn[0] += 1
                incs = [(ts_, 1)]
            t.emit_unchained("matmul", *args, waits=waits, incs=incs, **kw)
            return tn[0]

        def ttr(out, in_, ident, waits=()):
            tn[0] += 1
            t.emit_unchained("transpose", out, in_, ident, waits=waits,
                             incs=[(ts_, 1)])
            return tn[0]

        def ln_tile(src_ap, dst_ap, extra_s_wait=None):
            v.emit("tensor_reduce", cols[:, 0:1], src_ap, axis=AX.X,
                   op=ALU.add)
            wts = [(vs, v.n)]
            if extra_s_wait is not None:
                wts.append(extra_s_wait)
            msq = s.emit("activation", scr2[:], src_ap, AF.Square,
                         accum_out=cols[:, 1:2], waits=wts)
            v.wait(ss, msq)
            v.emit("tensor_scalar_mul", cols[:, 2:3], cols[:, 0:1], 1.0 / D)
            v.emit("tensor_scalar_mul", cols[:, 3:4], cols[:, 1:2], 1.0 / D)
            v.emit("tensor_tensor", cols[:, 4:5], cols[:, 2:3], cols[:, 2:3],
                   op=ALU.mult)
            mv = v.emit("tensor_tensor", cols[:, 5:6], cols[:, 3:4],
                        cols[:, 4:5], op=ALU.subtract)
            s.wait(vs, mv)
            msd = s.emit("activation", cols[:, 6:7], cols[:, 5:6], AF.Sqrt,
                         bias=epscol[:])
            v.wait(ss, msd)
            v.emit("reciprocal", cols[:, 7:8], cols[:, 6:7])
            return v.emit("tensor_scalar", dst_ap, src_ap, cols[:, 2:3],
                          cols[:, 7:8], op0=ALU.subtract, op1=ALU.mult)

        sy.emit_unchained("dma_start", slng[:], LNG, incs=[(dL, 16)])
        sy.emit_unchained("dma_start", slnb[:], LNB, incs=[(dL, 16)])
        sy.emit_unchained("dma_start", b1a[:], B1A, incs=[(dL, 16)])
        sy.emit_unchained("dma_start", b1g[:], B1G, incs=[(dL, 16)])
        sy.emit_unchained("dma_start", b2c[:], B2C, incs=[(dL, 16)])
        sy.emit_unchained("dma_start", sidb[:], IDTB, incs=[(dL, 16)])
        for j in range(NJ):
            g.emit_unchained("dma_start",
                             w2sb[:, j * 1024:(j + 1) * 1024]
                             .rearrange("p (c m) -> p c m", c=8),
                             W2H[j].transpose([1, 0, 2]), incs=[(dW2, 16)])
        g.emit("memset", epscol[:], EPS)
        g.emit("memset", onesf[:], 1.0)
        mg0 = g.n
        for jj in range(4):
            tmm(psA[:, jj * 256:(jj + 1) * 256], onesf[:],
                slng[:, jj * 256:(jj + 1) * 256], start=True, stop=True,
                waits=([(dL, 96), (gs, mg0)] if jj == 0 else ()),
                inc=(jj == 3))
        for jj in range(4):
            tmm(psB_[:, jj * 256:(jj + 1) * 256], onesf[:],
                slnb[:, jj * 256:(jj + 1) * 256], start=True, stop=True,
                inc=(jj == 3))
        v.wait(ts_, tn[0])
        v.emit("tensor_copy", lgB[:], psA[:, 0:1024])
        mlg = v.emit("tensor_copy", lbB[:], psB_[:, 0:1024])

        nh = [0, 0]
        last_tr = 0
        tile_done = {}
        for tt in range(NT):
            par = tt % 2
            dHp = dH0 if par == 0 else dH1
            wdma = [(vs, last_tr)] if tt >= 2 else []
            sy.emit_unchained("dma_start", hst[:, par * D:(par + 1) * D],
                              HS[tt], waits=wdma, incs=[(dHp, 16)])
            nh[par] += 1
            v.wait(dHp, 16 * nh[par])
            if tt > 0:
                s.wait(ts_, last_trn)
            mln = ln_tile(hst[:, par * D:(par + 1) * D], scr1[:],
                          extra_s_wait=(dHp, 16 * nh[par]))
            v.emit("tensor_tensor", scr1[:], scr1[:], lgB[:], op=ALU.mult,
                   waits=[(vs, mlg)] if tt == 0 else [])
            wxf = [(ts_, prev_tr)] if tt > 0 and (prev_tr := trlast) else []
            mxf = v.emit("tensor_tensor", scr2[:].bitcast(BF16)[:, 0:D],
                         scr1[:], lbB[:], op=ALU.add, waits=wxf)
            trm = {}
            for f in range(8):
                half = f % 2
                wtr = [(vs, mxf)]
                if f >= 2:
                    wtr.append((vs, trm[f - 2]))
                mtr = ttr(psT[:, half * 512:half * 512 + 64].bitcast(BF16),
                          scr2[:].bitcast(BF16)[:, f * 128:(f + 1) * 128],
                          sidb[:], waits=wtr)
                v.wait(ts_, mtr)
                trm[f] = v.emit(
                    "tensor_copy",
                    hnT[:, f * T + tt * 128:f * T + (tt + 1) * 128],
                    psT[:, half * 512:half * 512 + 64].bitcast(BF16)[:, 0:128])
            trlast = tn[0]
            last_trn = tn[0]
            last_tr = trm[7]
            tile_done[tt] = trm[7]
        mhnT = v.n

        ngu = [0, 0]
        nxc = 0
        nout = 0
        eps_m = {}
        for half in range(2):
            toff = half * 1024
            for j in range(NJ):
                par = j % 2
                dWp = dW0 if par == 0 else dW1
                wdma = []
                if eps_m.get(("mm", half, j - 2)) or \
                   (half and eps_m.get(("mm", half - 1, NJ - 2 + j))):
                    wdma = [(ts_, eps_m.get(("mm", half, j - 2),
                            eps_m.get(("mm", half - 1, NJ - 2 + j), 0)))]
                g.emit_unchained("dma_start",
                                 w1b[:, par * 2048:par * 2048 + 1024]
                                 .rearrange("p (c m) -> p c m", c=8),
                                 W1A[j].transpose([1, 0, 2]),
                                 waits=wdma, incs=[(dWp, 16)])
                g.emit_unchained("dma_start",
                                 w1b[:, par * 2048 + 1024:par * 2048 + 2048]
                                 .rearrange("p (c m) -> p c m", c=8),
                                 W1G[j].transpose([1, 0, 2]),
                                 incs=[(dWp, 16)])
                ngu[par] += 2
                for tch in range(2):
                    ph = (2 * j + tch) % 2
                    tsl = slice(toff + tch * 512, toff + (tch + 1) * 512)
                    lt_need = (toff + (tch + 1) * 512 - 1) // 128
                    w0 = [(dWp, 16 * ngu[par]), (vs, tile_done[lt_need])]
                    k1_ = ("hm", half, j, tch - 2)
                    k2_ = ("hm", half, j - 1, tch)
                    if eps_m.get(k1_) or eps_m.get(k2_):
                        w0.append((vs, eps_m.get(k1_, eps_m.get(k2_, 0))))
                    for f in range(8):
                        tmm(psA[:, ph * 512:(ph + 1) * 512],
                            w1b[:, par * 2048 + f * 128:par * 2048 + (f + 1) * 128],
                            hnT[:, f * T + toff + tch * 512:
                                f * T + toff + (tch + 1) * 512],
                            start=(f == 0), stop=(f == 7),
                            waits=(w0 if f == 0 else ()), inc=(f == 7))
                    for f in range(8):
                        tmm(psB_[:, ph * 512:(ph + 1) * 512],
                            w1b[:, par * 2048 + 1024 + f * 128:
                                par * 2048 + 1024 + (f + 1) * 128],
                            hnT[:, f * T + toff + tch * 512:
                                f * T + toff + (tch + 1) * 512],
                            start=(f == 0), stop=(f == 7), inc=(f == 7))
                    mGU = tn[0]
                    eps_m[("mm", half, j)] = mGU
                    s.wait(ts_, mGU)
                    if eps_m.get(("hml",)):
                        s.wait(vs, eps_m[("hml",)])
                    msl = s.emit("activation", scr3[:],
                                 psB_[:, ph * 512:(ph + 1) * 512], AF.Gelu,
                                 bias=b1g[:, j:j + 1])
                    v.wait(ts_, mGU)
                    v.emit("tensor_scalar_add", scr4[:],
                           psA[:, ph * 512:(ph + 1) * 512], b1a[:, j:j + 1])
                    v.wait(ss, msl)
                    eps_m[("hm", half, j, tch)] = v.emit(
                        "tensor_tensor",
                        h3T[:, j * 1024 + tch * 512:j * 1024 + (tch + 1) * 512],
                        scr4[:], scr3[:], op=ALU.mult)
                    eps_m[("hml",)] = eps_m[("hm", half, j, tch)]
            mh3 = v.n
            for mch in range(8):
                for tch in range(2):
                    ph = (2 * mch + tch) % 2
                    w0 = [(vs, mh3), (dW2, 16 * NJ)]
                    kd = ("dn", half, mch, tch - 2)
                    kd2 = ("dn", half, mch - 1, tch)
                    if eps_m.get(kd) or eps_m.get(kd2):
                        w0.append((vs, eps_m.get(kd, eps_m.get(kd2, 0))))
                    for j in range(NJ):
                        tmm(psD[:, ph * 512:(ph + 1) * 512],
                            w2sb[:, j * 1024 + mch * 128:j * 1024 + (mch + 1) * 128],
                            h3T[:, j * 1024 + tch * 512:j * 1024 + (tch + 1) * 512],
                            start=(j == 0), stop=(j == NJ - 1),
                            waits=(w0 if j == 0 else ()), inc=(j == NJ - 1))
                    v.wait(ts_, tn[0])
                    mdn = v.emit(
                        "tensor_scalar_add",
                        outT[:, mch * 1024 + tch * 512:
                             mch * 1024 + (tch + 1) * 512],
                        psD[:, ph * 512:(ph + 1) * 512], b2c[:, mch:mch + 1])
                    eps_m[("dn", half, mch, tch)] = mdn
            mout = v.n
            for th in range(8):
                tt = half * 8 + th
                wy = [(vs, eps_m.get(("yb",), 0))] if eps_m.get(("yb",)) else []
                sy.emit_unchained("dma_start", yb[:], Y[tt],
                                  waits=wy + ([(dS, 16 * nout)] if nout else []),
                                  incs=[(dXc, 16)])
                nxc += 1
                if final:
                    sy.emit_unchained("dma_start", x3b[:], X3[tt],
                                      incs=[(dXc, 16)])
                    nxc += 1
                trl = 0
                for mch in range(8):
                    phh = mch % 2
                    wtr = [(vs, mout)]
                    if trl:
                        wtr.append((vs, trl))
                    mtr = ttr(psT[:, phh * 512:phh * 512 + 64].bitcast(BF16),
                              outT[:, mch * 1024 + th * 128:
                                   mch * 1024 + (th + 1) * 128],
                              sidb[:], waits=wtr)
                    v.wait(ts_, mtr)
                    trl = v.emit("tensor_copy",
                                 scr1[:, mch * 128:(mch + 1) * 128],
                                 psT[:, phh * 512:phh * 512 + 64]
                                 .bitcast(BF16)[:, 0:128])
                v.wait(dXc, 16 * nxc)
                mf2 = v.emit("tensor_tensor", scr1[:], scr1[:], yb[:],
                             op=ALU.add)
                eps_m[("yb",)] = mf2
                if final:
                    ln_tile(scr1[:], scr2[:])
                    mf2 = v.emit("tensor_tensor", scr2[:], scr2[:], x3b[:],
                                 op=ALU.add)
                    outsrc = scr2
                else:
                    outsrc = scr1
                g.wait(vs, mf2)
                g.emit_unchained("dma_start", OUT[tt], outsrc[:],
                                 incs=[(dS, 16)])
                nout += 1
                v.wait(dS, 16 * nout)
        sy.wait(dS, 16 * nout)
        p.build(blk)
    return nc


def prep_f2(ins, layer, y_full, x3_full=None):
    w1 = np.asarray(ins["L1_ffn_w1"][layer], np.float32)
    b1 = np.asarray(ins["L1_ffn_b1"][layer], np.float32)
    w2 = np.asarray(ins["L1_ffn_w2"][layer], np.float32)
    b2 = np.asarray(ins["L1_ffn_b2"][layer], np.float32)
    lg = np.asarray(ins["L1_ffn_ln_g"][layer], np.float32)
    lb = np.asarray(ins["L1_ffn_ln_b"][layer], np.float32)
    HID = 2730
    W1A = np.zeros((NJ, 8, 128, 128), np.float32)
    W1G = np.zeros((NJ, 8, 128, 128), np.float32)
    B1Ah = np.zeros((128, NJ), np.float32)
    B1Gh = np.zeros((128, NJ), np.float32)
    W2Hh = np.zeros((NJ, 8, 128, 128), np.float32)
    for j in range(NJ):
        n = min(128, HID - j * 128)
        W1A[j, :, :, 0:n] = w1[:, j * 128:j * 128 + n].reshape(8, 128, n)
        W1G[j, :, :, 0:n] = w1[:, HID + j * 128:HID + j * 128 + n]\
            .reshape(8, 128, n)
        B1Ah[0:n, j] = b1[j * 128:j * 128 + n]
        B1Gh[0:n, j] = b1[HID + j * 128:HID + j * 128 + n]
        W2Hh[j, 0:8, 0:n, :] = w2[j * 128:j * 128 + n].reshape(n, 8, 128)\
            .transpose(1, 0, 2)
    B2Ch = np.ascontiguousarray(b2.reshape(8, 128).T)
    com = {"LNG": lg.reshape(1, D), "LNB": lb.reshape(1, D),
           "W1A": W1A, "W1G": W1G, "B1A": B1Ah, "B1G": B1Gh,
           "W2H": W2Hh, "B2C": B2Ch, "IDTB": np.eye(128, dtype=BF)}
    maps = []
    for c in range(NC):
        b, gq = c // 4, c % 4
        s0 = gq * T
        yr = np.asarray(y_full[b])
        hs = np.empty((T, D), np.float32)
        hs[:, 0:512] = yr[s0:s0 + T, 0:512]
        if s0 == 0:
            hs[0, 512:] = 0.0
            hs[1:, 512:] = yr[0:T - 1, 512:]
        else:
            hs[:, 512:] = yr[s0 - 1:s0 + T - 1, 512:]
        mp = dict(com, Y=tm_host(np.ascontiguousarray(yr[s0:s0 + T])),
                  HS=tm_host(hs))
        if x3_full is not None:
            mp["X3"] = tm_host(np.ascontiguousarray(
                np.asarray(x3_full[b])[s0:s0 + T]))
        maps.append(mp)
    return maps


def run_f2(ins, layer, y_full, x3_full=None):
    nc = build_f2(final=x3_full is not None)
    maps = prep_f2(ins, layer, y_full, x3_full)
    res = run_bass_kernel_spmd(nc, maps, core_ids=list(range(NC)))
    return [np.asarray(r["OUT"]).reshape(T, D) for r in res.results]


def kernel(**ins):
    x0 = np.ascontiguousarray(np.asarray(ins["L0_attn_x"], np.float32))
    xsl = core_slices(x0)
    xn_slices = run_a0(xsl)
    xn_full = assemble(xn_slices, D)
    outT = run_a1(ins, xn_full)
    ow = np.asarray(ins["L0_attn_output_weight"], np.float32)
    OW = np.ascontiguousarray(ow.transpose(0, 2, 1).reshape(D, D))
    catT_slices = [np.ascontiguousarray(
        outT[c // 4][:, (c % 4) * T:(c % 4 + 1) * T]) for c in range(NC)]
    x1_slices, _ = run_proj_boundary(xsl, catT_slices, OW, emit_ln=False)
    x2_slices = run_b(ins, x1_slices)
    qkv_slices = run_c(ins, x2_slices)
    qkv_full = assemble(qkv_slices, 3072)
    catT2 = run_d(ins, qkv_full)
    WoT = np.ascontiguousarray(
        np.asarray(ins["L1_attn_W_o"], np.float32).T)
    cat2_slices = [np.ascontiguousarray(
        catT2[c // 4][:, (c % 4) * T:(c % 4 + 1) * T]) for c in range(NC)]
    x3_slices, t4_slices = run_proj_boundary(x2_slices, cat2_slices, WoT,
                                             emit_ln=True)
    x3_full = assemble(x3_slices, D)
    t4_full = assemble(t4_slices, D)
    y1 = assemble(run_f1(ins, 0, t4_full), D)
    xl1 = assemble(run_f2(ins, 0, y1), D)
    y2 = assemble(run_f1(ins, 1, xl1), D)
    x4 = assemble(run_f2(ins, 1, y2, x3_full), D)
    return x4.astype(np.float32)


# version 07
# speedup vs baseline: 1.0020x; 1.0020x over previous
import numpy as np
import ml_dtypes
import concourse.bass as bass
import concourse.mybir as mybir
from concourse.bass_utils import run_bass_kernel_spmd

F32 = mybir.dt.float32
BF16 = mybir.dt.bfloat16
AF = mybir.ActivationFunctionType
ALU = mybir.AluOpType
AX = mybir.AxisListType
BF = ml_dtypes.bfloat16

NC = 8
T = 2048
NT = 16
D = 1024
B_, S_ = 2, 8192
EPS = 1e-5



def tm_host(x):
    t, f = x.shape
    return np.ascontiguousarray(x.reshape(t // 128, 128, f))


def fm_host(w):
    k, m = w.shape
    assert k % 128 == 0
    return np.ascontiguousarray(w.reshape(k // 128, 128, m))


def core_slices(x):
    return [np.ascontiguousarray(x[c // 4, (c % 4) * T:(c % 4 + 1) * T])
            for c in range(NC)]


def assemble(slices, f):
    out = np.empty((2, S_, f), slices[0].dtype)
    for c in range(NC):
        out[c // 4, (c % 4) * T:(c % 4 + 1) * T] = slices[c]
    return out



ENGS = ("sync", "gpsimd", "scalar", "vector", "tensor")


class _Rec:
    __slots__ = ("name", "args", "kw", "waits", "incs")

    def __init__(self, name, args, kw, waits, incs):
        self.name, self.args, self.kw = name, args, kw
        self.waits, self.incs = waits, incs


class Eng:

    def __init__(self, prog, ename, sem, chain):
        self.prog = prog
        self.ename = ename
        self.sem = sem
        self.chain = chain
        self.n = 0

    def wait(self, sem, val):
        self.prog.ops[self.ename].append(
            _Rec("__wait__", (sem, val), {}, [], []))

    def emit(self, name, *args, waits=(), **kw):
        w = list(waits)
        incs = []
        if self.chain:
            if self.n > 0:
                w.append((self.sem, self.n))
            self.n += 1
            incs.append((self.sem, 1))
        self.prog.ops[self.ename].append(_Rec(name, args, kw, w, incs))
        return self.n

    def emit_unchained(self, name, *args, waits=(), incs=(), **kw):
        self.prog.ops[self.ename].append(
            _Rec(name, args, kw, list(waits), list(incs)))


class Prog:
    def __init__(self, nc):
        self.nc = nc
        self.ops = {e: [] for e in ENGS}

    def engines(self, chain_sems):
        res = {}
        for e in ENGS:
            sem = chain_sems.get(e)
            res[e] = Eng(self, e, sem, sem is not None)
        return res

    def build(self, blk):
        def player(lst):
            def f(eng):
                for r in lst:
                    if r.name == "__wait__":
                        eng.wait_ge(*r.args)
                        continue
                    for sem, val in r.waits:
                        eng.wait_ge(sem, val)
                    ins = getattr(eng, r.name)(*r.args, **r.kw)
                    for sem, val in r.incs:
                        ins.then_inc(sem, val)
            return f
        for e in ENGS:
            if self.ops[e]:
                getattr(blk, e)(player(self.ops[e]))


def ln_stats(v, s, src, sq, sums, sqs, mu, rs, epscol, ntiles, fdim,
             v_ready=(), s_ready=()):
    msq = s.emit("activation", sq, src, AF.Square, waits=s_ready)
    view = src.rearrange("p (t f) -> p t f", t=ntiles)
    v.emit("tensor_reduce", sums, view, axis=AX.X, op=ALU.add, waits=v_ready)
    v.wait(s.sem, msq)
    v.emit("tensor_reduce", sqs, sq.rearrange("p (t f) -> p t f", t=ntiles),
           axis=AX.X, op=ALU.add)
    v.emit("tensor_scalar_mul", mu, sums, 1.0 / fdim)
    v.emit("tensor_scalar_mul", sqs, sqs, 1.0 / fdim)
    v.emit("tensor_tensor", sums, mu, mu, op=ALU.mult)
    mvar = v.emit("tensor_tensor", sqs, sqs, sums, op=ALU.subtract)
    s.wait(v.sem, mvar)
    msd = s.emit("activation", rs, sqs, AF.Sqrt, bias=epscol)
    v.wait(s.sem, msd)
    return v.emit("reciprocal", rs, rs)



def build_a0():
    nc = bass.Bass("TRN2", target_bir_lowering=False)
    XS = nc.dram_tensor("XS", [NT, 128, D], F32, kind="ExternalInput").ap()
    XN = nc.dram_tensor("XN", [NT, 128, D], BF16, kind="ExternalOutput").ap()
    with (
        nc.sbuf_tensor("xs", [128, NT * D], F32) as xs,
        nc.sbuf_tensor("sq", [128, NT * D], F32) as sq,
        nc.sbuf_tensor("xn", [128, NT * D], BF16) as xn,
        nc.sbuf_tensor("sums", [128, NT], F32) as sums,
        nc.sbuf_tensor("sqs", [128, NT], F32) as sqs,
        nc.sbuf_tensor("mu", [128, NT], F32) as mu,
        nc.sbuf_tensor("rs", [128, NT], F32) as rs,
        nc.sbuf_tensor("epscol", [128, 1], F32) as epscol,
        nc.semaphore("dL") as dL,
        nc.semaphore("dS") as dS,
        nc.semaphore("vs") as vs,
        nc.semaphore("ss") as ss,
        nc.semaphore("gs") as gs,
        nc.Block() as blk,
    ):
        p = Prog(nc)
        E = p.engines({"vector": vs, "scalar": ss, "gpsimd": gs})
        v, s, g, sy = E["vector"], E["scalar"], E["gpsimd"], E["sync"]

        sy.emit_unchained("dma_start",
                          xs[:].rearrange("p (t f) -> p t f", t=NT),
                          XS.transpose([1, 0, 2]), incs=[(dL, 16)])
        g.emit("memset", epscol[:], EPS)
        s.wait(g.sem, 1)
        ln_stats(v, s, xs[:], sq[:], sums[:], sqs[:], mu[:], rs[:], epscol[:],
                 NT, D, v_ready=[(dL, 16)], s_ready=[(dL, 16)])
        for t in range(NT):
            v.emit("tensor_scalar", xn[:, t * D:(t + 1) * D],
                   xs[:, t * D:(t + 1) * D], mu[:, t:t + 1], rs[:, t:t + 1],
                   op0=ALU.subtract, op1=ALU.mult)
        g.wait(v.sem, v.n)
        g.emit_unchained("dma_start", XN.transpose([1, 0, 2]),
                         xn[:].rearrange("p (t f) -> p t f", t=NT),
                         incs=[(dS, 16)])
        sy.wait(dS, 16)
        p.build(blk)
    return nc


def run_a0(x_slices):
    nc = build_a0()
    in_maps = [{"XS": tm_host(xs)} for xs in x_slices]
    res = run_bass_kernel_spmd(nc, in_maps, core_ids=list(range(NC)))
    return [np.asarray(r["XN"]).reshape(T, D) for r in res.results]



TCA = 128
NCHA = S_ // TCA
CH5 = 512
NCH5 = S_ // CH5



def build_a1():
    nc = bass.Bass("TRN2", target_bir_lowering=False)
    XNT = nc.dram_tensor("XNT", [NCHA, 128, 8 * TCA], BF16, kind="ExternalInput").ap()
    WV = nc.dram_tensor("WV", [8, 128, 256], F32, kind="ExternalInput").ap()
    WK = nc.dram_tensor("WK", [8, 128, 68], F32, kind="ExternalInput").ap()
    IND = nc.dram_tensor("IND", [128, 256], F32, kind="ExternalInput").ap()
    CMASK = nc.dram_tensor("CMASK", [2, 128, 4], F32, kind="ExternalInput").ap()
    AB = nc.dram_tensor("AB", [128, 8], F32, kind="ExternalInput").ap()
    OUT = nc.dram_tensor("OUT", [2, 128, S_], BF16, kind="ExternalOutput").ap()

    import contextlib
    _st = contextlib.ExitStack()
    xnt = _st.enter_context(nc.sbuf_tensor("xnt", [128, 2 * 8 * TCA], BF16))
    swv = _st.enter_context(nc.sbuf_tensor("swv", [128, 8 * 256], BF16))
    swk = _st.enter_context(nc.sbuf_tensor("swk", [128, 8 * 68], BF16))
    sind = _st.enter_context(nc.sbuf_tensor("sind", [128, 2 * 128], F32))
    scm = _st.enter_context(nc.sbuf_tensor("scm", [128, 2 * 4], F32))
    sab = _st.enter_context(nc.sbuf_tensor("sab", [128, 8], F32))
    valsT = _st.enter_context(nc.sbuf_tensor("valsT", [128, S_], F32))
    rowsA = _st.enter_context(nc.sbuf_tensor("rowsA", [128, S_], F32))
    rowsB = _st.enter_context(nc.sbuf_tensor("rowsB", [128, S_], F32))
    rowsC = _st.enter_context(nc.sbuf_tensor("rowsC", [4, S_], F32))
    nB = _st.enter_context(nc.sbuf_tensor("nB", [128, S_], F32))
    cv = _st.enter_context(nc.sbuf_tensor("cv", [128, S_], F32))
    sinv1 = _st.enter_context(nc.sbuf_tensor("sinv1", [128, TCA], F32))
    sinv2 = _st.enter_context(nc.sbuf_tensor("sinv2", [128, 2 * TCA], F32))
    ones_col = _st.enter_context(nc.sbuf_tensor("ones_col", [128, 1], F32))
    psV = _st.enter_context(nc.psum_tensor("psV", [128, 2 * 512], F32))
    psC = _st.enter_context(nc.psum_tensor("psC", [68, 512], F32))
    psP1 = _st.enter_context(nc.psum_tensor("psP1", [4, 512], F32))
    psP2 = _st.enter_context(nc.psum_tensor("psP2", [4, 512], F32))
    psB = _st.enter_context(nc.psum_tensor("psB", [128, 2 * 512], F32))
    dW = _st.enter_context(nc.semaphore("dW"))
    dWg = _st.enter_context(nc.semaphore("dWg"))
    dX0 = _st.enter_context(nc.semaphore("dX0"))
    dX1 = _st.enter_context(nc.semaphore("dX1"))
    dS = _st.enter_context(nc.semaphore("dS"))
    vs = _st.enter_context(nc.semaphore("vs"))
    ss = _st.enter_context(nc.semaphore("ss"))
    gs = _st.enter_context(nc.semaphore("gs"))
    ts_ = _st.enter_context(nc.semaphore("ts"))
    blk = _st.enter_context(nc.Block())
    with _st:
        p = Prog(nc)
        E = p.engines({"vector": vs, "scalar": ss, "gpsimd": gs})
        v, s, g, sy, t = (E["vector"], E["scalar"], E["gpsimd"], E["sync"],
                          E["tensor"])
        tn = [0]

        def tmm(*args, waits=(), inc=False, **kw):
            incs = []
            if inc:
                tn[0] += 1
                incs = [(ts_, 1)]
            t.emit_unchained("matmul", *args, waits=waits, incs=incs, **kw)
            return tn[0]

        g.emit_unchained("dma_start", swv[:].rearrange("p (c m) -> p c m", c=8),
                         WV.transpose([1, 0, 2]), incs=[(dWg, 16)])
        g.emit_unchained("dma_start", swk[:].rearrange("p (c m) -> p c m", c=8),
                         WK.transpose([1, 0, 2]), incs=[(dWg, 16)])
        sy.emit_unchained("dma_start", sind[:], IND, incs=[(dW, 16)])
        sy.emit_unchained("dma_start", scm[:].rearrange("p (c m) -> p c m", c=2),
                          CMASK.transpose([1, 0, 2]), incs=[(dW, 16)])
        sy.emit_unchained("dma_start", sab[:], AB, incs=[(dW, 16)])
        g.emit("memset", ones_col[:], 1.0)
        g.emit("iota", nB[:], [[1, S_]], channel_multiplier=0,
               allow_small_or_imprecise_dtypes=True)
        g.emit("memset", rowsA[:], 0.0)
        v.wait(dW, 48)
        v.wait(gs, 2)
        acols = sab[:].rearrange("p (c m) -> p c m", c=4)[:, :, 0:1]
        mab = v.emit("tensor_scalar_mul", acols, acols, 1.0 / 8191.0)

        M = {}
        last_sin = 0
        last_fold = {0: 0, 1: 0}
        ndma = [0, 0]
        mrows = 0
        for pair in range(2):
            pair_v0 = v.n
            if pair == 1:
                v.wait(dS, 16)
            for ct in range(NCHA):
                buf = ct % 2
                xch = xnt[:, buf * 8 * TCA:(buf + 1) * 8 * TCA]
                w_dma = []
                if ct >= 2:
                    w_dma = [(ts_, M[("xch", pair, ct - 2)])]
                elif pair == 1:
                    w_dma = [(ts_, M[("xch", 0, NCHA - 2 + ct)])]
                par = ct % 2
                dXp = dX0 if par == 0 else dX1
                sy.emit_unchained("dma_start", xch, XNT[ct],
                                  waits=w_dma, incs=[(dXp, 16)])
                ndma[par] += 1
                wait_first = [(dXp, 16 * ndma[par])]
                if pair == 0 and ct == 0:
                    wait_first.append((dW, 48))
                    wait_first.append((dWg, 32))
                if pair == 1 and ct == 0:
                    wait_first.append((vs, pair_v0))
                    wait_first.append((dS, 16))
                pk = ("v", pair, ct - 2) if ct >= 2 else ("v", 0, NCHA - 2 + ct)
                if pk in M:
                    wait_first.append((ss, M[pk]))
                pv = psV[:, (ct % 2) * 512:(ct % 2) * 512 + TCA]
                for f in range(8):
                    tmm(pv, swv[:, f * 256 + pair * 128:f * 256 + pair * 128 + 128],
                        xch[:, f * TCA:(f + 1) * TCA],
                        start=(f == 0), stop=(f == 7),
                        waits=(wait_first if f == 0 else ()),
                        inc=(f == 7))
                mV = tn[0]
                s.wait(ts_, mV)
                M[("v", pair, ct)] = s.emit(
                    "copy", valsT[:, ct * TCA:(ct + 1) * TCA], pv)
                if pair == 0:
                    wc = [(ss, M[("c", ct - 1)])] if ct > 0 else []
                    for f in range(8):
                        tmm(psC[:, 0:TCA], swk[:, f * 68:(f + 1) * 68],
                            xch[:, f * TCA:(f + 1) * TCA],
                            start=(f == 0), stop=(f == 7),
                            waits=(wc if f == 0 else ()),
                            inc=(f == 7))
                    mC = tn[0]
                    s.wait(ts_, mC)
                    if ct == 0:
                        s.wait(gs, 3)
                    mk = 0
                    for (po, ro) in ((0, 0), (32, 32), (64, 64)):
                        mk = s.emit("copy",
                                    rowsA[ro:ro + 4, ct * TCA:(ct + 1) * TCA],
                                    psC[po:po + 4, 0:TCA])
                    M[("c", ct)] = mk
                    for pp in range(2):
                        for pr in range(2):
                            col = 4 * pr + 2 * pp
                            wv_ = [(vs, mab)] if (ct == 0 and pp == 0 and
                                                  pr == 0) else []
                            if last_sin:
                                wv_.append((ss, last_sin))
                            va = v.emit("tensor_scalar", sinv1[:],
                                        nB[:, ct * TCA:(ct + 1) * TCA],
                                        sab[:, col:col + 1],
                                        sab[:, col + 1:col + 2],
                                        op0=ALU.mult, op1=ALU.add, waits=wv_)
                            ws_ = [(vs, va)]
                            if last_fold[pr]:
                                ws_.append((ts_, last_fold[pr]))
                            last_sin = s.emit(
                                "activation", sinv2[:, pr * TCA:(pr + 1) * TCA],
                                sinv1[:], AF.Sin, waits=ws_)
                            psP = psP1 if pp == 0 else psP2
                            wt_ = [(ss, last_sin)]
                            if pr == 0 and M.get(("p", pp, ct - 1)):
                                wt_.append((vs, M[("p", pp, ct - 1)]))
                            last_fold[pr] = tmm(
                                psP[:, 0:TCA], scm[:, pr * 4:(pr + 1) * 4],
                                sinv2[:, pr * TCA:(pr + 1) * TCA],
                                start=(pr == 0), stop=(pr == 1),
                                waits=wt_, inc=(pr == 1))
                        mP = tn[0]
                        v.wait(ts_, mP)
                        if pp == 0:
                            v.wait(ss, M[("c", ct)])
                            M[("p", 0, ct)] = v.emit(
                                "tensor_tensor",
                                rowsA[0:4, ct * TCA:(ct + 1) * TCA],
                                rowsA[0:4, ct * TCA:(ct + 1) * TCA],
                                psP1[:, 0:TCA], op=ALU.add)
                        else:
                            M[("p", 1, ct)] = v.emit(
                                "tensor_copy",
                                rowsC[0:4, ct * TCA:(ct + 1) * TCA],
                                psP2[:, 0:TCA])
                M[("xch", pair, ct)] = tn[0]

            if pair == 0:
                v.emit("tensor_scalar_max", rowsA[:], rowsA[:], -20.0)
                mclip = v.emit("tensor_scalar_min", rowsA[:], rowsA[:], 20.0)
                s.wait(vs, mclip)
                mexp = s.emit("activation", rowsA[:], rowsA[:], AF.Exp)
                v.wait(ss, mexp)
                v.emit("tensor_scalar_max", rowsC[:], rowsC[:], -20.0)
                mclip2 = v.emit("tensor_scalar_min", rowsC[:], rowsC[:], 20.0)
                s.wait(vs, mclip2)
                mexp2 = s.emit("activation", rowsC[:], rowsC[:], AF.Exp)
                v.wait(ss, mexp2)
                ones4 = ones_col[0:4, 0:1].broadcast_to([4, S_])
                v.emit("tensor_tensor_scan", rowsB[32:36, :], ones4,
                       rowsA[0:4, :], 0.0, op0=ALU.mult, op1=ALU.add)
                v.emit("tensor_tensor", rowsB[0:4, :], rowsB[32:36, :],
                       rowsA[32:36, :], op=ALU.mult)
                v.emit("tensor_tensor", rowsB[64:68, :], rowsB[0:4, :],
                       rowsC[0:4, :], op=ALU.mult)
                v.emit("tensor_tensor", rowsB[64:68, :], rowsB[64:68, :],
                       rowsA[64:68, :], op=ALU.add)
                mrows = v.emit("reciprocal", rowsB[64:68, :], rowsB[64:68, :])

            pb_last_use = {0: 0, 1: 0}
            pbu = 0
            for ch in range(NCH5):
                half = pbu % 2
                pbu += 1
                pb = psB[:, half * 512:half * 512 + 512]
                wt_ = [(vs, mrows), (ss, M[("v", pair, NCHA - 1)])]
                if pb_last_use[half]:
                    wt_.append((vs, pb_last_use[half]))
                if pair == 1 and ch == 0:
                    wt_.append((vs, pair_v0))
                for h2 in range(2):
                    tmm(pb[:, h2 * 256:(h2 + 1) * 256],
                        sind[0:4, pair * 128:(pair + 1) * 128],
                        rowsA[0:4, ch * 512 + h2 * 256:ch * 512 + (h2 + 1) * 256],
                        start=True, stop=True,
                        waits=(wt_ if h2 == 0 else ()), inc=(h2 == 1))
                v.wait(ts_, tn[0])
                pb_last_use[half] = v.emit(
                    "tensor_tensor", cv[:, ch * 512:(ch + 1) * 512],
                    valsT[:, ch * 512:(ch + 1) * 512], pb, op=ALU.mult)
            ones128 = ones_col[:, 0:1].broadcast_to([128, S_])
            v.emit("tensor_tensor_scan", nB[:], ones128, cv[:], 0.0,
                   op0=ALU.mult, op1=ALU.add)
            outb = cv[:].bitcast(BF16)[:, 0:S_]
            for ch in range(NCH5):
                sl = slice(ch * 512, (ch + 1) * 512)
                for use, base, rows_src in (
                        (0, 32, rowsA[32:36, :]), (1, 0, rowsC[0:4, :]),
                        (2, 64, rowsA[64:68, :]), (3, 64, rowsB[64:68, :])):
                    half = pbu % 2
                    pbu += 1
                    pb = psB[:, half * 512:half * 512 + 512]
                    wt_ = []
                    if pb_last_use[half]:
                        wt_.append((vs, pb_last_use[half]))
                    for h2 in range(2):
                        tmm(pb[:, h2 * 256:(h2 + 1) * 256],
                            sind[base:base + 4, pair * 128:(pair + 1) * 128],
                            rows_src[:, ch * 512 + h2 * 256:
                                     ch * 512 + (h2 + 1) * 256],
                            start=True, stop=True,
                            waits=(wt_ if h2 == 0 else ()), inc=(h2 == 1))
                    v.wait(ts_, tn[0])
                    if use in (0, 1):
                        pb_last_use[half] = v.emit(
                            "tensor_tensor", nB[:, sl], nB[:, sl], pb,
                            op=ALU.mult)
                    elif use == 2:
                        pb_last_use[half] = v.emit(
                            "tensor_tensor", valsT[:, sl], valsT[:, sl], pb,
                            op=ALU.mult)
                        v.emit("tensor_tensor", nB[:, sl], nB[:, sl],
                               valsT[:, sl], op=ALU.add)
                    else:
                        pb_last_use[half] = v.emit(
                            "tensor_tensor", outb[:, sl], nB[:, sl], pb,
                            op=ALU.mult)
            g.wait(vs, v.n)
            g.emit_unchained("dma_start", OUT[pair], outb,
                             incs=[(dS, 16)])
        sy.wait(dS, 32)
        p.build(blk)
    return nc


def prep_a1(ins, xnT_b):
    vw = ins["L0_attn_value_weight"]
    k1, k2, k3 = (ins["L0_attn_k1"], ins["L0_attn_k2"], ins["L0_attn_k3"])
    a1, a2 = ins["L0_attn_a1"], ins["L0_attn_a2"]
    b1, b2 = ins["L0_attn_b1"], ins["L0_attn_b2"]
    cc = ins["L0_attn_c"]
    maps = []
    for c in range(NC):
        b, gq = c // 4, c % 4
        hs = slice(4 * gq, 4 * gq + 4)
        WVh = np.ascontiguousarray(
            vw[hs].transpose(1, 0, 2).reshape(D, 256))
        WKh = np.zeros((D, 68), np.float32)
        WKh[:, 0:4] = k1[hs].T
        WKh[:, 64:68] = k2[hs].T
        WKh[:, 32:36] = k3[hs].T
        INDh = np.zeros((128, 256), np.float32)
        for pr in range(2):
            for col in range(128):
                for base in (0, 32, 64):
                    INDh[base + 2 * pr + col // 64, pr * 128 + col] = 1.0
        CM = np.zeros((2, 128, 4), np.float32)
        for pr in range(2):
            for j in (2 * pr, 2 * pr + 1):
                CM[pr, (j - 2 * pr) * 64:(j - 2 * pr) * 64 + 64, j] =                     cc[4 * gq + j, :, 0]
        ABh = np.zeros((128, 8), np.float32)
        for pr in range(2):
            for pp, (aa, bb) in enumerate(((a1, b1), (a2, b2))):
                col = 4 * pr + 2 * pp
                ABh[0:64, col] = aa[4 * gq + 2 * pr, :, 0]
                ABh[64:128, col] = aa[4 * gq + 2 * pr + 1, :, 0]
                ABh[0:64, col + 1] = bb[4 * gq + 2 * pr, :, 0]
                ABh[64:128, col + 1] = bb[4 * gq + 2 * pr + 1, :, 0]
        maps.append({"XNT": xnT_b[b], "WV": fm_host(WVh), "WK": fm_host(WKh),
                     "IND": INDh, "CMASK": CM, "AB": ABh})
    return maps


def run_a1(ins, xn_full):
    xnT_b = []
    for b in range(2):
        xt = np.ascontiguousarray(xn_full[b].T)
        x8 = xt.reshape(8, 128, NCHA, TCA)
        xnT_b.append(np.ascontiguousarray(x8.transpose(2, 1, 0, 3)
                                          ).reshape(NCHA, 128, 8 * TCA))
    nc = build_a1()
    maps = prep_a1(ins, xnT_b)
    res = run_bass_kernel_spmd(nc, maps, core_ids=list(range(NC)))
    outT = np.zeros((2, 16 * 64, S_), BF)
    for c in range(NC):
        b, gq = c // 4, c % 4
        o = np.asarray(res.results[c]["OUT"])
        outT[b, gq * 256:gq * 256 + 128] = o[0]
        outT[b, gq * 256 + 128:gq * 256 + 256] = o[1]
    return outT



def build_proj_boundary(emit_ln=False):
    nc = bass.Bass("TRN2", target_bir_lowering=False)
    XS = nc.dram_tensor("XS", [NT, 128, D], F32, kind="ExternalInput").ap()
    CT = nc.dram_tensor("CT", [8, 128, T], BF16, kind="ExternalInput").ap()
    WO = nc.dram_tensor("WO", [8, 128, D], F32, kind="ExternalInput").ap()
    XO = nc.dram_tensor("XO", [NT, 128, D], F32, kind="ExternalOutput").ap()
    if emit_ln:
        XT = nc.dram_tensor("XT", [NT, 128, D], F32, kind="ExternalOutput").ap()
    import contextlib
    _st = contextlib.ExitStack()
    xs = _st.enter_context(nc.sbuf_tensor("xs", [128, NT * D], F32))
    ct_ = _st.enter_context(nc.sbuf_tensor("ct", [128, 8 * T], BF16))
    swo = _st.enter_context(nc.sbuf_tensor("swo", [128, 8 * D], BF16))
    xout = _st.enter_context(nc.sbuf_tensor("xout", [128, NT * D], F32))
    xtn = _st.enter_context(nc.sbuf_tensor("xtn", [128, D], F32))
    sq = _st.enter_context(nc.sbuf_tensor("sq", [128, D], F32))
    cols = _st.enter_context(nc.sbuf_tensor("cols", [128, 8], F32))
    st_sums = _st.enter_context(nc.sbuf_tensor("st_sums", [128, NT], F32))
    st_sqs = _st.enter_context(nc.sbuf_tensor("st_sqs", [128, NT], F32))
    st_mu = _st.enter_context(nc.sbuf_tensor("st_mu", [128, NT], F32))
    st_rs = _st.enter_context(nc.sbuf_tensor("st_rs", [128, NT], F32))
    epscol = _st.enter_context(nc.sbuf_tensor("epscol", [128, 1], F32))
    ts_ = _st.enter_context(nc.semaphore("ts"))
    psY = _st.enter_context(nc.psum_tensor("psY", [128, 2 * 1024], F32))
    dL = _st.enter_context(nc.semaphore("dL"))
    dWg = _st.enter_context(nc.semaphore("dWg"))
    dS = _st.enter_context(nc.semaphore("dS"))
    vs = _st.enter_context(nc.semaphore("vs"))
    ss = _st.enter_context(nc.semaphore("ss"))
    gs = _st.enter_context(nc.semaphore("gs"))
    blk = _st.enter_context(nc.Block())
    with _st:
        p = Prog(nc)
        E = p.engines({"vector": vs, "scalar": ss, "gpsimd": gs})
        v, s, g, sy, t = (E["vector"], E["scalar"], E["gpsimd"], E["sync"],
                          E["tensor"])
        tn = [0]

        def tmm(*args, waits=(), inc=False, **kw):
            incs = []
            if inc:
                tn[0] += 1
                incs = [(ts_, 1)]
            t.emit_unchained("matmul", *args, waits=waits, incs=incs, **kw)
            return tn[0]

        sy.emit_unchained("dma_start", xs[:].rearrange("p (t f) -> p t f", t=NT),
                          XS.transpose([1, 0, 2]), incs=[(dL, 16)])
        sy.emit_unchained("dma_start", ct_[:].rearrange("p (c m) -> p c m", c=8),
                          CT.transpose([1, 0, 2]), incs=[(dL, 16)])
        g.emit_unchained("dma_start", swo[:].rearrange("p (c m) -> p c m", c=8),
                         WO.transpose([1, 0, 2]), incs=[(dWg, 16)])
        g.emit("memset", epscol[:], EPS)

        apply_mark = {}
        for tt in range(NT):
            half = tt % 2
            ps = psY[:, half * 1024:(half + 1) * 1024]
            w0 = [(dL, 32), (dWg, 16)] if tt == 0 else []
            if apply_mark.get(tt - 2):
                w0 = [(vs, apply_mark[tt - 2])]
            for mch in range(2):
                for f in range(8):
                    tmm(ps[:, mch * 512:(mch + 1) * 512],
                        ct_[:, f * T + tt * 128:f * T + (tt + 1) * 128],
                        swo[:, f * D + mch * 512:f * D + (mch + 1) * 512],
                        start=(f == 0), stop=(f == 7),
                        waits=(w0 if (mch == 0 and f == 0) else ()),
                        inc=(f == 7 and mch == 1))
            mY = tn[0]
            v.wait(ts_, mY)
            mrd = v.emit("tensor_reduce", cols[:, 0:1], ps, axis=AX.X,
                         op=ALU.add)
            s.wait(vs, mrd)
            if tt == 0:
                s.wait(g.sem, 1)
            msq = s.emit("activation", sq[:], ps, AF.Square,
                         accum_out=cols[:, 1:2])
            v.wait(ss, msq)
            v.emit("tensor_scalar_mul", cols[:, 2:3], cols[:, 0:1], 1.0 / D)
            v.emit("tensor_scalar_mul", cols[:, 3:4], cols[:, 1:2], 1.0 / D)
            v.emit("tensor_tensor", cols[:, 4:5], cols[:, 2:3], cols[:, 2:3],
                   op=ALU.mult)
            mv = v.emit("tensor_tensor", cols[:, 5:6], cols[:, 3:4],
                        cols[:, 4:5], op=ALU.subtract)
            s.wait(vs, mv)
            msd = s.emit("activation", cols[:, 6:7], cols[:, 5:6], AF.Sqrt,
                         bias=epscol[:])
            v.wait(ss, msd)
            v.emit("reciprocal", cols[:, 7:8], cols[:, 6:7])
            sl = slice(tt * D, (tt + 1) * D)
            v.emit("tensor_scalar", xtn[:], ps, cols[:, 2:3], cols[:, 7:8],
                   op0=ALU.subtract, op1=ALU.mult, waits=[(dL, 32)])
            apply_mark[tt] = v.emit("tensor_tensor", xout[:, sl], xs[:, sl],
                                    xtn[:], op=ALU.add)
        g.wait(vs, v.n)
        g.emit_unchained("dma_start", XO.transpose([1, 0, 2]),
                         xout[:].rearrange("p (t f) -> p t f", t=NT),
                         incs=[(dS, 16)])
        nout = 16
        if emit_ln:
            for tt in range(NT):
                sl = slice(tt * D, (tt + 1) * D)
                v.emit("tensor_reduce", cols[:, 0:1], xout[:, sl], axis=AX.X,
                       op=ALU.add)
                msq = s.emit("activation", sq[:], xout[:, sl], AF.Square,
                             accum_out=cols[:, 1:2], waits=[(vs, v.n)])
                v.wait(ss, msq)
                v.emit("tensor_scalar_mul", cols[:, 2:3], cols[:, 0:1], 1.0 / D)
                v.emit("tensor_scalar_mul", cols[:, 3:4], cols[:, 1:2], 1.0 / D)
                v.emit("tensor_tensor", cols[:, 4:5], cols[:, 2:3],
                       cols[:, 2:3], op=ALU.mult)
                mv = v.emit("tensor_tensor", cols[:, 5:6], cols[:, 3:4],
                            cols[:, 4:5], op=ALU.subtract)
                s.wait(vs, mv)
                msd = s.emit("activation", cols[:, 6:7], cols[:, 5:6], AF.Sqrt,
                             bias=epscol[:])
                v.wait(ss, msd)
                v.emit("reciprocal", cols[:, 7:8], cols[:, 6:7])
                v.emit("tensor_scalar", xs[:, sl], xout[:, sl], cols[:, 2:3],
                       cols[:, 7:8], op0=ALU.subtract, op1=ALU.mult)
            g.wait(vs, v.n)
            g.emit_unchained("dma_start", XT.transpose([1, 0, 2]),
                             xs[:].rearrange("p (t f) -> p t f", t=NT),
                             incs=[(dS, 16)])
            nout = 32
        sy.wait(dS, nout)
        p.build(blk)
    return nc


def run_proj_boundary(x_slices, catT_slices, W, emit_ln=False):
    nc = build_proj_boundary(emit_ln)
    maps = []
    for c in range(NC):
        maps.append({"XS": tm_host(x_slices[c]),
                     "CT": np.ascontiguousarray(
                         catT_slices[c].reshape(8, 128, T)),
                     "WO": fm_host(W)})
    res = run_bass_kernel_spmd(nc, maps, core_ids=list(range(NC)))
    xo = [np.asarray(r["XO"]).reshape(T, D) for r in res.results]
    xt = ([np.asarray(r["XT"]).reshape(T, D) for r in res.results]
          if emit_ln else None)
    return xo, xt



NE = 9
NI = 16
TH = 1024


def build_b():
    nc = bass.Bass("TRN2", target_bir_lowering=False)
    X1 = nc.dram_tensor("X1", [NT, 128, D], F32, kind="ExternalInput").ap()
    LNW = nc.dram_tensor("LNW", [1, D], F32, kind="ExternalInput").ap()
    LNB = nc.dram_tensor("LNB", [1, D], F32, kind="ExternalInput").ap()
    RW = nc.dram_tensor("RW", [8, 128, 8], F32, kind="ExternalInput").ap()
    DESC = nc.dram_tensor("DESC", [128, 8], F32, kind="ExternalInput").ap()
    IDTF = nc.dram_tensor("IDTF", [128, 128], F32, kind="ExternalInput").ap()
    IDTB = nc.dram_tensor("IDTB", [128, 128], BF16, kind="ExternalInput").ap()
    INDE = nc.dram_tensor("INDE", [8, 8 * 128], F32, kind="ExternalInput").ap()
    GUH = nc.dram_tensor("GUH", [NE, 2, NI, 8, 128, 128], F32,
                         kind="ExternalInput").ap()
    DNH = nc.dram_tensor("DNH", [NE, NI, 8, 128, 128], F32,
                         kind="ExternalInput").ap()
    X2 = nc.dram_tensor("X2", [NT, 128, D], F32, kind="ExternalOutput").ap()
    WQ = nc.dram_tensor("WQ", [8, 128, 3072], F32, kind="ExternalInput").ap()
    QKV = nc.dram_tensor("QKV", [NT, 128, 3072], BF16, kind="ExternalOutput").ap()

    import contextlib
    _st = contextlib.ExitStack()
    x1t = _st.enter_context(nc.sbuf_tensor("x1t", [128, 2 * D], F32))
    xfT = _st.enter_context(nc.sbuf_tensor("xfT", [128, 8 * T], BF16))
    xeT = _st.enter_context(nc.sbuf_tensor("xeT", [128, 8 * TH], BF16))
    acc = _st.enter_context(nc.sbuf_tensor("acc", [128, 8 * TH], F32))
    hmT = _st.enter_context(nc.sbuf_tensor("hmT", [128, NI * TH], BF16))
    gub = _st.enter_context(nc.sbuf_tensor("gub", [128, 2 * 2 * 8 * 128], BF16))
    dnb = _st.enter_context(nc.sbuf_tensor("dnb", [128, NI * 8 * 128], BF16))
    srw = _st.enter_context(nc.sbuf_tensor("srw", [128, 8 * 8], BF16))
    slnw = _st.enter_context(nc.sbuf_tensor("slnw", [1, D], F32))
    slnb = _st.enter_context(nc.sbuf_tensor("slnb", [1, D], F32))
    lgB = _st.enter_context(nc.sbuf_tensor("lgB", [128, D], F32))
    lbB = _st.enter_context(nc.sbuf_tensor("lbB", [128, D], F32))
    sdesc = _st.enter_context(nc.sbuf_tensor("sdesc", [128, 8], F32))
    sidf = _st.enter_context(nc.sbuf_tensor("sidf", [128, 128], F32))
    sidb = _st.enter_context(nc.sbuf_tensor("sidb", [128, 128], BF16))
    sinde = _st.enter_context(nc.sbuf_tensor("sinde", [8, 8 * 128], F32))
    wrows = _st.enter_context(nc.sbuf_tensor("wrows", [8, T], F32))
    wb_sb = _st.enter_context(nc.sbuf_tensor("wb_sb", [128, TH], F32))
    scr1 = _st.enter_context(nc.sbuf_tensor("scr1", [128, D], F32))
    scr2 = _st.enter_context(nc.sbuf_tensor("scr2", [128, D], F32))
    scr4 = _st.enter_context(nc.sbuf_tensor("scr4", [128, D], F32))
    wcols = _st.enter_context(nc.sbuf_tensor("wcols", [128, NT * 8], F32))
    cols = _st.enter_context(nc.sbuf_tensor("cols", [128, 16], F32))
    epscol = _st.enter_context(nc.sbuf_tensor("epscol", [128, 1], F32))
    onesf = _st.enter_context(nc.sbuf_tensor("onesf", [1, 128], F32))
    psA = _st.enter_context(nc.psum_tensor("psA", [128, 2 * 512], F32))
    psB_ = _st.enter_context(nc.psum_tensor("psB_", [128, 2 * 512], F32))
    psD = _st.enter_context(nc.psum_tensor("psD", [128, 2 * 512], F32))
    psW = _st.enter_context(nc.psum_tensor("psW", [128, 2 * 512], F32))
    dL = _st.enter_context(nc.semaphore("dL"))
    dX0 = _st.enter_context(nc.semaphore("dX0"))
    dX1 = _st.enter_context(nc.semaphore("dX1"))
    dG0 = _st.enter_context(nc.semaphore("dG0"))
    dG1 = _st.enter_context(nc.semaphore("dG1"))
    dDn = _st.enter_context(nc.semaphore("dDn"))
    dRw = _st.enter_context(nc.semaphore("dRw"))
    dS = _st.enter_context(nc.semaphore("dS"))
    cWg = _st.enter_context(nc.semaphore("cWg"))
    cX0 = _st.enter_context(nc.semaphore("cX0"))
    cX1 = _st.enter_context(nc.semaphore("cX1"))
    cS0 = _st.enter_context(nc.semaphore("cS0"))
    cS1 = _st.enter_context(nc.semaphore("cS1"))
    vs = _st.enter_context(nc.semaphore("vs"))
    ss = _st.enter_context(nc.semaphore("ss"))
    gs = _st.enter_context(nc.semaphore("gs"))
    ts_ = _st.enter_context(nc.semaphore("ts"))
    with _st:
        blk = _st.enter_context(nc.Block())
        p = Prog(nc)
        E = p.engines({"vector": vs, "scalar": ss, "gpsimd": gs})
        v, s, g, sy, t = (E["vector"], E["scalar"], E["gpsimd"], E["sync"],
                          E["tensor"])
        tn = [0]

        def tmm(*args, waits=(), inc=False, **kw):
            incs = []
            if inc:
                tn[0] += 1
                incs = [(ts_, 1)]
            t.emit_unchained("matmul", *args, waits=waits, incs=incs, **kw)
            return tn[0]

        def ttr(out, in_, ident, waits=()):
            tn[0] += 1
            t.emit_unchained("transpose", out, in_, ident, waits=waits,
                             incs=[(ts_, 1)])
            return tn[0]

        def ln_tile(src_ap, dst_ap, extra_s_wait=None):
            v.emit("tensor_reduce", cols[:, 0:1], src_ap, axis=AX.X,
                   op=ALU.add)
            wts = [(vs, v.n)]
            if extra_s_wait is not None:
                wts.append(extra_s_wait)
            msq = s.emit("activation", scr2[:], src_ap, AF.Square,
                         accum_out=cols[:, 1:2], waits=wts)
            v.wait(ss, msq)
            v.emit("tensor_scalar_mul", cols[:, 2:3], cols[:, 0:1], 1.0 / D)
            v.emit("tensor_scalar_mul", cols[:, 3:4], cols[:, 1:2], 1.0 / D)
            v.emit("tensor_tensor", cols[:, 4:5], cols[:, 2:3], cols[:, 2:3],
                   op=ALU.mult)
            mv = v.emit("tensor_tensor", cols[:, 5:6], cols[:, 3:4],
                        cols[:, 4:5], op=ALU.subtract)
            s.wait(vs, mv)
            msd = s.emit("activation", cols[:, 6:7], cols[:, 5:6], AF.Sqrt,
                         bias=epscol[:])
            v.wait(ss, msd)
            v.emit("reciprocal", cols[:, 7:8], cols[:, 6:7])
            return v.emit("tensor_scalar", dst_ap, src_ap, cols[:, 2:3],
                          cols[:, 7:8], op0=ALU.subtract, op1=ALU.mult)

        sy.emit_unchained("dma_start", slnw[:], LNW, incs=[(dL, 16)])
        sy.emit_unchained("dma_start", slnb[:], LNB, incs=[(dL, 16)])
        sy.emit_unchained("dma_start", sdesc[:], DESC, incs=[(dL, 16)])
        sy.emit_unchained("dma_start", sidf[:], IDTF, incs=[(dL, 16)])
        sy.emit_unchained("dma_start", sidb[:], IDTB, incs=[(dL, 16)])
        sy.emit_unchained("dma_start", sinde[:], INDE, incs=[(dL, 16)])
        g.emit_unchained("dma_start", srw[:].rearrange("p (c m) -> p c m", c=8),
                         RW.transpose([1, 0, 2]), incs=[(dRw, 16)])
        g.emit("memset", epscol[:], EPS)
        g.emit("memset", onesf[:], 1.0)
        mg0 = g.n

        for j in range(4):
            tmm(psW[:, j * 256:(j + 1) * 256], onesf[:],
                slnw[:, j * 256:(j + 1) * 256], start=True, stop=True,
                waits=([(dL, 96), (gs, mg0)] if j == 0 else ()), inc=(j == 3))
        for j in range(4):
            tmm(psD[:, j * 256:(j + 1) * 256], onesf[:],
                slnb[:, j * 256:(j + 1) * 256], start=True, stop=True,
                inc=(j == 3))
        v.wait(ts_, tn[0])
        v.emit("tensor_copy", lgB[:], psW[:, 0:1024])
        mlg = v.emit("tensor_copy", lbB[:], psD[:, 0:1024])

        nx = [0, 0]
        xf_done = 0
        last_tr = 0
        for tt in range(NT):
            par = tt % 2
            dXp = dX0 if par == 0 else dX1
            wdma = [(vs, xf_done)] if tt >= 2 else []
            sy.emit_unchained("dma_start", x1t[:, par * D:(par + 1) * D], X1[tt],
                              waits=wdma, incs=[(dXp, 16)])
            nx[par] += 1
            v.wait(dXp, 16 * nx[par])
            x1sl = x1t[:, par * D:(par + 1) * D]
            ln_tile(x1sl, scr1[:], extra_s_wait=(dXp, 16 * nx[par]))
            ln_tile(scr1[:], scr2[:])
            v.emit("tensor_tensor", scr2[:], scr2[:], lgB[:], op=ALU.mult,
                   waits=[(vs, mlg)] if tt == 0 else ())
            wxf = [(ts_, last_tr)] if tt > 0 else []
            mxf = v.emit("tensor_tensor", scr4[:].bitcast(BF16)[:, 0:D],
                         scr2[:], lbB[:], op=ALU.add, waits=wxf)
            trm = {}
            for f in range(8):
                half = f % 2
                wtr = [(vs, mxf)]
                if f >= 2:
                    wtr.append((vs, trm[f - 2]))
                mtr = ttr(psW[:, half * 512:half * 512 + 64].bitcast(BF16),
                          scr4[:].bitcast(BF16)[:, f * 128:(f + 1) * 128],
                          sidb[:], waits=wtr)
                v.wait(ts_, mtr)
                trm[f] = v.emit(
                    "tensor_copy",
                    xfT[:, f * T + tt * 128:f * T + (tt + 1) * 128],
                    psW[:, half * 512:half * 512 + 64].bitcast(BF16)[:, 0:128])
            xf_done = trm[7]
            last_tr = tn[0]

        M_rt = {}
        for tt in range(NT):
            w0 = [(vs, xf_done), (dRw, 16)]
            if tt >= 1:
                w0.append((vs, M_rt[tt - 1]))
            for f in range(8):
                tmm(psD[:, 0:8],
                    xfT[:, f * T + tt * 128:f * T + (tt + 1) * 128],
                    srw[:, f * 8:(f + 1) * 8], start=(f == 0), stop=(f == 7),
                    waits=(w0 if f == 0 else ()), inc=(f == 7))
            mR = tn[0]
            v.wait(ts_, mR)
            v.emit("reduce_max", cols[:, 8:9], psD[:, 0:8], axis=AX.X)
            v.emit("tensor_scalar", wcols[:, tt * 8:(tt + 1) * 8], psD[:, 0:8],
                   cols[:, 8:9], None, op0=ALU.is_ge)
            v.emit("tensor_tensor", wcols[:, tt * 8:(tt + 1) * 8],
                   wcols[:, tt * 8:(tt + 1) * 8], sdesc[:], op=ALU.mult)
            v.emit("reduce_max", cols[:, 9:10], wcols[:, tt * 8:(tt + 1) * 8],
                   axis=AX.X)
            mse = s.emit("activation", cols[:, 10:11], cols[:, 8:9],
                         AF.Sigmoid, waits=[(vs, v.n)])
            v.wait(ss, mse)
            v.emit("tensor_scalar", wcols[:, tt * 8:(tt + 1) * 8],
                   wcols[:, tt * 8:(tt + 1) * 8], cols[:, 9:10], None,
                   op0=ALU.is_equal)
            M_rt[tt] = v.emit("tensor_scalar", wcols[:, tt * 8:(tt + 1) * 8],
                              wcols[:, tt * 8:(tt + 1) * 8], cols[:, 10:11],
                              None, op0=ALU.mult)
        wr_last = 0
        for tt in range(NT):
            wtr = [(vs, M_rt[tt])]
            if wr_last:
                wtr.append((vs, wr_last))
            mtr = ttr(psW[0:8, 0:128], wcols[:, tt * 8:(tt + 1) * 8], sidf[:],
                      waits=wtr)
            v.wait(ts_, mtr)
            wr_last = v.emit("tensor_copy", wrows[0:8, tt * 128:(tt + 1) * 128],
                             psW[0:8, 0:128])

        ngu = [0, 0]
        ndn = 0
        out_dmas = 0
        for half in range(2):
            hsl = slice(half * TH, (half + 1) * TH)
            macc0 = g.emit("memset", acc[:], 0.0,
                           waits=[(vs, v.n)] if half else [])
            m_dn_done = 0
            eps_m = {}
            for e in range(NE):
                if e < 8:
                    wmm0 = [(vs, wr_last)]
                    if m_dn_done:
                        wmm0.append((vs, m_dn_done))
                    for j in range(4):
                        tmm(psW[:, (j % 2) * 512:(j % 2) * 512 + 512][:, 0:256]
                            if False else
                            psW[:, j * 256:(j + 1) * 256],
                            sinde[:, e * 128:(e + 1) * 128],
                            wrows[0:8, half * TH + j * 256:
                                  half * TH + (j + 1) * 256],
                            start=True, stop=True,
                            waits=(wmm0 if j == 0 else ()), inc=(j == 3))
                    mW = tn[0]
                    v.wait(ts_, mW)
                    mwb = v.emit("tensor_copy", wb_sb[:], psW[:, 0:1024])
                    xe_mark = 0
                    for f in range(8):
                        xe_mark = v.emit(
                            "tensor_tensor", xeT[:, f * TH:(f + 1) * TH],
                            xfT[:, f * T + half * TH:f * T + (half + 1) * TH],
                            wb_sb[:], op=ALU.mult)
                    act = xeT
                    act_stride = TH
                    act_off = 0
                else:
                    xe_mark = m_dn_done
                    act = xfT
                    act_stride = T
                    act_off = half * TH
                for i_ in range(NI):
                    g.emit_unchained(
                        "dma_start",
                        dnb[:, i_ * 1024:(i_ + 1) * 1024]
                        .rearrange("p (c m) -> p c m", c=8),
                        DNH[e, i_].transpose([1, 0, 2]),
                        waits=([(ts_, dn_mm_last)]
                               if (e + half > 0 and i_ == 0) else []),
                        incs=[(dDn, 16)])
                ndn += NI
                for i in range(NI):
                    par = i % 2
                    dGp = dG0 if par == 0 else dG1
                    wdma = []
                    if (e, i) != (0, 0) or half:
                        wdma = [(ts_, eps_m.get(("mm", e, i - 2),
                                                eps_m.get(("mmprev",), 0)))]
                    g.emit_unchained(
                        "dma_start",
                        gub[:, par * 2048:par * 2048 + 1024]
                        .rearrange("p (c m) -> p c m", c=8),
                        GUH[e, 0, i].transpose([1, 0, 2]),
                        waits=wdma, incs=[(dGp, 16)])
                    g.emit_unchained(
                        "dma_start",
                        gub[:, par * 2048 + 1024:par * 2048 + 2048]
                        .rearrange("p (c m) -> p c m", c=8),
                        GUH[e, 1, i].transpose([1, 0, 2]),
                        incs=[(dGp, 16)])
                    ngu[par] += 2
                    for h in range(2):
                        ph = (2 * i + h) % 2
                        tsl = slice(h * 512, (h + 1) * 512)
                        w0 = [(dGp, 16 * ngu[par]), (vs, xe_mark)]
                        if eps_m.get(("hm", e, i, h - 2)) or \
                           eps_m.get(("hm", e, i - 1, h)):
                            w0.append((vs, eps_m.get(("hm", e, i, h - 2),
                                      eps_m.get(("hm", e, i - 1, h), 0))))
                        for f in range(8):
                            tmm(psA[:, ph * 512:(ph + 1) * 512],
                                gub[:, par * 2048 + f * 128:
                                    par * 2048 + (f + 1) * 128],
                                act[:, f * act_stride + act_off + h * 512:
                                    f * act_stride + act_off + (h + 1) * 512],
                                start=(f == 0), stop=(f == 7),
                                waits=(w0 if f == 0 else ()), inc=(f == 7))
                        for f in range(8):
                            tmm(psB_[:, ph * 512:(ph + 1) * 512],
                                gub[:, par * 2048 + 1024 + f * 128:
                                    par * 2048 + 1024 + (f + 1) * 128],
                                act[:, f * act_stride + act_off + h * 512:
                                    f * act_stride + act_off + (h + 1) * 512],
                                start=(f == 0), stop=(f == 7), inc=(f == 7))
                        mGU = tn[0]
                        eps_m[("mm", e, i)] = mGU
                        s.wait(ts_, mGU)
                        if eps_m.get(("hmlast",)):
                            s.wait(vs, eps_m[("hmlast",)])
                        msl = s.emit("activation", scr2[:, 0:512],
                                     psA[:, ph * 512:(ph + 1) * 512], AF.Silu)
                        v.wait(ss, msl)
                        eps_m[("hm", e, i, h)] = v.emit(
                            "tensor_tensor",
                            hmT[:, i * TH + h * 512:i * TH + (h + 1) * 512],
                            scr2[:, 0:512], psB_[:, ph * 512:(ph + 1) * 512],
                            op=ALU.mult)
                        eps_m[("hmlast",)] = eps_m[("hm", e, i, h)]
                eps_m[("mmprev",)] = eps_m[("mm", e, NI - 1)]
                hm_done = eps_m[("hm", e, NI - 1, 1)]
                dn_mm_last = 0
                for mch in range(8):
                    for h2 in range(2):
                        ph = (2 * mch + h2) % 2
                        w0 = [(dDn, 16 * ndn), (vs, hm_done)]
                        if eps_m.get(("dacc", e, mch, h2 - 2)) or \
                           eps_m.get(("dacc", e, mch - 1, h2)):
                            w0.append((vs, eps_m.get(("dacc", e, mch, h2 - 2),
                                      eps_m.get(("dacc", e, mch - 1, h2), 0))))
                        for i in range(NI):
                            dn_mm_last = tmm(
                                psD[:, ph * 512:(ph + 1) * 512],
                                dnb[:, i * 1024 + mch * 128:
                                    i * 1024 + (mch + 1) * 128],
                                hmT[:, i * TH + h2 * 512:i * TH + (h2 + 1) * 512],
                                start=(i == 0), stop=(i == NI - 1),
                                waits=(w0 if i == 0 else ()),
                                inc=(i == NI - 1))
                        v.wait(ts_, dn_mm_last)
                        eps_m[("dacc", e, mch, h2)] = v.emit(
                            "tensor_tensor",
                            acc[:, mch * TH + h2 * 512:mch * TH + (h2 + 1) * 512],
                            acc[:, mch * TH + h2 * 512:mch * TH + (h2 + 1) * 512],
                            psD[:, ph * 512:(ph + 1) * 512], op=ALU.add)
                m_dn_done = eps_m[("dacc", e, 7, 1)]

            for tt_h in range(8):
                tt = half * 8 + tt_h
                par = tt % 2
                dXp = dX0 if par == 0 else dX1
                sy.emit_unchained("dma_start", x1t[:, par * D:(par + 1) * D],
                                  X1[tt], waits=[(vs, v.n)],
                                  incs=[(dXp, 16)])
                nx[par] += 1
                trl = 0
                for mch in range(8):
                    ph = mch % 2
                    wtr = [(vs, m_dn_done)]
                    if trl:
                        wtr.append((vs, trl))
                    mtr = ttr(psA[:, ph * 512:ph * 512 + 128],
                              acc[:, mch * TH + tt_h * 128:
                                  mch * TH + (tt_h + 1) * 128],
                              sidf[:], waits=wtr if mch < 2 else [(vs, trl)])
                    v.wait(ts_, mtr)
                    trl = v.emit("tensor_copy", scr1[:, mch * 128:(mch + 1) * 128],
                                 psA[:, ph * 512:ph * 512 + 128])
                v.wait(dXp, 16 * nx[par])
                ln_tile(x1t[:, par * D:(par + 1) * D], scr4[:],
                        extra_s_wait=(dXp, 16 * nx[par]))
                v.emit("tensor_tensor", scr1[:], scr1[:], scr4[:], op=ALU.add)
                ln_tile(scr1[:], scr2[:])
                v.emit("tensor_tensor", scr2[:], scr2[:],
                       x1t[:, par * D:(par + 1) * D], op=ALU.add)
                g.wait(vs, v.n)
                g.emit_unchained("dma_start", X2[tt], scr2[:],
                                 incs=[(dS, 16)])
                out_dmas += 1
                v.wait(dS, 16 * out_dmas)

        vB = v.n

        def swq_of(f):
            if f < 5:
                return dnb[:, f * 3072:(f + 1) * 3072]
            if f == 5:
                return gub[:, 0:3072]
            return xeT[:, (f - 6) * 3072:(f - 5) * 3072]

        t3Tv = xeT[:, 6144:8192]
        qkvbv = xfT[:, 0:6144]

        def psQv(ph):
            return (psA[:, ph * 512:(ph + 1) * 512] if ph < 2 else
                    psB_[:, (ph - 2) * 512:(ph - 2 + 1) * 512])

        g.emit_unchained("dma_start",
                         dnb[:, 0:5 * 3072].rearrange("p (c m) -> p c m", c=5),
                         WQ[0:5].transpose([1, 0, 2]),
                         waits=[(dS, 16 * out_dmas)], incs=[(cWg, 16)])
        g.emit_unchained("dma_start", gub[:, 0:3072], WQ[5],
                         incs=[(cWg, 16)])
        g.emit_unchained("dma_start",
                         xeT[:, 0:2 * 3072].rearrange("p (c m) -> p c m", c=2),
                         WQ[6:8].transpose([1, 0, 2]), incs=[(cWg, 16)])

        cnx = [0, 0]
        cns = [0, 0]
        cevac = {}
        for tt in range(NT):
            par = tt % 2
            cXp = cX0 if par == 0 else cX1
            wdma = ([(vs, cevac[("xf", tt - 2)])] if tt >= 2 else
                    [(vs, vB), (dS, 16 * out_dmas)])
            sy.emit_unchained("dma_start", x1t[:, par * D:(par + 1) * D],
                              X2[tt], waits=wdma, incs=[(cXp, 16)])
            cnx[par] += 1
            v.wait(cXp, 16 * cnx[par])
            mxf = ln_tile(x1t[:, par * D:(par + 1) * D],
                          scr1[:].bitcast(BF16)[:, 0:D],
                          extra_s_wait=(cXp, 16 * cnx[par]))
            cevac[("xf", tt)] = mxf
            trm = {}
            for f in range(8):
                half = f % 2
                wtr = [(vs, mxf)]
                if f >= 2:
                    wtr = [(vs, trm[f - 2])]
                mtr = ttr(psD[:, half * 512:half * 512 + 64].bitcast(BF16),
                          scr1[:].bitcast(BF16)[:, f * 128:(f + 1) * 128],
                          sidb[:], waits=wtr)
                v.wait(ts_, mtr)
                trm[f] = v.emit(
                    "tensor_copy",
                    t3Tv[:, par * 1024 + f * 128:par * 1024 + (f + 1) * 128],
                    psD[:, half * 512:half * 512 + 64].bitcast(BF16)[:, 0:128])
            for mch in range(6):
                ph = mch % 4
                w0 = [(vs, trm[7]), (cWg, 48)]
                if cevac.get(("q", tt, mch - 4)) or \
                   cevac.get(("q", tt - 1, mch + 2)):
                    w0.append((ss, cevac.get(("q", tt, mch - 4),
                                             cevac.get(("q", tt - 1, mch + 2),
                                                       0))))
                for f in range(8):
                    tmm(psQv(ph),
                        t3Tv[:, par * 1024 + f * 128:par * 1024 + (f + 1) * 128],
                        swq_of(f)[:, mch * 512:(mch + 1) * 512],
                        start=(f == 0), stop=(f == 7),
                        waits=(w0 if f == 0 else ()), inc=(f == 7))
                s.wait(ts_, tn[0])
                if mch == 0 and tt >= 2:
                    s.wait(cS0 if par == 0 else cS1, 16 * cns[par])
                cevac[("q", tt, mch)] = s.emit(
                    "copy", qkvbv[:, par * 3072 + mch * 512:
                                  par * 3072 + (mch + 1) * 512],
                    psQv(ph))
            cSp = cS0 if par == 0 else cS1
            g.wait(ss, cevac[("q", tt, 5)])
            if tt >= 2:
                g.wait(cSp, 16 * cns[par])
            g.emit_unchained("dma_start", QKV[tt],
                             qkvbv[:, par * 3072:(par + 1) * 3072],
                             incs=[(cSp, 16)])
            cns[par] += 1
        sy.wait(dS, 16 * out_dmas)
        sy.wait(cS0, 16 * cns[0])
        sy.wait(cS1, 16 * cns[1])
        p.build(blk)
    return nc


def prep_b(ins):
    gw, uw, dw = (ins["L0_ffn_gate_w"], ins["L0_ffn_up_w"],
                  ins["L0_ffn_down_w"])
    GUH = np.zeros((NE, 2, NI, 8, 128, 128), np.float32)
    DNH = np.zeros((NE, NI, 8, 128, 128), np.float32)
    for e in range(8):
        GUH[e, 0] = gw[e].reshape(8, 128, NI, 128).transpose(2, 0, 1, 3)
        GUH[e, 1] = uw[e].reshape(8, 128, NI, 128).transpose(2, 0, 1, 3)
        DNH[e] = dw[e].reshape(NI, 128, 8, 128).transpose(0, 2, 1, 3)
    GUH[8, 0] = ins["L0_ffn_sh_gate"].reshape(8, 128, NI, 128).transpose(2, 0, 1, 3)
    GUH[8, 1] = ins["L0_ffn_sh_up"].reshape(8, 128, NI, 128).transpose(2, 0, 1, 3)
    DNH[8] = ins["L0_ffn_sh_down"].reshape(NI, 128, 8, 128).transpose(0, 2, 1, 3)
    DESC = np.tile((8.0 - np.arange(8, dtype=np.float32)), (128, 1))
    IDTF = np.eye(128, dtype=np.float32)
    IDTB = np.eye(128, dtype=BF)
    INDE = np.zeros((8, 8 * 128), np.float32)
    for e in range(8):
        INDE[e, e * 128:(e + 1) * 128] = 1.0
    wq = ins["L1_attn_W_qkv"]
    WQh = np.ascontiguousarray(
        wq.transpose(3, 0, 1, 2).reshape(D, 3072)).astype(np.float32)
    com = {"LNW": ins["L0_ffn_ln_w"].reshape(1, D).astype(np.float32),
           "LNB": ins["L0_ffn_ln_b"].reshape(1, D).astype(np.float32),
           "RW": fm_host(ins["L0_ffn_router_w"].astype(np.float32)),
           "DESC": DESC, "IDTF": IDTF, "IDTB": IDTB, "INDE": INDE,
           "GUH": GUH, "DNH": DNH, "WQ": fm_host(WQh)}
    return com


def run_b(ins, x1_slices):
    nc = build_b()
    com = prep_b(ins)
    maps = [dict(com, X1=tm_host(x1_slices[c])) for c in range(NC)]
    res = run_bass_kernel_spmd(nc, maps, core_ids=list(range(NC)))
    return ([np.asarray(r["X2"]).reshape(T, D) for r in res.results],
            [np.asarray(r["QKV"]).reshape(T, 3072) for r in res.results])



def build_c():
    nc = bass.Bass("TRN2", target_bir_lowering=False)
    X2 = nc.dram_tensor("X2", [NT, 128, D], F32, kind="ExternalInput").ap()
    WQ = nc.dram_tensor("WQ", [8, 128, 3072], F32, kind="ExternalInput").ap()
    IDTB = nc.dram_tensor("IDTB", [128, 128], BF16, kind="ExternalInput").ap()
    QKV = nc.dram_tensor("QKV", [NT, 128, 3072], BF16, kind="ExternalOutput").ap()

    import contextlib
    _st = contextlib.ExitStack()
    x2t = _st.enter_context(nc.sbuf_tensor("x2t", [128, 2 * D], F32))
    swq = _st.enter_context(nc.sbuf_tensor("swq", [128, 8 * 3072], BF16))
    sidb = _st.enter_context(nc.sbuf_tensor("sidb", [128, 128], BF16))
    t3T = _st.enter_context(nc.sbuf_tensor("t3T", [128, 2 * 8 * 128], BF16))
    qkvb = _st.enter_context(nc.sbuf_tensor("qkvb", [128, 2 * 3072], BF16))
    scr1 = _st.enter_context(nc.sbuf_tensor("scr1", [128, D], F32))
    scr2 = _st.enter_context(nc.sbuf_tensor("scr2", [128, D], F32))
    cols = _st.enter_context(nc.sbuf_tensor("cols", [128, 16], F32))
    epscol = _st.enter_context(nc.sbuf_tensor("epscol", [128, 1], F32))
    psQ = _st.enter_context(nc.psum_tensor("psQ", [128, 4 * 512], F32))
    psT = _st.enter_context(nc.psum_tensor("psT", [128, 2 * 512], F32))
    dL = _st.enter_context(nc.semaphore("dL"))
    dX0 = _st.enter_context(nc.semaphore("dX0"))
    dX1 = _st.enter_context(nc.semaphore("dX1"))
    dWg = _st.enter_context(nc.semaphore("dWg"))
    dS0 = _st.enter_context(nc.semaphore("dS0"))
    dS1 = _st.enter_context(nc.semaphore("dS1"))
    vs = _st.enter_context(nc.semaphore("vs"))
    ss = _st.enter_context(nc.semaphore("ss"))
    gs = _st.enter_context(nc.semaphore("gs"))
    ts_ = _st.enter_context(nc.semaphore("ts"))
    with _st:
        blk = _st.enter_context(nc.Block())
        p = Prog(nc)
        E = p.engines({"vector": vs, "scalar": ss, "gpsimd": gs})
        v, s, g, sy, t = (E["vector"], E["scalar"], E["gpsimd"], E["sync"],
                          E["tensor"])
        tn = [0]

        def tmm(*args, waits=(), inc=False, **kw):
            incs = []
            if inc:
                tn[0] += 1
                incs = [(ts_, 1)]
            t.emit_unchained("matmul", *args, waits=waits, incs=incs, **kw)
            return tn[0]

        def ttr(out, in_, ident, waits=()):
            tn[0] += 1
            t.emit_unchained("transpose", out, in_, ident, waits=waits,
                             incs=[(ts_, 1)])
            return tn[0]

        def ln_tile(src_ap, dst_ap, extra_s_wait=None):
            v.emit("tensor_reduce", cols[:, 0:1], src_ap, axis=AX.X,
                   op=ALU.add)
            wts = [(vs, v.n)]
            if extra_s_wait is not None:
                wts.append(extra_s_wait)
            msq = s.emit("activation", scr2[:], src_ap, AF.Square,
                         accum_out=cols[:, 1:2], waits=wts)
            v.wait(ss, msq)
            v.emit("tensor_scalar_mul", cols[:, 2:3], cols[:, 0:1], 1.0 / D)
            v.emit("tensor_scalar_mul", cols[:, 3:4], cols[:, 1:2], 1.0 / D)
            v.emit("tensor_tensor", cols[:, 4:5], cols[:, 2:3], cols[:, 2:3],
                   op=ALU.mult)
            mv = v.emit("tensor_tensor", cols[:, 5:6], cols[:, 3:4],
                        cols[:, 4:5], op=ALU.subtract)
            s.wait(vs, mv)
            msd = s.emit("activation", cols[:, 6:7], cols[:, 5:6], AF.Sqrt,
                         bias=epscol[:])
            v.wait(ss, msd)
            v.emit("reciprocal", cols[:, 7:8], cols[:, 6:7])
            return v.emit("tensor_scalar", dst_ap, src_ap, cols[:, 2:3],
                          cols[:, 7:8], op0=ALU.subtract, op1=ALU.mult)

        sy.emit_unchained("dma_start", sidb[:], IDTB, incs=[(dL, 16)])
        g.emit_unchained("dma_start", swq[:].rearrange("p (c m) -> p c m", c=8),
                         WQ.transpose([1, 0, 2]), incs=[(dWg, 16)])
        g.emit("memset", epscol[:], EPS)
        mg0 = g.n

        nx = [0, 0]
        ns = [0, 0]
        evac = {}
        for tt in range(NT):
            par = tt % 2
            dXp = dX0 if par == 0 else dX1
            wdma = [(vs, evac.get(("xf", tt - 2), 0))] if tt >= 2 else []
            sy.emit_unchained("dma_start", x2t[:, par * D:(par + 1) * D],
                              X2[tt], waits=wdma, incs=[(dXp, 16)])
            nx[par] += 1
            v.wait(dXp, 16 * nx[par])
            v.wait(gs, mg0)
            mxf = ln_tile(x2t[:, par * D:(par + 1) * D],
                          scr1[:].bitcast(BF16)[:, 0:D],
                          extra_s_wait=(dXp, 16 * nx[par]))
            evac[("xf", tt)] = mxf
            trm = {}
            for f in range(8):
                half = f % 2
                wtr = [(vs, mxf), (dL, 16)]
                if f >= 2:
                    wtr = [(vs, trm[f - 2])]
                mtr = ttr(psT[:, half * 512:half * 512 + 64].bitcast(BF16),
                          scr1[:].bitcast(BF16)[:, f * 128:(f + 1) * 128],
                          sidb[:], waits=wtr)
                v.wait(ts_, mtr)
                trm[f] = v.emit(
                    "tensor_copy",
                    t3T[:, par * 1024 + f * 128:par * 1024 + (f + 1) * 128],
                    psT[:, half * 512:half * 512 + 64].bitcast(BF16)[:, 0:128])
            for mch in range(6):
                ph = mch % 4
                w0 = [(vs, trm[7]), (dWg, 16)]
                if evac.get(("q", tt, mch - 4)) or evac.get(("q", tt - 1, mch + 2)):
                    w0.append((ss, evac.get(("q", tt, mch - 4),
                                            evac.get(("q", tt - 1, mch + 2), 0))))
                for f in range(8):
                    tmm(psQ[:, ph * 512:(ph + 1) * 512],
                        t3T[:, par * 1024 + f * 128:par * 1024 + (f + 1) * 128],
                        swq[:, f * 3072 + mch * 512:f * 3072 + (mch + 1) * 512],
                        start=(f == 0), stop=(f == 7),
                        waits=(w0 if f == 0 else ()), inc=(f == 7))
                s.wait(ts_, tn[0])
                if mch == 0 and tt >= 2:
                    s.wait(dS0 if par == 0 else dS1, 16 * ns[par])
                evac[("q", tt, mch)] = s.emit(
                    "copy", qkvb[:, par * 3072 + mch * 512:
                                 par * 3072 + (mch + 1) * 512],
                    psQ[:, ph * 512:(ph + 1) * 512])
            dSp = dS0 if par == 0 else dS1
            g.wait(ss, evac[("q", tt, 5)])
            if tt >= 2:
                g.wait(dSp, 16 * ns[par])
            g.emit_unchained("dma_start", QKV[tt],
                             qkvb[:, par * 3072:(par + 1) * 3072],
                             incs=[(dSp, 16)])
            ns[par] += 1
        sy.wait(dS0, 16 * ns[0])
        sy.wait(dS1, 16 * ns[1])
        p.build(blk)
    return nc


def run_c(ins, x2_slices):
    wq = ins["L1_attn_W_qkv"]
    WQh = np.ascontiguousarray(
        wq.transpose(3, 0, 1, 2).reshape(D, 3072)).astype(np.float32)
    nc = build_c()
    maps = [{"X2": tm_host(x2_slices[c]), "WQ": fm_host(WQh),
             "IDTB": np.eye(128, dtype=BF)} for c in range(NC)]
    res = run_bass_kernel_spmd(nc, maps, core_ids=list(range(NC)))
    return [np.asarray(r["QKV"]).reshape(T, 3072) for r in res.results]



def build_d():
    nc = bass.Bass("TRN2", target_bir_lowering=False)
    QT = nc.dram_tensor("QT", [4, 65, S_], BF16, kind="ExternalInput").ap()
    KT = nc.dram_tensor("KT", [4, 65, S_], BF16, kind="ExternalInput").ap()
    VA = nc.dram_tensor("VA", [4, S_, 65], BF16, kind="ExternalInput").ap()
    STRI = nc.dram_tensor("STRI", [128, 2048], F32, kind="ExternalInput").ap()
    OUT = nc.dram_tensor("OUT", [4, 64, S_], BF16, kind="ExternalOutput").ap()

    import contextlib
    _st = contextlib.ExitStack()
    qta = _st.enter_context(nc.sbuf_tensor("qta", [65, 2 * S_], BF16))
    kta = _st.enter_context(nc.sbuf_tensor("kta", [65, 2 * S_], BF16))
    va = _st.enter_context(nc.sbuf_tensor("va", [128, 2 * 64 * 65], BF16))
    sq_ = _st.enter_context(nc.sbuf_tensor("sq_", [64, S_], BF16))
    outb = _st.enter_context(nc.sbuf_tensor("outb", [64, S_], BF16))
    ptb4 = _st.enter_context(nc.sbuf_tensor("ptb4", [128, 4 * 512], BF16))
    ctxs = _st.enter_context(nc.sbuf_tensor("ctxs", [64, 512], F32))
    sstri = _st.enter_context(nc.sbuf_tensor("sstri", [128, 2048], F32))
    rl = _st.enter_context(nc.sbuf_tensor("rl", [1, 512], F32))
    kmax = _st.enter_context(nc.sbuf_tensor("kmax", [1, 2], F32))
    onesb = _st.enter_context(nc.sbuf_tensor("onesb", [64, 1], BF16))
    onesf = _st.enter_context(nc.sbuf_tensor("onesf", [1, 64], F32))
    psS4 = _st.enter_context(nc.psum_tensor("psS4", [128, 4 * 512], F32))
    psC = _st.enter_context(nc.psum_tensor("psC", [65, 512], F32))
    psN = _st.enter_context(nc.psum_tensor("psN", [1, 512], F32))
    psR = _st.enter_context(nc.psum_tensor("psR", [64, 512], F32))
    dL = _st.enter_context(nc.semaphore("dL"))
    dP0 = _st.enter_context(nc.semaphore("dP0"))
    dP1 = _st.enter_context(nc.semaphore("dP1"))
    dS = _st.enter_context(nc.semaphore("dS"))
    vs = _st.enter_context(nc.semaphore("vs"))
    ss = _st.enter_context(nc.semaphore("ss"))
    gs = _st.enter_context(nc.semaphore("gs"))
    ts_ = _st.enter_context(nc.semaphore("ts"))
    with _st:
        blk = _st.enter_context(nc.Block())
        p = Prog(nc)
        E = p.engines({"vector": vs, "scalar": ss, "gpsimd": gs})
        v, s, g, sy, t = (E["vector"], E["scalar"], E["gpsimd"], E["sync"],
                          E["tensor"])
        tn = [0]

        def tmm(*args, waits=(), inc=False, **kw):
            incs = []
            if inc:
                tn[0] += 1
                incs = [(ts_, 1)]
            t.emit_unchained("matmul", *args, waits=waits, incs=incs, **kw)
            return tn[0]

        sy.emit_unchained("dma_start", sstri[:], STRI, incs=[(dL, 16)])
        g.emit("memset", onesb[:], 1.0)
        g.emit("memset", onesf[:], 1.0)
        mg0 = g.n

        def load_pair(pair, waits):
            pb = pair % 2
            dPp = dP0 if pb == 0 else dP1
            sy.emit_unchained("dma_start", qta[:, pb * S_:(pb + 1) * S_],
                              QT[pair], waits=waits, incs=[(dPp, 16)])
            sy.emit_unchained("dma_start", kta[:, pb * S_:(pb + 1) * S_],
                              KT[pair], incs=[(dPp, 16)])
            sy.emit_unchained(
                "dma_start",
                va[:, pb * 64 * 65:(pb + 1) * 64 * 65]
                .rearrange("p (k c) -> p k c", k=64),
                VA[pair].rearrange("(k p) c -> p k c", p=128),
                incs=[(dPp, 16)])

        load_pair(0, [])
        load_pair(1, [])
        nload = [1, 1]
        last_ex = {0: 0, 1: 0}
        last_ctx = {0: 0, 1: 0}
        prev_sq = 0
        for pair in range(4):
            pb = pair % 2
            dPp = dP0 if pb == 0 else dP1
            qsl = qta[:, pb * S_:(pb + 1) * S_]
            ksl = kta[:, pb * S_:(pb + 1) * S_]
            vsl = va[:, pb * 64 * 65:(pb + 1) * 64 * 65]
            nwait = 16 * 3 * nload[pb]
            msqk = s.emit("activation", sq_[:], ksl[0:64, :], AF.Square,
                          waits=[(dPp, nwait), (vs, prev_sq)])
            knm = 0
            for ch in range(16):
                mq = tmm(psN[:], onesb[:], sq_[:, ch * 512:(ch + 1) * 512],
                         start=True, stop=True,
                         waits=([(ss, msqk), (gs, mg0)] if ch == 0 else
                                [(vs, knm)]),
                         inc=True)
                v.wait(ts_, mq)
                if ch == 0:
                    v.emit("reduce_max", kmax[:, 0:1], psN[:], axis=AX.X)
                else:
                    v.emit("reduce_max", kmax[:, 1:2], psN[:], axis=AX.X)
                    v.emit("tensor_tensor", kmax[:, 0:1], kmax[:, 0:1],
                           kmax[:, 1:2], op=ALU.max)
                knm = v.n
            mkh = v.emit("tensor_scalar_mul", kmax[:, 1:2], kmax[:, 0:1], -0.5)
            msqq = s.emit("activation", sq_[:], qsl[0:64, :], AF.Square,
                          waits=[(vs, knm)])
            qnm = 0
            for ch in range(16):
                mq = tmm(psN[:], onesb[:], sq_[:, ch * 512:(ch + 1) * 512],
                         start=True, stop=True,
                         waits=([(ss, msqq), (vs, mkh)] if ch == 0 else
                                [(vs, qnm)]),
                         inc=True)
                v.wait(ts_, mq)
                v.emit("tensor_scalar", qsl[64:65, ch * 512:(ch + 1) * 512],
                       psN[:], -0.5, kmax[:, 1:2], op0=ALU.mult, op1=ALU.add)
                qnm = v.n
            prev_sq = qnm
            maug = qnm

            mo_prev = 0
            mcs_prev = 0
            if pair > 0:
                v.wait(dS, 16 * pair)
            for qg in range(16):
                ngrp = qg + 1
                for g2 in range(ngrp):
                    diag = (g2 == ngrp - 1)
                    w0 = []
                    if qg == 0 and g2 == 0:
                        w0.append((vs, maug))
                    if last_ex[0]:
                        w0.append((ss, last_ex[0]))
                    for jj in range(4):
                        kt = 4 * g2 + jj
                        mS = tmm(psS4[:, jj * 512:(jj + 1) * 512],
                                 ksl[:, kt * 128:(kt + 1) * 128],
                                 qsl[:, qg * 512:(qg + 1) * 512],
                                 start=True, stop=True,
                                 waits=(w0 if jj == 0 else ()), inc=(jj == 3))
                    dmark = 0
                    if diag:
                        v.wait(ts_, mS)
                        dmark = v.emit(
                            "tensor_tensor", psS4[:], psS4[:], sstri[:],
                            op=ALU.add,
                            waits=[(dL, 16)] if (pair == 0 and qg == 0) else ())
                    ws = [(vs, dmark)] if dmark else [(ts_, mS)]
                    if last_ctx[0]:
                        ws.append((ts_, last_ctx[0]))
                    mex = s.emit("activation", ptb4[:], psS4[:], AF.Exp,
                                 scale=0.125, waits=ws)
                    last_ex[0] = mex
                    for jj in range(4):
                        kt = 4 * g2 + jj
                        wc = [(ss, mex)] if jj == 0 else []
                        if g2 == 0 and jj == 0:
                            if mo_prev:
                                wc.append((vs, mo_prev))
                            if mcs_prev:
                                wc.append((ss, mcs_prev))
                        last_ctx[0] = tmm(
                            psC[:], vsl[:, kt * 65:(kt + 1) * 65],
                            ptb4[:, jj * 512:(jj + 1) * 512],
                            start=(g2 == 0 and jj == 0),
                            stop=(g2 == ngrp - 1 and jj == 3),
                            waits=wc, inc=(jj == 3))
                v.wait(ts_, last_ctx[0])
                mrl = v.emit("reciprocal", rl[:], psC[64:65, :])
                for j2 in range(2):
                    tmm(psR[:, j2 * 256:(j2 + 1) * 256], onesf[:],
                        rl[:, j2 * 256:(j2 + 1) * 256], start=True, stop=True,
                        waits=([(vs, mrl), (vs, mo_prev)] if j2 == 0 else ()),
                        inc=(j2 == 1))
                mcs_prev = s.emit("copy", ctxs[:], psC[0:64, :],
                                  waits=[(vs, mrl)])
                v.wait(ts_, tn[0])
                v.wait(ss, mcs_prev)
                mo_prev = v.emit("tensor_tensor",
                                 outb[:, qg * 512:(qg + 1) * 512],
                                 ctxs[:], psR[:], op=ALU.mult)
            g.wait(vs, mo_prev)
            g.emit_unchained("dma_start", OUT[pair], outb[:], incs=[(dS, 16)])
            g.wait(dS, 16 * (pair + 1))
            if pair + 2 < 4:
                load_pair(pair + 2, [(ts_, tn[0]), (ss, s.n)])
                nload[pb] += 1
        sy.wait(dS, 64)
        p.build(blk)
    return nc


def d_masks():
    m = np.zeros((4, 128, 512), np.float32)
    tri = np.where(np.arange(128)[:, None] > np.arange(128)[None, :],
                   np.float32(-1e30), np.float32(0.0))
    for j in range(4):
        for jp in range(4):
            if jp < j:
                m[j][:, jp * 128:(jp + 1) * 128] = -1e30
            elif jp == j:
                m[j][:, jp * 128:(jp + 1) * 128] = tri
    return np.ascontiguousarray(m.transpose(1, 0, 2).reshape(128, 2048))


def run_d(ins, qkv_full):
    nc = build_d()
    stri = d_masks()
    maps = []
    for c in range(NC):
        b, gq = c // 4, c % 4
        QTh = np.zeros((4, 65, S_), BF)
        KTh = np.zeros((4, 65, S_), BF)
        VAh = np.zeros((4, S_, 65), BF)
        for j in range(4):
            h = 4 * gq + j
            QTh[j, 0:64] = qkv_full[b][:, h * 64:(h + 1) * 64].T
            KTh[j, 0:64] = qkv_full[b][:, 1024 + h * 64:1024 + (h + 1) * 64].T
            KTh[j, 64] = 1.0
            VAh[j, :, 0:64] = qkv_full[b][:, 2048 + h * 64:2048 + (h + 1) * 64]
            VAh[j, :, 64] = 1.0
        maps.append({"QT": QTh, "KT": KTh, "VA": VAh, "STRI": stri})
    res = run_bass_kernel_spmd(nc, maps, core_ids=list(range(NC)))
    catT = np.zeros((2, 1024, S_), BF)
    for c in range(NC):
        b, gq = c // 4, c % 4
        o = np.asarray(res.results[c]["OUT"])
        for j in range(4):
            h = 4 * gq + j
            catT[b, h * 64:(h + 1) * 64] = o[j]
    return catT



NLT = 20


def build_f1():
    nc = bass.Bass("TRN2", target_bir_lowering=False)
    H1T = nc.dram_tensor("H1T", [NLT, 8, 128, 128], F32, kind="ExternalInput").ap()
    XB = nc.dram_tensor("XB", [NT, 128, D], F32, kind="ExternalInput").ap()
    WQKV = nc.dram_tensor("WQKV", [8, 128, 1536], F32, kind="ExternalInput").ap()
    WO = nc.dram_tensor("WO", [4, 128, D], F32, kind="ExternalInput").ap()
    QKS = nc.dram_tensor("QKS", [128, 128], F32, kind="ExternalInput").ap()
    CST = nc.dram_tensor("CST", [NLT, 128, 128], F32, kind="ExternalInput").ap()
    SNT = nc.dram_tensor("SNT", [NLT, 128, 128], F32, kind="ExternalInput").ap()
    MBAND = nc.dram_tensor("MBAND", [8, 128, 512], F32, kind="ExternalInput").ap()
    KILLC = nc.dram_tensor("KILLC", [128, 8], F32, kind="ExternalInput").ap()
    IDTB = nc.dram_tensor("IDTB", [128, 128], BF16, kind="ExternalInput").ap()
    IDTF = nc.dram_tensor("IDTF", [128, 128], F32, kind="ExternalInput").ap()
    BSCR = nc.dram_tensor("BSCR", [8, 2048], F32).ap()
    Y1 = nc.dram_tensor("Y1", [NT, 128, D], F32, kind="ExternalOutput").ap()

    import contextlib
    _st = contextlib.ExitStack()
    h1t = _st.enter_context(nc.sbuf_tensor("h1t", [128, 8 * 128], BF16))
    swq = _st.enter_context(nc.sbuf_tensor("swq", [128, 8 * 1536], BF16))
    swo = _st.enter_context(nc.sbuf_tensor("swo", [128, 4 * D], BF16))
    sqks = _st.enter_context(nc.sbuf_tensor("sqks", [128, 128], F32))
    scst = _st.enter_context(nc.sbuf_tensor("scst", [128, NLT * 128], F32))
    ssnt = _st.enter_context(nc.sbuf_tensor("ssnt", [128, NLT * 128], F32))
    smb = _st.enter_context(nc.sbuf_tensor("smb", [128, 8 * 512], BF16))
    skill = _st.enter_context(nc.sbuf_tensor("skill", [128, 8], F32))
    sidb = _st.enter_context(nc.sbuf_tensor("sidb", [128, 128], BF16))
    sidf = _st.enter_context(nc.sbuf_tensor("sidf", [128, 128], F32))
    QA = [_st.enter_context(nc.sbuf_tensor(f"QA{h}", [65, T], BF16))
          for h in range(8)]
    KA = [_st.enter_context(nc.sbuf_tensor(f"KA{h}", [65, NLT * 128], BF16))
          for h in range(8)]
    vbuf = _st.enter_context(nc.sbuf_tensor("vbuf", [128, NLT * 520], BF16))
    oT = _st.enter_context(nc.sbuf_tensor("oT", [128, 4 * T], BF16))
    qn2T = _st.enter_context(nc.sbuf_tensor("qn2T", [8, NLT * 128], F32))
    kmaxs = _st.enter_context(nc.sbuf_tensor("kmaxs", [8, 32], F32))
    wm = _st.enter_context(nc.sbuf_tensor("wm", [8, 16], F32))
    scr_qk = _st.enter_context(nc.sbuf_tensor("scr_qk", [128, 1024], F32))
    t1b = _st.enter_context(nc.sbuf_tensor("t1b", [128, 1024], F32))
    rotb = _st.enter_context(nc.sbuf_tensor("rotb", [128, 1024], F32))
    nrm = _st.enter_context(nc.sbuf_tensor("nrm", [128, 48], F32))
    ptb4 = _st.enter_context(nc.sbuf_tensor("ptb4", [128, 4 * 512], BF16))
    ctxs = _st.enter_context(nc.sbuf_tensor("ctxs", [64, 512], F32))
    rl = _st.enter_context(nc.sbuf_tensor("rl", [1, 512], F32))
    xb = _st.enter_context(nc.sbuf_tensor("xb", [128, D], F32))
    onesf_ = _st.enter_context(nc.sbuf_tensor("onesf_", [1, 64], F32))
    psT1 = _st.enter_context(nc.psum_tensor("psT1", [128, 512], F32))
    psS4 = _st.enter_context(nc.psum_tensor("psS4", [128, 4 * 512], F32))
    psC = _st.enter_context(nc.psum_tensor("psC", [65, 512], F32))
    psR = _st.enter_context(nc.psum_tensor("psR", [64, 512], F32))
    dL = _st.enter_context(nc.semaphore("dL"))
    dH0 = _st.enter_context(nc.semaphore("dH0"))
    dH1 = _st.enter_context(nc.semaphore("dH1"))
    dB = _st.enter_context(nc.semaphore("dB"))
    dXb = _st.enter_context(nc.semaphore("dXb"))
    dXc = _st.enter_context(nc.semaphore("dXc"))
    dS = _st.enter_context(nc.semaphore("dS"))
    vs = _st.enter_context(nc.semaphore("vs"))
    ss = _st.enter_context(nc.semaphore("ss"))
    gs = _st.enter_context(nc.semaphore("gs"))
    ts_ = _st.enter_context(nc.semaphore("ts"))
    with _st:
        blk = _st.enter_context(nc.Block())
        p = Prog(nc)
        E = p.engines({"vector": vs, "scalar": ss, "gpsimd": gs})
        v, s, g, sy, t = (E["vector"], E["scalar"], E["gpsimd"], E["sync"],
                          E["tensor"])
        tn = [0]

        def tmm(*args, waits=(), inc=False, **kw):
            incs = []
            if inc:
                tn[0] += 1
                incs = [(ts_, 1)]
            t.emit_unchained("matmul", *args, waits=waits, incs=incs, **kw)
            return tn[0]

        def ttr(out, in_, ident, waits=()):
            tn[0] += 1
            t.emit_unchained("transpose", out, in_, ident, waits=waits,
                             incs=[(ts_, 1)])
            return tn[0]

        sy.emit_unchained("dma_start", sqks[:], QKS, incs=[(dL, 16)])
        sy.emit_unchained("dma_start",
                          scst[:].rearrange("p (c m) -> p c m", c=NLT),
                          CST.transpose([1, 0, 2]), incs=[(dL, 16)])
        sy.emit_unchained("dma_start",
                          ssnt[:].rearrange("p (c m) -> p c m", c=NLT),
                          SNT.transpose([1, 0, 2]), incs=[(dL, 16)])
        sy.emit_unchained("dma_start", skill[:], KILLC, incs=[(dL, 16)])
        sy.emit_unchained("dma_start", sidb[:], IDTB, incs=[(dL, 16)])
        sy.emit_unchained("dma_start", sidf[:], IDTF, incs=[(dL, 16)])
        g.emit_unchained("dma_start",
                         smb[:].rearrange("p (c m) -> p c m", c=8),
                         MBAND.transpose([1, 0, 2]), incs=[(dB, 16)])
        g.emit_unchained("dma_start",
                         swq[:].rearrange("p (c m) -> p c m", c=8),
                         WQKV.transpose([1, 0, 2]), incs=[(dB, 16)])
        g.emit_unchained("dma_start",
                         swo[:].rearrange("p (c m) -> p c m", c=4),
                         WO.transpose([1, 0, 2]), incs=[(dB, 16)])
        g.emit("memset", vbuf[:], 1.0)
        g.emit("memset", onesf_[:], 1.0)
        for h in range(8):
            g.emit("memset", KA[h][64:65, :], 1.0)
        mg0 = g.n

        nh = [0, 0]
        prev_tile_v = 0
        prev_tile_t = 0
        prev_tile_s = 0
        for lt in range(NLT):
            par = lt % 2
            dHp = dH0 if par == 0 else dH1
            wdma = [(ts_, prev_tile_t)] if lt >= 1 else []
            g.emit_unchained("dma_start",
                             h1t[:].rearrange("p (c m) -> p c m", c=8),
                             H1T[lt].transpose([1, 0, 2]), waits=wdma,
                             incs=[(dHp, 16)])
            nh[par] += 1
            w0 = [(dHp, 16 * nh[par]), (dB, 48)]
            if lt >= 1:
                w0.append((vs, prev_psq_free))
                w0.append((ss, prev_tile_s))
            for mch in range(3):
                for f in range(8):
                    tmm(psS4[:, mch * 512:(mch + 1) * 512],
                        h1t[:, f * 128:(f + 1) * 128],
                        swq[:, f * 1536 + mch * 512:f * 1536 + (mch + 1) * 512],
                        start=(f == 0), stop=(f == 7),
                        waits=(w0 if (mch == 0 and f == 0) else ()),
                        inc=(f == 7))
            mQ = tn[0]
            prev_tile_t = mQ
            s.wait(ts_, mQ)
            msq = s.emit("activation", rotb[:], psS4[:, 0:1024], AF.Square)
            v.wait(ss, msq)
            v.emit("tensor_reduce", nrm[:, 0:16],
                   rotb[:].rearrange("p (g x) -> p g x", g=16), axis=AX.X,
                   op=ALU.add)
            mnr = v.n
            s.wait(vs, mnr)
            msd = s.emit("activation", nrm[:, 16:32], nrm[:, 0:16], AF.Sqrt)
            v.wait(ss, msd)
            v.emit("tensor_scalar_max", nrm[:, 16:32], nrm[:, 16:32], 1e-12)
            v.emit("reciprocal", nrm[:, 32:48], nrm[:, 16:32])
            rnview = nrm[:, 32:48].rearrange("p (g o) -> p g o", g=16)\
                .broadcast_to([128, 16, 64])
            m_psq_free = v.emit("tensor_tensor", scr_qk[:], psS4[:, 0:1024],
                                rnview, op=ALU.mult)
            qksview = sqks[:].rearrange("p (q m) -> p q m", q=2)\
                .unsqueeze(2).broadcast_to([128, 2, 8, 64])
            v.emit("tensor_tensor", scr_qk[:],
                   scr_qk[:].rearrange("p (q h m) -> p q h m", q=2, h=8),
                   qksview, op=ALU.mult, waits=[(dL, 96)] if lt == 0 else ())
            msq2 = s.emit("activation", t1b[:], scr_qk[:], AF.Square,
                          waits=[(vs, v.n)])
            v.wait(ss, msq2)
            v.emit("tensor_reduce", nrm[:, 0:16],
                   t1b[:].rearrange("p (g x) -> p g x", g=16), axis=AX.X,
                   op=ALU.add)
            cstv = scst[:, lt * 128:(lt + 1) * 128]\
                .rearrange("p (q m) -> p q m", q=2).unsqueeze(2)\
                .broadcast_to([128, 2, 8, 64])
            sntv = ssnt[:, lt * 128:(lt + 1) * 128]\
                .rearrange("p (q m) -> p q m", q=2).unsqueeze(2)\
                .broadcast_to([128, 2, 8, 64])
            v.emit("tensor_tensor", t1b[:],
                   scr_qk[:].rearrange("p (q h m) -> p q h m", q=2, h=8),
                   cstv, op=ALU.mult)
            rview_out = rotb[:].rearrange("p (g m) -> p g m", g=16)[:, :, 0:32]
            rview_in = scr_qk[:].rearrange("p (g m) -> p g m", g=16)[:, :, 32:64]
            v.emit("tensor_scalar_mul", rview_out, rview_in, -1.0)
            rview_out2 = rotb[:].rearrange("p (g m) -> p g m", g=16)[:, :, 32:64]
            rview_in2 = scr_qk[:].rearrange("p (g m) -> p g m", g=16)[:, :, 0:32]
            v.emit("tensor_copy", rview_out2, rview_in2)
            v.emit("tensor_tensor", rotb[:],
                   rotb[:].rearrange("p (q h m) -> p q h m", q=2, h=8),
                   sntv, op=ALU.mult)
            mrot = v.emit("tensor_tensor", t1b[:], t1b[:], rotb[:], op=ALU.add)
            mbf = v.emit("tensor_copy", scr_qk[:].bitcast(BF16)[:, 0:1024],
                         t1b[:])
            trm = 0
            for f in range(8):
                wtr = [(vs, mbf)]
                if trm:
                    wtr.append((vs, trm))
                tdst = (psT1[:, 0:64] if f % 2 == 0 else
                        psS4[:, 1536:1536 + 64])
                mtr = ttr(tdst.bitcast(BF16),
                          scr_qk[:].bitcast(BF16)[:, f * 128:(f + 1) * 128],
                          sidb[:], waits=wtr)
                v.wait(ts_, mtr)
                h2_, h2b = 2 * (f % 4), 2 * (f % 4) + 1
                if f < 4:
                    if lt >= 4:
                        v.emit("tensor_copy",
                               QA[h2_][0:64, (lt - 4) * 128:(lt - 3) * 128],
                               tdst[0:64, 0:64].bitcast(BF16))
                        trm = v.emit("tensor_copy",
                                     QA[h2b][0:64, (lt - 4) * 128:(lt - 3) * 128],
                                     tdst[64:128, 0:64].bitcast(BF16))
                else:
                    v.emit("tensor_copy",
                           KA[h2_][0:64, lt * 128:(lt + 1) * 128],
                           tdst[0:64, 0:64].bitcast(BF16))
                    trm = v.emit("tensor_copy",
                                 KA[h2b][0:64, lt * 128:(lt + 1) * 128],
                                 tdst[64:128, 0:64].bitcast(BF16))
            for h in range(8):
                s.emit("copy",
                       vbuf[:, lt * 520 + h * 65:lt * 520 + h * 65 + 64]
                       if False else
                       vbuf[:, lt * 520 + h * 65:lt * 520 + h * 65 + 64],
                       psS4[:, 1024 + h * 64:1024 + (h + 1) * 64],
                       waits=[(gs, mg0), (ts_, mQ)] if h == 0 else ())
            prev_tile_s = s.n
            mtq = ttr(psT1[0:8, 0:128], nrm[:, 0:8], sidf[:],
                      waits=[(vs, v.n)])
            v.wait(ts_, mtq)
            v.emit("tensor_copy", qn2T[0:8, lt * 128:(lt + 1) * 128],
                   psT1[0:8, 0:128])
            mtk = ttr(psT1[0:8, 0:128], nrm[:, 8:16], sidf[:], waits=[(vs, v.n)])
            v.wait(ts_, mtk)
            v.emit("reduce_max", kmaxs[0:8, lt:lt + 1], psT1[0:8, 0:128],
                   axis=AX.X)
            prev_tile_v = v.n
            prev_psq_free = m_psq_free

        v.emit("tensor_copy", wm[:], kmaxs[0:8, 0:16])
        for o in range(1, 5):
            v.emit("tensor_tensor", wm[:], wm[:], kmaxs[0:8, o:o + 16],
                   op=ALU.max)
        v.emit("tensor_scalar_mul", wm[:], wm[:], -0.5)
        v.emit("tensor_scalar_mul", qn2T[0:8, 512:2560],
               qn2T[0:8, 512:2560], -0.5)
        wmv = wm[:].unsqueeze(2).broadcast_to([8, 16, 128])
        mbb = v.emit("tensor_tensor",
                     qn2T[0:8, 512:2560].rearrange("p (t m) -> p t m", t=16),
                     qn2T[0:8, 512:2560].rearrange("p (t m) -> p t m", t=16),
                     wmv, op=ALU.add)
        g.wait(vs, mbb)
        g.emit_unchained("dma_start", BSCR, qn2T[0:8, 512:2560],
                         incs=[(dXb, 16)])
        for h in range(8):
            g.emit_unchained("dma_start", QA[h][64:65, :], BSCR[h:h + 1, :],
                             waits=[(dXb, 16)] if h == 0 else [],
                             incs=[(dXb, 16)])

        last_ex = 0
        last_ctx = 0
        mo_prev = 0
        mcs_prev = 0
        for h in range(8):
            for qg in range(4):
                for g2 in range(2):
                    w0 = ([(dXb, 144), (ss, prev_tile_s)]
                          if (h == 0 and qg == 0 and g2 == 0) else [])
                    if last_ex:
                        w0.append((ss, last_ex))
                    for jj in range(4):
                        kt = 4 * qg + 4 * g2 + jj
                        mS = tmm(psS4[:, jj * 512:(jj + 1) * 512],
                                 KA[h][:, kt * 128:(kt + 1) * 128],
                                 QA[h][:, qg * 512:(qg + 1) * 512],
                                 start=True, stop=True,
                                 waits=(w0 if jj == 0 else ()), inc=(jj == 3))
                    v.wait(ts_, mS)
                    dmark = v.emit(
                        "tensor_tensor", psS4[:], psS4[:],
                        smb[:, g2 * 2048:(g2 + 1) * 2048], op=ALU.add,
                        waits=[(dB, 48)] if (h == 0 and qg == 0) else ())
                    if qg == 0 and g2 == 0:
                        for jj in range(4):
                            dmark = v.emit(
                                "tensor_scalar",
                                psS4[:, jj * 512:(jj + 1) * 512],
                                psS4[:, jj * 512:(jj + 1) * 512],
                                skill[:, jj:jj + 1], None, op0=ALU.add)
                    ws = [(vs, dmark)]
                    if last_ctx:
                        ws.append((ts_, last_ctx))
                    mex = s.emit("activation", ptb4[:], psS4[:], AF.Exp,
                                 scale=8.0, waits=ws)
                    last_ex = mex
                    for jj in range(4):
                        kt = 4 * qg + 4 * g2 + jj
                        wc = [(ss, mex)] if jj == 0 else []
                        if g2 == 0 and jj == 0:
                            if mo_prev:
                                wc.append((vs, mo_prev))
                            if mcs_prev:
                                wc.append((ss, mcs_prev))
                        last_ctx = tmm(
                            psC[:],
                            vbuf[:, kt * 520 + h * 65:kt * 520 + (h + 1) * 65],
                            ptb4[:, jj * 512:(jj + 1) * 512],
                            start=(g2 == 0 and jj == 0),
                            stop=(g2 == 1 and jj == 3),
                            waits=wc, inc=(jj == 3))
                v.wait(ts_, last_ctx)
                mrl = v.emit("reciprocal", rl[:], psC[64:65, :])
                for j2 in range(2):
                    tmm(psR[:, j2 * 256:(j2 + 1) * 256], onesf_[:],
                        rl[:, j2 * 256:(j2 + 1) * 256], start=True, stop=True,
                        waits=([(vs, mrl), (vs, mo_prev)] if j2 == 0 else ()),
                        inc=(j2 == 1))
                mcs_prev = s.emit("copy", ctxs[:], psC[0:64, :],
                                  waits=[(vs, mrl)])
                v.wait(ts_, tn[0])
                v.wait(ss, mcs_prev)
                co = (h // 2) * T + qg * 512
                ro = (h % 2) * 64
                mo_prev = v.emit("tensor_tensor",
                                 oT[ro:ro + 64, co:co + 512] if ro else
                                 oT[0:64, co:co + 512],
                                 ctxs[:], psR[:], op=ALU.mult)

        nxb = 0
        mdout = 0
        for tt in range(NT):
            sy.emit_unchained("dma_start", xb[:], XB[tt],
                              waits=([(vs, mdout), (dS, 16 * tt)] if tt
                                     else [(vs, mo_prev)]),
                              incs=[(dXc, 16)])
            nxb += 1
            w0 = [(vs, mo_prev), (dB, 48), (dXc, 16 * nxb)]
            if tt >= 1:
                w0.append((vs, mdout))
            for mch in range(2):
                for c in range(4):
                    tmm(psS4[:, mch * 512:(mch + 1) * 512],
                        oT[:, c * T + tt * 128:c * T + (tt + 1) * 128],
                        swo[:, c * D + mch * 512:c * D + (mch + 1) * 512],
                        start=(c == 0), stop=(c == 3),
                        waits=(w0 if (mch == 0 and c == 0) else ()),
                        inc=(c == 3))
            v.wait(ts_, tn[0])
            v.wait(dXc, 16 * nxb)
            v.emit("tensor_tensor", xb[:, 0:512], xb[:, 0:512],
                   psS4[:, 0:512], op=ALU.add)
            mdout = v.emit("tensor_tensor", xb[:, 512:1024],
                           xb[:, 512:1024], psS4[:, 512:1024], op=ALU.add)
            g.wait(vs, mdout)
            g.emit_unchained("dma_start", Y1[tt], xb[:], incs=[(dS, 16)])
            g.wait(dS, 16 * (tt + 1))
        sy.wait(dS, 16 * NT)
        p.build(blk)
    return nc


def f1_tables():
    DIM_HEAD, WINDOW = 64, 512
    half = np.arange(0, DIM_HEAD, 2, dtype=np.float64)
    inv_freq = 1.0 / 10000.0 ** (half / DIM_HEAD)
    base = (half + 0.4 * DIM_HEAD) / (1.4 * DIM_HEAD)

    def make(pos0):
        tpos = pos0 + np.arange(NLT * 128, dtype=np.float64)
        freqs = tpos[:, None] * inv_freq[None, :]
        freqs2 = np.concatenate([freqs, freqs], axis=1)
        power = (tpos - S_ // 2) / float(WINDOW // 2)
        scale = base[None, :] ** power[:, None]
        scale2 = np.concatenate([scale, scale], axis=1)
        cos, sin = np.cos(freqs2), np.sin(freqs2)
        cst = np.concatenate([cos * scale2, cos / scale2], axis=1)
        snt = np.concatenate([sin * scale2, sin / scale2], axis=1)
        return (cst.reshape(NLT, 128, 128).astype(np.float32),
                snt.reshape(NLT, 128, 128).astype(np.float32))
    return make


def f1_mband():
    m = np.zeros((8, 128, 512), np.float32)
    kk = np.arange(128)[:, None]
    qq = np.arange(128)[None, :]
    tri_hi = np.where(kk > qq, np.float32(-1e30), np.float32(0.0))
    tri_lo = np.where(kk < qq, np.float32(-1e30), np.float32(0.0))
    for d in range(8):
        for j in range(4):
            r = 4 + j - d
            if r < 0 or r > 4:
                m[d][:, j * 128:(j + 1) * 128] = -1e30
            elif r == 0:
                m[d][:, j * 128:(j + 1) * 128] = tri_hi
            elif r == 4:
                m[d][:, j * 128:(j + 1) * 128] = tri_lo
    return m


def prep_f1(ins, layer, x_full):
    wqkv = np.asarray(ins["L1_ffn_wqkv"][layer], np.float32)
    wo = np.asarray(ins["L1_ffn_wo"][layer], np.float32)
    qs = np.asarray(ins["L1_ffn_q_scale"][layer], np.float32)
    ks = np.asarray(ins["L1_ffn_k_scale"][layer], np.float32)
    QKS = np.tile(np.concatenate([qs, ks])[None, :], (128, 1)).astype(np.float32)
    mk = f1_tables()
    maps = []
    for c in range(NC):
        b, gq = c // 4, c % 4
        s0 = gq * T
        lo = s0 - 512
        h1 = np.zeros((NLT * 128, D), np.float32)
        xr = np.asarray(x_full[b])
        a0 = max(lo, 0)
        h1[a0 - lo:NLT * 128, 0:512] = xr[a0:s0 + T, 0:512]
        a1 = max(lo - 1, 0)
        h1[(a1 - (lo - 1)) + 0:NLT * 128, 512:1024] = xr[a1:s0 + T - 1, 512:1024]
        h1T = np.ascontiguousarray(h1.T.reshape(8, 128, NLT, 128)
                                   .transpose(2, 0, 1, 3))
        cst, snt = mk(lo)
        kill = np.zeros((128, 8), np.float32)
        if gq == 0:
            kill[:, 0:4] = -1e30
        maps.append({
            "H1T": h1T,
            "XB": tm_host(np.ascontiguousarray(xr[s0:s0 + T])),
            "WQKV": fm_host(wqkv),
            "WO": np.ascontiguousarray(wo.reshape(4, 128, D)),
            "QKS": QKS, "CST": cst, "SNT": snt,
            "MBAND": f1_mband(), "KILLC": kill,
            "IDTB": np.eye(128, dtype=BF),
            "IDTF": np.eye(128, dtype=np.float32),
        })
    return maps


def run_f1(ins, layer, x_full):
    nc = build_f1()
    maps = prep_f1(ins, layer, x_full)
    res = run_bass_kernel_spmd(nc, maps, core_ids=list(range(NC)))
    return [np.asarray(r["Y1"]).reshape(T, D) for r in res.results]



NJ = 22


def build_f2(final=False):
    nc = bass.Bass("TRN2", target_bir_lowering=False)
    Y = nc.dram_tensor("Y", [NT, 128, D], F32, kind="ExternalInput").ap()
    HS = nc.dram_tensor("HS", [NT, 128, D], F32, kind="ExternalInput").ap()
    LNG = nc.dram_tensor("LNG", [1, D], F32, kind="ExternalInput").ap()
    LNB = nc.dram_tensor("LNB", [1, D], F32, kind="ExternalInput").ap()
    W1A = nc.dram_tensor("W1A", [NJ, 8, 128, 128], F32, kind="ExternalInput").ap()
    W1G = nc.dram_tensor("W1G", [NJ, 8, 128, 128], F32, kind="ExternalInput").ap()
    B1A = nc.dram_tensor("B1A", [128, NJ], F32, kind="ExternalInput").ap()
    B1G = nc.dram_tensor("B1G", [128, NJ], F32, kind="ExternalInput").ap()
    W2H = nc.dram_tensor("W2H", [NJ, 8, 128, 128], F32, kind="ExternalInput").ap()
    B2C = nc.dram_tensor("B2C", [128, 8], F32, kind="ExternalInput").ap()
    IDTB = nc.dram_tensor("IDTB", [128, 128], BF16, kind="ExternalInput").ap()
    if final:
        X3 = nc.dram_tensor("X3", [NT, 128, D], F32, kind="ExternalInput").ap()
    OUT = nc.dram_tensor("OUT", [NT, 128, D], F32, kind="ExternalOutput").ap()

    import contextlib
    _st = contextlib.ExitStack()
    hst = _st.enter_context(nc.sbuf_tensor("hst", [128, 2 * D], F32))
    hnT = _st.enter_context(nc.sbuf_tensor("hnT", [128, 8 * T], BF16))
    h3T = _st.enter_context(nc.sbuf_tensor("h3T", [128, NJ * 1024], BF16))
    w2sb = _st.enter_context(nc.sbuf_tensor("w2sb", [128, NJ * 1024], BF16))
    w1b = _st.enter_context(nc.sbuf_tensor("w1b", [128, 2 * 2 * 1024], BF16))
    outT = _st.enter_context(nc.sbuf_tensor("outT", [128, 8 * 1024], BF16))
    slng = _st.enter_context(nc.sbuf_tensor("slng", [1, D], F32))
    slnb = _st.enter_context(nc.sbuf_tensor("slnb", [1, D], F32))
    lgB = _st.enter_context(nc.sbuf_tensor("lgB", [128, D], F32))
    lbB = _st.enter_context(nc.sbuf_tensor("lbB", [128, D], F32))
    b1a = _st.enter_context(nc.sbuf_tensor("b1a", [128, NJ], F32))
    b1g = _st.enter_context(nc.sbuf_tensor("b1g", [128, NJ], F32))
    b2c = _st.enter_context(nc.sbuf_tensor("b2c", [128, 8], F32))
    sidb = _st.enter_context(nc.sbuf_tensor("sidb", [128, 128], BF16))
    yb = _st.enter_context(nc.sbuf_tensor("yb", [128, D], F32))
    x3b = _st.enter_context(nc.sbuf_tensor("x3b", [128, D], F32))
    scr1 = _st.enter_context(nc.sbuf_tensor("scr1", [128, D], F32))
    scr2 = _st.enter_context(nc.sbuf_tensor("scr2", [128, D], F32))
    scr3 = _st.enter_context(nc.sbuf_tensor("scr3", [128, 512], F32))
    scr4 = _st.enter_context(nc.sbuf_tensor("scr4", [128, 512], F32))
    cols = _st.enter_context(nc.sbuf_tensor("cols", [128, 16], F32))
    epscol = _st.enter_context(nc.sbuf_tensor("epscol", [128, 1], F32))
    onesf = _st.enter_context(nc.sbuf_tensor("onesf", [1, 128], F32))
    psA = _st.enter_context(nc.psum_tensor("psA", [128, 2 * 512], F32))
    psB_ = _st.enter_context(nc.psum_tensor("psB_", [128, 2 * 512], F32))
    psD = _st.enter_context(nc.psum_tensor("psD", [128, 2 * 512], F32))
    psT = _st.enter_context(nc.psum_tensor("psT", [128, 2 * 512], F32))
    dL = _st.enter_context(nc.semaphore("dL"))
    dH0 = _st.enter_context(nc.semaphore("dH0"))
    dH1 = _st.enter_context(nc.semaphore("dH1"))
    dW0 = _st.enter_context(nc.semaphore("dW0"))
    dW1 = _st.enter_context(nc.semaphore("dW1"))
    dW2 = _st.enter_context(nc.semaphore("dW2"))
    dXc = _st.enter_context(nc.semaphore("dXc"))
    dS = _st.enter_context(nc.semaphore("dS"))
    vs = _st.enter_context(nc.semaphore("vs"))
    ss = _st.enter_context(nc.semaphore("ss"))
    gs = _st.enter_context(nc.semaphore("gs"))
    ts_ = _st.enter_context(nc.semaphore("ts"))
    with _st:
        blk = _st.enter_context(nc.Block())
        p = Prog(nc)
        E = p.engines({"vector": vs, "scalar": ss, "gpsimd": gs})
        v, s, g, sy, t = (E["vector"], E["scalar"], E["gpsimd"], E["sync"],
                          E["tensor"])
        tn = [0]

        def tmm(*args, waits=(), inc=False, **kw):
            incs = []
            if inc:
                tn[0] += 1
                incs = [(ts_, 1)]
            t.emit_unchained("matmul", *args, waits=waits, incs=incs, **kw)
            return tn[0]

        def ttr(out, in_, ident, waits=()):
            tn[0] += 1
            t.emit_unchained("transpose", out, in_, ident, waits=waits,
                             incs=[(ts_, 1)])
            return tn[0]

        def ln_tile(src_ap, dst_ap, extra_s_wait=None):
            v.emit("tensor_reduce", cols[:, 0:1], src_ap, axis=AX.X,
                   op=ALU.add)
            wts = [(vs, v.n)]
            if extra_s_wait is not None:
                wts.append(extra_s_wait)
            msq = s.emit("activation", scr2[:], src_ap, AF.Square,
                         accum_out=cols[:, 1:2], waits=wts)
            v.wait(ss, msq)
            v.emit("tensor_scalar_mul", cols[:, 2:3], cols[:, 0:1], 1.0 / D)
            v.emit("tensor_scalar_mul", cols[:, 3:4], cols[:, 1:2], 1.0 / D)
            v.emit("tensor_tensor", cols[:, 4:5], cols[:, 2:3], cols[:, 2:3],
                   op=ALU.mult)
            mv = v.emit("tensor_tensor", cols[:, 5:6], cols[:, 3:4],
                        cols[:, 4:5], op=ALU.subtract)
            s.wait(vs, mv)
            msd = s.emit("activation", cols[:, 6:7], cols[:, 5:6], AF.Sqrt,
                         bias=epscol[:])
            v.wait(ss, msd)
            v.emit("reciprocal", cols[:, 7:8], cols[:, 6:7])
            return v.emit("tensor_scalar", dst_ap, src_ap, cols[:, 2:3],
                          cols[:, 7:8], op0=ALU.subtract, op1=ALU.mult)

        sy.emit_unchained("dma_start", slng[:], LNG, incs=[(dL, 16)])
        sy.emit_unchained("dma_start", slnb[:], LNB, incs=[(dL, 16)])
        sy.emit_unchained("dma_start", b1a[:], B1A, incs=[(dL, 16)])
        sy.emit_unchained("dma_start", b1g[:], B1G, incs=[(dL, 16)])
        sy.emit_unchained("dma_start", b2c[:], B2C, incs=[(dL, 16)])
        sy.emit_unchained("dma_start", sidb[:], IDTB, incs=[(dL, 16)])
        for j in range(NJ):
            g.emit_unchained("dma_start",
                             w2sb[:, j * 1024:(j + 1) * 1024]
                             .rearrange("p (c m) -> p c m", c=8),
                             W2H[j].transpose([1, 0, 2]), incs=[(dW2, 16)])
        g.emit("memset", epscol[:], EPS)
        g.emit("memset", onesf[:], 1.0)
        mg0 = g.n
        for jj in range(4):
            tmm(psA[:, jj * 256:(jj + 1) * 256], onesf[:],
                slng[:, jj * 256:(jj + 1) * 256], start=True, stop=True,
                waits=([(dL, 96), (gs, mg0)] if jj == 0 else ()),
                inc=(jj == 3))
        for jj in range(4):
            tmm(psB_[:, jj * 256:(jj + 1) * 256], onesf[:],
                slnb[:, jj * 256:(jj + 1) * 256], start=True, stop=True,
                inc=(jj == 3))
        v.wait(ts_, tn[0])
        v.emit("tensor_copy", lgB[:], psA[:, 0:1024])
        mlg = v.emit("tensor_copy", lbB[:], psB_[:, 0:1024])

        nh = [0, 0]
        last_tr = 0
        for tt in range(NT):
            par = tt % 2
            dHp = dH0 if par == 0 else dH1
            wdma = [(vs, last_tr)] if tt >= 2 else []
            sy.emit_unchained("dma_start", hst[:, par * D:(par + 1) * D],
                              HS[tt], waits=wdma, incs=[(dHp, 16)])
            nh[par] += 1
            v.wait(dHp, 16 * nh[par])
            if tt > 0:
                s.wait(ts_, last_trn)
            mln = ln_tile(hst[:, par * D:(par + 1) * D], scr1[:],
                          extra_s_wait=(dHp, 16 * nh[par]))
            v.emit("tensor_tensor", scr1[:], scr1[:], lgB[:], op=ALU.mult,
                   waits=[(vs, mlg)] if tt == 0 else [])
            wxf = [(ts_, prev_tr)] if tt > 0 and (prev_tr := trlast) else []
            mxf = v.emit("tensor_tensor", scr2[:].bitcast(BF16)[:, 0:D],
                         scr1[:], lbB[:], op=ALU.add, waits=wxf)
            trm = {}
            for f in range(8):
                half = f % 2
                wtr = [(vs, mxf)]
                if f >= 2:
                    wtr.append((vs, trm[f - 2]))
                mtr = ttr(psT[:, half * 512:half * 512 + 64].bitcast(BF16),
                          scr2[:].bitcast(BF16)[:, f * 128:(f + 1) * 128],
                          sidb[:], waits=wtr)
                v.wait(ts_, mtr)
                trm[f] = v.emit(
                    "tensor_copy",
                    hnT[:, f * T + tt * 128:f * T + (tt + 1) * 128],
                    psT[:, half * 512:half * 512 + 64].bitcast(BF16)[:, 0:128])
            trlast = tn[0]
            last_trn = tn[0]
            last_tr = trm[7]
        mhnT = v.n

        ngu = [0, 0]
        nxc = 0
        nout = 0
        eps_m = {}
        for half in range(2):
            toff = half * 1024
            for j in range(NJ):
                par = j % 2
                dWp = dW0 if par == 0 else dW1
                wdma = []
                if eps_m.get(("mm", half, j - 2)) or \
                   (half and eps_m.get(("mm", half - 1, NJ - 2 + j))):
                    wdma = [(ts_, eps_m.get(("mm", half, j - 2),
                            eps_m.get(("mm", half - 1, NJ - 2 + j), 0)))]
                g.emit_unchained("dma_start",
                                 w1b[:, par * 2048:par * 2048 + 1024]
                                 .rearrange("p (c m) -> p c m", c=8),
                                 W1A[j].transpose([1, 0, 2]),
                                 waits=wdma, incs=[(dWp, 16)])
                g.emit_unchained("dma_start",
                                 w1b[:, par * 2048 + 1024:par * 2048 + 2048]
                                 .rearrange("p (c m) -> p c m", c=8),
                                 W1G[j].transpose([1, 0, 2]),
                                 incs=[(dWp, 16)])
                ngu[par] += 2
                for tch in range(2):
                    ph = (2 * j + tch) % 2
                    tsl = slice(toff + tch * 512, toff + (tch + 1) * 512)
                    w0 = [(dWp, 16 * ngu[par]), (vs, mhnT)]
                    k1_ = ("hm", half, j, tch - 2)
                    k2_ = ("hm", half, j - 1, tch)
                    if eps_m.get(k1_) or eps_m.get(k2_):
                        w0.append((vs, eps_m.get(k1_, eps_m.get(k2_, 0))))
                    for f in range(8):
                        tmm(psA[:, ph * 512:(ph + 1) * 512],
                            w1b[:, par * 2048 + f * 128:par * 2048 + (f + 1) * 128],
                            hnT[:, f * T + toff + tch * 512:
                                f * T + toff + (tch + 1) * 512],
                            start=(f == 0), stop=(f == 7),
                            waits=(w0 if f == 0 else ()), inc=(f == 7))
                    for f in range(8):
                        tmm(psB_[:, ph * 512:(ph + 1) * 512],
                            w1b[:, par * 2048 + 1024 + f * 128:
                                par * 2048 + 1024 + (f + 1) * 128],
                            hnT[:, f * T + toff + tch * 512:
                                f * T + toff + (tch + 1) * 512],
                            start=(f == 0), stop=(f == 7), inc=(f == 7))
                    mGU = tn[0]
                    eps_m[("mm", half, j)] = mGU
                    s.wait(ts_, mGU)
                    if eps_m.get(("hml",)):
                        s.wait(vs, eps_m[("hml",)])
                    msl = s.emit("activation", scr3[:],
                                 psB_[:, ph * 512:(ph + 1) * 512], AF.Gelu,
                                 bias=b1g[:, j:j + 1])
                    v.wait(ts_, mGU)
                    v.emit("tensor_scalar_add", scr4[:],
                           psA[:, ph * 512:(ph + 1) * 512], b1a[:, j:j + 1])
                    v.wait(ss, msl)
                    eps_m[("hm", half, j, tch)] = v.emit(
                        "tensor_tensor",
                        h3T[:, j * 1024 + tch * 512:j * 1024 + (tch + 1) * 512],
                        scr4[:], scr3[:], op=ALU.mult)
                    eps_m[("hml",)] = eps_m[("hm", half, j, tch)]
            mh3 = v.n
            for mch in range(8):
                for tch in range(2):
                    ph = (2 * mch + tch) % 2
                    w0 = [(vs, mh3), (dW2, 16 * NJ)]
                    kd = ("dn", half, mch, tch - 2)
                    kd2 = ("dn", half, mch - 1, tch)
                    if eps_m.get(kd) or eps_m.get(kd2):
                        w0.append((vs, eps_m.get(kd, eps_m.get(kd2, 0))))
                    for j in range(NJ):
                        tmm(psD[:, ph * 512:(ph + 1) * 512],
                            w2sb[:, j * 1024 + mch * 128:j * 1024 + (mch + 1) * 128],
                            h3T[:, j * 1024 + tch * 512:j * 1024 + (tch + 1) * 512],
                            start=(j == 0), stop=(j == NJ - 1),
                            waits=(w0 if j == 0 else ()), inc=(j == NJ - 1))
                    v.wait(ts_, tn[0])
                    mdn = v.emit(
                        "tensor_scalar_add",
                        outT[:, mch * 1024 + tch * 512:
                             mch * 1024 + (tch + 1) * 512],
                        psD[:, ph * 512:(ph + 1) * 512], b2c[:, mch:mch + 1])
                    eps_m[("dn", half, mch, tch)] = mdn
            mout = v.n
            for th in range(8):
                tt = half * 8 + th
                wy = [(vs, eps_m.get(("yb",), 0))] if eps_m.get(("yb",)) else []
                sy.emit_unchained("dma_start", yb[:], Y[tt],
                                  waits=wy + ([(dS, 16 * nout)] if nout else []),
                                  incs=[(dXc, 16)])
                nxc += 1
                if final:
                    sy.emit_unchained("dma_start", x3b[:], X3[tt],
                                      incs=[(dXc, 16)])
                    nxc += 1
                trl = 0
                for mch in range(8):
                    phh = mch % 2
                    wtr = [(vs, mout)]
                    if trl:
                        wtr.append((vs, trl))
                    mtr = ttr(psT[:, phh * 512:phh * 512 + 64].bitcast(BF16),
                              outT[:, mch * 1024 + th * 128:
                                   mch * 1024 + (th + 1) * 128],
                              sidb[:], waits=wtr)
                    v.wait(ts_, mtr)
                    trl = v.emit("tensor_copy",
                                 scr1[:, mch * 128:(mch + 1) * 128],
                                 psT[:, phh * 512:phh * 512 + 64]
                                 .bitcast(BF16)[:, 0:128])
                v.wait(dXc, 16 * nxc)
                mf2 = v.emit("tensor_tensor", scr1[:], scr1[:], yb[:],
                             op=ALU.add)
                eps_m[("yb",)] = mf2
                if final:
                    ln_tile(scr1[:], scr2[:])
                    mf2 = v.emit("tensor_tensor", scr2[:], scr2[:], x3b[:],
                                 op=ALU.add)
                    outsrc = scr2
                else:
                    outsrc = scr1
                g.wait(vs, mf2)
                g.emit_unchained("dma_start", OUT[tt], outsrc[:],
                                 incs=[(dS, 16)])
                nout += 1
                v.wait(dS, 16 * nout)
        sy.wait(dS, 16 * nout)
        p.build(blk)
    return nc


def prep_f2(ins, layer, y_full, x3_full=None):
    w1 = np.asarray(ins["L1_ffn_w1"][layer], np.float32)
    b1 = np.asarray(ins["L1_ffn_b1"][layer], np.float32)
    w2 = np.asarray(ins["L1_ffn_w2"][layer], np.float32)
    b2 = np.asarray(ins["L1_ffn_b2"][layer], np.float32)
    lg = np.asarray(ins["L1_ffn_ln_g"][layer], np.float32)
    lb = np.asarray(ins["L1_ffn_ln_b"][layer], np.float32)
    HID = 2730
    W1A = np.zeros((NJ, 8, 128, 128), np.float32)
    W1G = np.zeros((NJ, 8, 128, 128), np.float32)
    B1Ah = np.zeros((128, NJ), np.float32)
    B1Gh = np.zeros((128, NJ), np.float32)
    W2Hh = np.zeros((NJ, 8, 128, 128), np.float32)
    for j in range(NJ):
        n = min(128, HID - j * 128)
        W1A[j, :, :, 0:n] = w1[:, j * 128:j * 128 + n].reshape(8, 128, n)
        W1G[j, :, :, 0:n] = w1[:, HID + j * 128:HID + j * 128 + n]\
            .reshape(8, 128, n)
        B1Ah[0:n, j] = b1[j * 128:j * 128 + n]
        B1Gh[0:n, j] = b1[HID + j * 128:HID + j * 128 + n]
        W2Hh[j, 0:8, 0:n, :] = w2[j * 128:j * 128 + n].reshape(n, 8, 128)\
            .transpose(1, 0, 2)
    B2Ch = np.ascontiguousarray(b2.reshape(8, 128).T)
    com = {"LNG": lg.reshape(1, D), "LNB": lb.reshape(1, D),
           "W1A": W1A, "W1G": W1G, "B1A": B1Ah, "B1G": B1Gh,
           "W2H": W2Hh, "B2C": B2Ch, "IDTB": np.eye(128, dtype=BF)}
    maps = []
    for c in range(NC):
        b, gq = c // 4, c % 4
        s0 = gq * T
        yr = np.asarray(y_full[b])
        hs = np.empty((T, D), np.float32)
        hs[:, 0:512] = yr[s0:s0 + T, 0:512]
        if s0 == 0:
            hs[0, 512:] = 0.0
            hs[1:, 512:] = yr[0:T - 1, 512:]
        else:
            hs[:, 512:] = yr[s0 - 1:s0 + T - 1, 512:]
        mp = dict(com, Y=tm_host(np.ascontiguousarray(yr[s0:s0 + T])),
                  HS=tm_host(hs))
        if x3_full is not None:
            mp["X3"] = tm_host(np.ascontiguousarray(
                np.asarray(x3_full[b])[s0:s0 + T]))
        maps.append(mp)
    return maps


def run_f2(ins, layer, y_full, x3_full=None):
    nc = build_f2(final=x3_full is not None)
    maps = prep_f2(ins, layer, y_full, x3_full)
    res = run_bass_kernel_spmd(nc, maps, core_ids=list(range(NC)))
    return [np.asarray(r["OUT"]).reshape(T, D) for r in res.results]


def kernel(**ins):
    x0 = np.ascontiguousarray(np.asarray(ins["L0_attn_x"], np.float32))
    xsl = core_slices(x0)
    xn_slices = run_a0(xsl)
    xn_full = assemble(xn_slices, D)
    outT = run_a1(ins, xn_full)
    ow = np.asarray(ins["L0_attn_output_weight"], np.float32)
    OW = np.ascontiguousarray(ow.transpose(0, 2, 1).reshape(D, D))
    catT_slices = [np.ascontiguousarray(
        outT[c // 4][:, (c % 4) * T:(c % 4 + 1) * T]) for c in range(NC)]
    x1_slices, _ = run_proj_boundary(xsl, catT_slices, OW, emit_ln=False)
    x2_slices, qkv_slices = run_b(ins, x1_slices)
    qkv_full = assemble(qkv_slices, 3072)
    catT2 = run_d(ins, qkv_full)
    WoT = np.ascontiguousarray(
        np.asarray(ins["L1_attn_W_o"], np.float32).T)
    cat2_slices = [np.ascontiguousarray(
        catT2[c // 4][:, (c % 4) * T:(c % 4 + 1) * T]) for c in range(NC)]
    x3_slices, t4_slices = run_proj_boundary(x2_slices, cat2_slices, WoT,
                                             emit_ln=True)
    x3_full = assemble(x3_slices, D)
    t4_full = assemble(t4_slices, D)
    y1 = assemble(run_f1(ins, 0, t4_full), D)
    xl1 = assemble(run_f2(ins, 0, y1), D)
    y2 = assemble(run_f1(ins, 1, xl1), D)
    x4 = assemble(run_f2(ins, 1, y2, x3_full), D)
    return x4.astype(np.float32)
